# version 03
# baseline (speedup 1.0000x reference)
import math
import numpy as np
import ml_dtypes
import concourse.bass as bass
import concourse.mybir as mybir
from concourse.bass_utils import run_bass_kernel_spmd
from contextlib import ExitStack

F32 = mybir.dt.float32
BF16 = mybir.dt.bfloat16
AF = mybir.ActivationFunctionType
ALU = mybir.AluOpType
AX = mybir.AxisListType

D = 2048
T = 2048
NCH = 16
WIN = 512
DH = 128
NH = 16
MAGIC = 1.5 * 2.0 ** 23
SCALE = DH ** -0.5
ACT_CONSTS = [1e-5, 1e-38, -0.5, 0.5, -2.0, SCALE, MAGIC, math.log(SCALE)]
C2 = 1.0 / math.sqrt(2048.0)
C4 = 1.0 / math.sqrt(4096.0)
EPSQ = 1e-5
NEG = -1.0e30

ENGINES = ("tensor", "vector", "scalar", "gpsimd", "sync")



def _rope_tables():
    j = np.arange(64)
    invf = 10000.0 ** (-2.0 * j / 128.0)
    t = np.arange(T)
    fr = t[None, :] * invf[:, None]
    cos = np.cos(fr).astype(np.float32)
    sin = np.sin(fr).astype(np.float32)
    ct0 = np.concatenate([cos, cos], axis=0)
    st0 = np.concatenate([-sin, sin], axis=0)
    ct1 = np.zeros((128, T), np.float32)
    st1 = np.zeros((128, T), np.float32)
    ct1[0::2] = cos
    ct1[1::2] = cos
    st1[0::2] = -sin
    st1[1::2] = sin
    bf = ml_dtypes.bfloat16
    return (ct0.astype(bf), st0.astype(bf), ct1.astype(bf), st1.astype(bf))


def _hadamard(n):
    h = np.array([[1.0]])
    while h.shape[0] < n:
        h = np.block([[h, h], [h, -h]])
    return h


def host_constants():
    bf = ml_dtypes.bfloat16
    ct0, st0, ct1, st1 = _rope_tables()
    h128 = _hadamard(128).astype(bf)
    psw = np.zeros((128, 128), np.float32)
    for i in range(64):
        psw[2 * i + 1, 2 * i] = 1.0
        psw[2 * i, 2 * i + 1] = 1.0
    psw = psw.astype(bf)
    ph0 = np.zeros((128, 128), np.float32)
    for i in range(64):
        ph0[i + 64, i] = 1.0
        ph0[i, i + 64] = 1.0
    ph0 = ph0.astype(bf)
    r = np.arange(128)[:, None]
    c = np.arange(640)[None, :]
    mband = np.where((c >= r) & (c < r + WIN), 0.0, NEG).astype(np.float32)
    ohc = np.zeros((128, 16 * 16), np.float32)
    for h in range(16):
        ohc[:, 16 * h + h] = 1.0
    ohc = ohc.astype(bf)
    ohr = np.zeros((16, 16 * 128), np.float32)
    for h in range(16):
        ohr[h, 128 * h:128 * (h + 1)] = 1.0
    ones_col = np.ones((128, 1), np.float32).astype(bf)
    cmean_col = np.full((128, 1), 1.0 / 2048.0, np.float32).astype(bf)
    ones_row = np.ones((1, 128), np.float32)
    acv = np.array(ACT_CONSTS, np.float32)
    actc = np.broadcast_to(acv[None, :], (128, len(ACT_CONSTS))).copy()
    return dict(
        c_actc=actc,
        c_ct0=ct0, c_st0=st0, c_ct1=ct1, c_st1=st1,
        c_h128=h128, c_psw=psw, c_psw0=ph0, c_mband=mband,
        c_ohc=ohc, c_ohr=ohr.astype(np.float32),
        c_ones=ones_col, c_cmean=cmean_col, c_onesrow=ones_row,
        c_id=np.eye(128, dtype=np.float32),
    )



class Sem:
    __slots__ = ("name", "h", "n")

    def __init__(self, name, handle):
        self.name = name
        self.h = handle
        self.n = 0


class Ev:
    __slots__ = ("sem", "n", "eng")

    def __init__(self, sem, n, eng):
        self.sem = sem
        self.n = n
        self.eng = eng


def hard(ev):
    return Ev(ev.sem, ev.n, "hard") if ev is not None else None


class Prog:

    def __init__(self, nc):
        self.nc = nc
        self.streams = {e: [] for e in ENGINES}
        self.sems = {}
        for name in ("pe", "dv", "ac", "pl"):
            self.sems[name] = Sem(name, nc.alloc_semaphore("s_" + name))
        self.dsems = {}
        self.bank_war = {}
        self.nsteps = 0

    def go(self, eng, fn, waits=(), inc=None, inck=1):
        waits = tuple(w for w in waits if w is not None)
        for w in waits:
            assert w.n <= w.sem.n, f"wait on future event {w.sem.name}:{w.n}>{w.sem.n}"
        self.streams[eng].append((waits, fn, inc, inck))
        self.nsteps += 1
        if inc is not None:
            inc.n += inck
            return Ev(inc, inc.n, "dma" if eng in ("sync",) and inck >= 16 else eng)
        return None

    def t(self, fn, waits=(), inc=True):
        return self.go("tensor", fn, waits, self.sems["pe"] if inc else None)

    def v(self, fn, waits=(), inc=True):
        return self.go("vector", fn, waits, self.sems["dv"] if inc else None)

    def s(self, fn, waits=(), inc=True):
        return self.go("scalar", fn, waits, self.sems["ac"] if inc else None)

    def g(self, fn, waits=(), inc=True):
        return self.go("gpsimd", fn, waits, self.sems["pl"] if inc else None)

    def _dsem(self, name):
        if name not in self.dsems:
            self.dsems[name] = Sem(name, self.nc.alloc_semaphore("d_" + name))
        return self.dsems[name]

    def dma_s(self, out, in_, waits=(), sem="misc_s"):
        ev = self.go("sync", lambda e, o=out, i=in_: e.dma_start(out=o, in_=i),
                     waits, self._dsem(sem), 16)
        return Ev(ev.sem, ev.n, "dma")

    def dma_g(self, out, in_, waits=(), sem="misc_g"):
        ev = self.go("gpsimd", lambda e, o=out, i=in_: e.dma_start(out=o, in_=i),
                     waits, self._dsem(sem), 16)
        return Ev(ev.sem, ev.n, "dma")

    def fence(self, evs, engines=ENGINES):
        evs = [e for e in evs if e is not None]
        for eng in engines:
            self.go(eng, None, tuple(Ev(e.sem, e.n, "dma") for e in evs))

    def replay(self, block):
        for eng in ENGINES:
            def body(e, eng=eng):
                seen = {}
                for (waits, fn, inc, inck) in self.streams[eng]:
                    for w in waits:
                        if w.eng == eng:
                            continue
                        if seen.get(w.sem.name, -1) >= w.n:
                            continue
                        e.wait_ge(w.sem.h, w.n)
                        seen[w.sem.name] = w.n
                    if fn is None:
                        continue
                    ins = fn(e)
                    if inc is not None:
                        ins.then_inc(inc.h, inck)
            getattr(block, eng)(body)



class KB:
    def __init__(self, nc, sublayers):
        self.nc = nc
        self.sub = tuple(sublayers)
        self.p = Prog(nc)

        dt = nc.dram_tensor
        self.xT = dt("xT", [D, T], F32, kind="ExternalInput").ap()
        self.io = {}
        for nm, sh, ty in [
            ("l0_wq", [D, 2048], F32), ("l0_wk", [D, 512], F32),
            ("l0_wv", [D, 512], F32), ("l0_wo", [D, 2048], F32),
            ("l0_wupT", [D, 4096], F32), ("l0_wdnT", [4096, D], F32),
            ("l1_wqT", [D, 2048], F32), ("l1_wkT", [D, 2048], F32),
            ("l1_wvT", [D, 2048], F32), ("l1_woT", [D, 2048], F32),
            ("l1_fw", [D, 2048], F32), ("l1_fscale", [128, 16], F32),
            ("c_ct0", [128, T], BF16), ("c_st0", [128, T], BF16),
            ("c_ct1", [128, T], BF16), ("c_st1", [128, T], BF16),
            ("c_h128", [128, 128], BF16), ("c_psw", [128, 128], BF16),
            ("c_psw0", [128, 128], BF16),
            ("c_mband", [128, 640], F32), ("c_ohc", [128, 256], BF16),
            ("c_ohr", [16, 2048], F32), ("c_ones", [128, 1], BF16),
            ("c_cmean", [128, 1], BF16), ("c_onesrow", [1, 128], F32),
            ("c_id", [128, 128], F32),
            ("c_actc", [128, len(ACT_CONSTS)], F32),
        ]:
            self.io[nm] = dt(nm, sh, ty, kind="ExternalInput").ap()
        self.xres = dt("xres", [D, T], BF16, kind="Internal").ap()
        self.ternup = dt("ternup", [D, 4096], BF16, kind="Internal").ap()
        self.terndn = dt("terndn", [4096, D], BF16, kind="Internal").ap()
        self.outT = dt("outT", [D, T], F32, kind="ExternalOutput").ap()

        A = nc.alloc_sbuf_tensor
        self.arA = A("arA", [128, 16384], BF16).ap()
        self.arB = A("arB", [128, 16384], BF16).ap()
        self.arC = A("arC", [128, 32768], BF16).ap()
        ovl_t = A("ovl", [128, 20480], BF16)
        ob = ovl_t.ap()
        obf = ovl_t.bitcast(F32).ap()
        self.ring = ob[:, 0:3840]
        self.ct = ob[:, 3840:5888]
        self.st = ob[:, 5888:7936]
        self.sq = ob[:, 7936:8960]
        self.rtmp = ob[:, 8960:9984]
        self.mband = obf[:, 4992:5632]
        self.ohc = ob[:, 11264:11520]
        self.ohr = obf[0:16, 5760:7808]
        self.fscr = obf[:, 0:8192]
        self.wf = obf[:, 8192:9216]
        self.of = obf[:, 9216:10240]
        self.ws = A("ws", [128, 8192], BF16).ap()
        self.srow = A("srow", [128, 2048], F32).ap()
        self.rb = A("rb", [128, 1024], F32).ap()
        self.xt = A("xt", [128, 2048], BF16).ap()
        self.cp = {}
        for nm, sh, ty in [("c_h128", [128, 128], BF16), ("c_psw", [128, 128], BF16),
                           ("c_psw0", [128, 128], BF16),
                           ("c_ones", [128, 1], BF16), ("c_cmean", [128, 1], BF16),
                           ("c_onesrow", [1, 128], F32), ("c_id", [128, 128], F32)]:
            self.cp[nm] = A(nm + "_s", sh, ty).ap()
        self.scal = A("scal", [128, 32], F32).ap()
        self.actc = A("actc", [128, len(ACT_CONSTS)], F32).ap()
        self.tchs = A("tchs", [128, 2], BF16).ap()[:, 0:1]
        for j, v in enumerate(ACT_CONSTS):
            nc.const_aps.register(F32, float(v), self.actc[:, j:j + 1])
        rem = nc.sbuf_bytes_remaining
        assert rem >= 0, f"sbuf over budget by {-rem}"
        self.sbuf_rem = rem
        self.first = min(self.sub)
        self.xt_war = [None, None]
        self.sq_war = [None, None]
        self.rb_war = [None, None]
        self.row_war = []
        self.ws_war = [None, None]
        self.kv_war = None
        self.phase_end = []
        self.ring_war = [None] * 6
        self.ru_war = []
        self.rn_war = []
        self.wf_war = None
        self.wf_war2 = None
        self.am_war = None
        self.lam_war = None
        self.mu_war = None
        self.t1_war = None
        self.ot_war = [None, None]
        self.limit_passes = None

        self.pbank = [nc.alloc_psum_tensor(f"pbk{i}", [128, 512], F32).ap()
                      for i in range(8)]
        self.bwar = [None] * 8

        evs = [self.p.dma_s(self.cp[nm], self.io[nm], sem="const")
               for nm in self.cp]
        evs.append(self.p.dma_s(self.actc, self.io["c_actc"], sem="const"))
        self.p.fence(evs)
        self._preload_act()

    def _preload_act(self):
        idx = None
        try:
            from concourse.hw_specs import get_activation_tables
            tabs = get_activation_tables(self.nc.m.arch)
            for i, fns in enumerate(tabs.values()):
                if AF.Ln in fns and AF.Exp in fns:
                    idx = i
                    break
        except Exception:
            idx = 6
        if idx is None:
            return
        def emit(e, idx=idx):
            return e.add_instruction(mybir.InstLoadActFuncSet(
                name=self.nc.get_next_instruction_name(),
                act_func_set_id=idx, ins=[], outs=[]))
        self.p.s(emit, inc=False)

    def mm(self, out_ap, lhsT, rhs, banks, waits=(), start=True, stop=True):
        w = list(waits)
        if start:
            for b in banks:
                if self.bwar[b] is not None:
                    w.append(self.bwar[b])
                    self.bwar[b] = None
        return self.p.t(lambda e, o=out_ap, l=lhsT, r=rhs, st=start, sp=stop:
                        e.matmul(o, l, r, start=st, stop=sp), w)

    def read_bank(self, b, ev):
        self.bwar[b] = ev

    def xsrc(self, i):
        return (self.xT, True) if i == self.first else (self.xres, False)

    def xdst(self, i):
        return (self.outT, True) if i == max(self.sub) else (self.xres, False)

    def load_x2(self, src, cast, c0, t0, g, waits=()):
        sap = src.rearrange("(kc p) t -> p kc t", p=128)[:, c0:c0 + 2, t0:t0 + 512]
        dstv = self.xt[:, g * 1024:(g + 1) * 1024].rearrange(
            "p (kc t) -> p kc t", t=512)
        return (self.p.dma_g if cast else self.p.dma_s)(dstv, sap, waits,
                                                        sem=f"xt{g}")

    def xsl(self, g, j):
        return self.xt[:, g * 1024 + j * 512: g * 1024 + (j + 1) * 512]

    def load_wchunk(self, w_ap, din, fo, par, waits=(), cast=True, touch=True):
        kch = din // 128
        src = w_ap.rearrange("(kc p) f -> p kc f", p=128)[
            :, :, fo * 128:(fo + 1) * 128]
        dst = self.ws[:, par * 4096:par * 4096 + kch * 128].rearrange(
            "p (kc f) -> p kc f", f=128)
        ev = (self.p.dma_g if cast else self.p.dma_s)(dst, src, waits,
                                                      sem=f"ws{par}")
        if touch:
            tc = self.ws[:, par * 4096 + kch * 128 - 1: par * 4096 + kch * 128]
            ev = self.p.v(lambda e, o=self.tchs, i=tc: e.tensor_copy(o, i),
                          waits=[ev])
        return ev

    def ws_tile(self, par, kc):
        return self.ws[:, par * 4096 + kc * 128: par * 4096 + (kc + 1) * 128]

    def yb(self, c, a, b):
        return self.arA[:, c * self.nq + a: c * self.nq + b]

    def qb(self, c, a, b):
        return self.arB[:, c * self.nq + a: c * self.nq + b]

    def obuf(self, c, a, b):
        return self.arA[:, c * self.nq + a: c * self.nq + b]

    def kb(self, c, a, b):
        return self.arC[:, c * self.nkv + a: c * self.nkv + b]

    def vb(self, tile, a, b):
        off = self.kvch * self.nkv
        return self.arC[:, off + tile * self.fv + a: off + tile * self.fv + b]

    def ringst(self, kt, a, b):
        s = (kt % 6) * 640
        return self.ring[:, s + a: s + b]

    def emit_ln(self, xap, cast, t0, tl):
        p = self.p
        PR1 = self.pbank[2][0:1, 0:512]
        PR2 = self.pbank[3][0:1, 0:512]
        cme = self.cp["c_cmean"]
        mm1 = mm2 = None
        sq0 = self.sq[:, 0:512]
        sq1 = self.sq[:, 512:1024]
        for g in range(8):
            c0 = 2 * g
            evd = self.load_x2(xap, cast, c0, t0, g % 2,
                               waits=[self.xt_war[g % 2]])
            for j in range(2):
                xs = self.xsl(g % 2, j)
                sqs = sq0 if j == 0 else sq1
                evq = p.s(lambda e, o=sqs, i=xs: e.activation(o, i, AF.Square),
                          waits=[evd, self.sq_war[j]])
                first = (g == 0 and j == 0)
                last = (g == 7 and j == 1)
                mm1 = self.mm(PR1, cme, xs, [2] if first else [],
                              waits=[evd], start=first, stop=last)
                mm2 = self.mm(PR2, cme, sqs, [3] if first else [],
                              waits=[evq], start=first, stop=last)
                self.sq_war[j] = mm2
            self.xt_war[g % 2] = mm1
        mrow = self.srow[0:1, 1024:1536]
        vrow = self.srow[0:1, 1536:2048]
        evm = p.s(lambda e, o=mrow, i=PR1: e.copy(o, i),
                  waits=[mm1] + self.row_war)
        ev2 = p.v(lambda e, o=vrow, a=mrow: e.tensor_mul(o, a, a),
                  waits=[evm] + self.row_war)
        ev3 = p.v(lambda e, o=vrow, a=PR2, b=vrow: e.tensor_sub(o, a, b),
                  waits=[mm2])
        self.read_bank(2, evm)
        self.read_bank(3, ev3)
        ev4 = p.s(lambda e, o=vrow, i=vrow: e.activation(o, i, AF.Ln, bias=1e-5),
                  waits=[ev3])
        evr = p.s(lambda e, o=vrow, i=vrow: e.activation(o, i, AF.Exp, scale=-0.5),
                  waits=[ev4])
        bm = self.mm(self.pbank[6], self.cp["c_onesrow"], mrow, [6], waits=[evm])
        br = self.mm(self.pbank[7], self.cp["c_onesrow"], vrow, [7], waits=[evr])
        RBm = self.rb[:, 0:512]
        RBr = self.rb[:, 512:1024]
        evbm = p.s(lambda e, o=RBm, i=self.pbank[6]: e.copy(o, i),
                   waits=[bm, self.rb_war[0]])
        evbr = p.s(lambda e, o=RBr, i=self.pbank[7]: e.copy(o, i),
                   waits=[br, self.rb_war[1]])
        self.read_bank(6, evbm)
        self.read_bank(7, evbr)
        self.row_war = [bm, br, ev2]
        evy = None
        for g in range(8):
            c0 = 2 * g
            evd = self.load_x2(xap, cast, c0, t0, g % 2,
                               waits=[self.xt_war[g % 2]])
            for j in range(2):
                c = c0 + j
                xs = self.xsl(g % 2, j)
                rt = self.rtmp[:, j * 512:(j + 1) * 512]
                ev5 = p.v(lambda e, o=rt, a=xs, b=RBm: e.tensor_sub(o, a, b),
                          waits=[evd, evbm])
                ys = self.yb(c, tl, tl + 512)
                evy = p.v(lambda e, o=ys, a=rt, b=RBr: e.tensor_mul(o, a, b),
                          waits=[evbr])
                self.xt_war[g % 2] = ev5
        self.rb_war[0] = evy
        self.rb_war[1] = evy
        return evy

    def emit_proj(self, wap, nfo, rhs_fn, ntt, out_fn, waits, scale=1.0):
        p = self.p
        eve = None
        for fo in range(nfo):
            par = fo % 2
            wev = self.load_wchunk(wap, 2048, fo, par, waits=[self.ws_war[par]])
            for tt in range(ntt):
                bk = (fo * ntt + tt) % 2
                pbx = self.pbank[bk]
                mmev = None
                for kc in range(16):
                    mmev = self.mm(pbx, self.ws_tile(par, kc), rhs_fn(kc, tt),
                                   [bk] if kc == 0 else [],
                                   waits=([wev] + list(waits)) if kc == 0 else [],
                                   start=(kc == 0), stop=(kc == 15))
                eve = p.s(lambda e, o=out_fn(fo, tt), i=pbx, sc=scale:
                          e.activation(o, i, AF.Copy, scale=sc), waits=[mmev])
                self.read_bank(bk, eve)
                self.ws_war[par] = mmev
        return eve

    def emit_attn(self, si, L):
        p = self.p
        nq = 1024 if L == 0 else 512
        self.nq = nq
        self.nkv = nq + 512
        self.kvch = 4 if L == 0 else 16
        self.fv = 512 if L == 0 else 2048
        rep = 4 if L == 0 else 1
        names = (("l0_wq", "l0_wk", "l0_wv", "l0_wo", "c_ct0", "c_st0") if L == 0
                 else ("l1_wqT", "l1_wkT", "l1_wvT", "l1_woT", "c_ct1", "c_st1"))
        wq, wk, wv, wo, ctn, stn = names
        xap, cast = self.xsrc(si)
        dap, fin = self.xdst(si)
        npass = T // nq if self.limit_passes is None else self.limit_passes

        pw = list(self.phase_end)
        evs = [p.dma_s(self.ct, self.io[ctn], waits=pw, sem="const"),
               p.dma_s(self.st, self.io[stn], waits=pw, sem="const"),
               p.dma_s(self.mband, self.io["c_mband"], waits=pw, sem="const"),
               p.dma_s(self.ohc, self.io["c_ohc"], waits=pw, sem="const"),
               p.dma_s(self.ohr, self.io["c_ohr"], waits=pw, sem="const")]
        p.fence(evs + pw)

        for ps in range(npass):
            if ps > 0:
                p.fence(list(self.phase_end))
            Q0 = ps * nq
            B0 = max(0, Q0 - 512)
            halo = Q0 - B0
            nkt = (Q0 + nq - B0) // 128
            hkt = halo // 128

            cpev = []
            if ps > 0:
                B0p = max(0, Q0 - nq - 512)
                soff = Q0 - 512 - B0p
                for c in range(self.kvch):
                    src = self.kb(c, soff, soff + 512)
                    dst = self.kb(c, 0, 512)
                    cpev.append(p.g(lambda e, o=dst, i=src: e.tensor_copy(o, i),
                                    waits=[self.kv_war]))
                stile = soff // 128
                for tl in range(4):
                    src = self.vb(stile + tl, 0, self.fv)
                    dst = self.vb(tl, 0, self.fv)
                    cpev.append(p.g(lambda e, o=dst, i=src: e.tensor_copy(o, i),
                                    waits=[self.kv_war]))
            cpall = cpev[-1] if cpev else None

            evy = None
            for tt in range(nq // 512):
                evy = self.emit_ln(xap, cast, Q0 + tt * 512, tt * 512)
            ylast = evy

            def yrhs(kc, tt):
                return self.yb(kc, tt * 512, (tt + 1) * 512)
            evk = self.emit_proj(self.io[wk], self.kvch, yrhs, nq // 512,
                                 lambda fo, tt: self.kb(fo, halo + tt * 512,
                                                        halo + (tt + 1) * 512),
                                 waits=[ylast] + cpev)
            evv = None
            for fc in range(self.fv // 512):
                src = self.io[wv].rearrange("(kc p) f -> p kc f", p=128)[
                    :, :, fc * 512:(fc + 1) * 512]
                dstw = self.ws.rearrange("p (kc f) -> p kc f", f=512)
                wev = p.dma_g(dstw, src, waits=[self.ws_war[0], self.ws_war[1]],
                              sem="wsv")
                for b in range(nq // 128):
                    bk = (fc * (nq // 128) + b) % 2
                    pbx = self.pbank[bk]
                    mmev = None
                    for kc in range(16):
                        lt = self.yb(kc, b * 128, (b + 1) * 128)
                        rt = self.ws[:, kc * 512:(kc + 1) * 512]
                        mmev = self.mm(pbx, lt, rt, [bk] if kc == 0 else [],
                                       waits=([wev, ylast] + cpev) if kc == 0 else [],
                                       start=(kc == 0), stop=(kc == 15))
                    dst = self.vb(hkt + b, fc * 512, (fc + 1) * 512)
                    evv = p.s(lambda e, o=dst, i=pbx: e.copy(o, i), waits=[mmev])
                    self.read_bank(bk, evv)
                    self.ws_war[0] = mmev
                    self.ws_war[1] = mmev
            evq = self.emit_proj(self.io[wq], 16, yrhs, nq // 512,
                                 lambda fo, tt: self.qb(fo, tt * 512, (tt + 1) * 512),
                                 waits=[ylast], scale=(1.0 if L == 0 else SCALE))

            evrope_q = evq
            evrope_k = evk
            if L == 0:
                evrope_q = self.emit_l2norm(True, nq, Q0, evq)
                evrope_k = self.emit_l2norm(False, nq, Q0, evk, halo=halo)
            evrope_q = self.emit_rope(L, True, nq, Q0, evrope_q, halo=0)
            evrope_k = self.emit_rope(L, False, nq, Q0, evrope_k, halo=halo)

            evob = self.emit_attn_core(L, nq, Q0, B0, halo, nkt, rep,
                                       [evrope_q, evrope_k, evv] + cpev,
                                       proj_done=[evq, evk, evv])
            self.kv_war = evob

            self.emit_out_proj(self.io[wo], nq, Q0, xap, cast, dap, fin,
                               ob_ready=evob)
            self.phase_end = [e for e in self.ot_war if e]
        return self.phase_end

    def emit_l2norm(self, qside, nq, Q0, ready, halo=0):
        p = self.p
        nch = 16 if qside else self.kvch
        STAT = self.pbank[2][0:16, 0:512]
        rnoff = 0 if qside else 512
        RN = self.srow[0:16, rnoff:rnoff + 512]
        evn = ready
        for tt in range(nq // 512):
            mmev = None
            for c in range(nch):
                sl = (self.qb(c, tt * 512, (tt + 1) * 512) if qside else
                      self.kb(c, halo + tt * 512, halo + (tt + 1) * 512))
                sqs = self.sq[:, (c % 2) * 512:(c % 2 + 1) * 512]
                evq = p.s(lambda e, o=sqs, i=sl: e.activation(o, i, AF.Square),
                          waits=[ready, self.sq_war[c % 2]])
                mmev = self.mm(STAT, self.ohc[:, c * 16:(c + 1) * 16], sqs,
                               [2] if c == 0 else [], waits=[evq],
                               start=(c == 0), stop=(c == nch - 1))
                self.sq_war[c % 2] = mmev
            e1 = p.s(lambda e, o=RN, i=STAT: e.activation(o, i, AF.Ln, bias=1e-38),
                     waits=[mmev] + list(self.rn_war))
            self.read_bank(2, e1)
            bias = math.log(SCALE) if qside else 0.0
            mv = 1e12 * SCALE if qside else 1e12
            e2 = p.s(lambda e, o=RN, i=RN, b=bias:
                     e.activation(o, i, AF.Exp, scale=-0.5, bias=b), waits=[])
            e3 = p.v(lambda e, o=RN, i=RN, m=mv: e.tensor_scalar_min(o, i, m),
                     waits=[e2])
            evn = None
            bc = None
            for c in range(nch):
                bk = 6 + (c % 2)
                bc = self.mm(self.pbank[bk], self.ohr[0:16, c * 128:(c + 1) * 128],
                             RN, [bk], waits=[e3])
                sl = (self.qb(c, tt * 512, (tt + 1) * 512) if qside else
                      self.kb(c, halo + tt * 512, halo + (tt + 1) * 512))
                evn = p.v(lambda e, o=sl, a=self.pbank[bk], b=sl:
                          e.tensor_mul(o, a, b), waits=[bc])
                self.read_bank(bk, evn)
            self.rn_war = [bc]
        return evn

    def emit_rope(self, L, qside, nq, Q0, ready, halo=0):
        p = self.p
        nch = 16 if qside else self.kvch
        evn = ready
        for tt in range(nq // 512):
            ta = Q0 + tt * 512
            cts = self.ct[:, ta:ta + 512]
            sts = self.st[:, ta:ta + 512]
            for c in range(nch):
                sl = (self.qb(c, tt * 512, (tt + 1) * 512) if qside else
                      self.kb(c, halo + tt * 512, halo + (tt + 1) * 512))
                rt = self.rtmp[:, (c % 2) * 512:(c % 2 + 1) * 512]
                perm = self.cp["c_psw0" if L == 0 else "c_psw"]
                bk = 4
                sw = self.mm(self.pbank[bk], perm, sl, [bk], waits=[ready])
                ev = p.v(lambda e, o=rt, a=self.pbank[bk], b=sts:
                         e.tensor_mul(o, a, b), waits=[sw])
                self.read_bank(bk, ev)
                p.v(lambda e, o=sl, a=sl, b=cts: e.tensor_mul(o, a, b), waits=[])
                evn = p.v(lambda e, o=sl, a=sl, b=rt: e.tensor_add(o, a, b),
                          waits=[])
        return evn

    def emit_attn_core(self, L, nq, Q0, B0, halo, nkt, rep, ready, proj_done):
        p = self.p
        hkt = halo // 128
        nqt = nq // 128
        evob = None
        first_ob = list(proj_done)
        for h in range(NH):
            kc_h = h // rep if L == 0 else h
            strip_ev = {}
            po_mm = None
            su_mm = None
            av_first = True
            for kt in range(nkt):
                K0 = B0 + kt * 128
                c_lo = max(K0, Q0) - K0
                c_hi = min(K0 + 640, Q0 + nq) - K0
                par = kt % 2
                SPA = self.pbank[par]
                SPB = self.pbank[2 + par][:, 0:128]
                ktile = self.kb(kc_h, kt * 128, (kt + 1) * 128)
                a0, a1 = c_lo, min(c_hi, 512)
                evm = []
                if a1 > a0:
                    m1 = self.mm(SPA[:, a0:a1], ktile,
                                 self.qb(h, K0 + a0 - Q0, K0 + a1 - Q0),
                                 [par], waits=ready)
                    e1 = p.v(lambda e, o=SPA[:, a0:a1], i=self.mband[:, a0:a1]:
                             e.tensor_add(o, o, i), waits=[m1])
                    evm.append((SPA, a0, a1, e1))
                if c_hi > 512:
                    b0, b1 = 512, c_hi
                    m2 = self.mm(SPB[:, 0:b1 - b0], ktile,
                                 self.qb(h, K0 + b0 - Q0, K0 + b1 - Q0),
                                 [2 + par], waits=ready)
                    e2 = p.v(lambda e, o=SPB[:, 0:b1 - b0],
                             i=self.mband[:, b0:b1]:
                             e.tensor_add(o, o, i), waits=[m2])
                    evm.append((SPB, b0, b1, e2))
                rw = self.ring_war[kt % 6]
                eexp = None
                for (src, x0, x1, ev) in evm:
                    off = 0 if src is SPA else 512
                    eexp = p.s(lambda e, o=self.ringst(kt, x0, x1),
                               i=src[:, x0 - off:x1 - off]:
                               e.activation(o, i, AF.Exp),
                               waits=[ev] + ([rw] if rw else []))
                    bkid = par if src is SPA else 2 + par
                    self.read_bank(bkid, eexp)
                strip_ev[kt] = eexp
                if K0 >= Q0:
                    qt = (K0 - Q0) // 128
                    q4 = qt % 4
                    kts = list(range(max(0, kt - 4), kt + 1))
                    for i2, kt2 in enumerate(kts):
                        K02 = B0 + kt2 * 128
                        c0 = K0 - K02
                        rsl = self.ringst(kt2, c0, c0 + 128)
                        vt = self.vb(kt2, kc_h * 128 if L == 0 else h * 128,
                                     (kc_h * 128 if L == 0 else h * 128) + 128)
                        st_first = av_first and i2 == 0
                        po_mm = self.mm(self.pbank[4][:, q4 * 128:(q4 + 1) * 128],
                                        vt, rsl, [4] if st_first else [],
                                        waits=[strip_ev[kt2]],
                                        start=(i2 == 0), stop=(i2 == len(kts) - 1))
                        su_mm = self.mm(self.pbank[5][0:1, q4 * 128:(q4 + 1) * 128],
                                        self.cp["c_ones"], rsl,
                                        [5] if st_first else [],
                                        waits=[], start=(i2 == 0),
                                        stop=(i2 == len(kts) - 1))
                        self.ring_war[kt2 % 6] = su_mm
                    av_first = False
                    if q4 == 3 or qt == nqt - 1:
                        ng = (q4 + 1) * 128
                        qg0 = (qt - q4) * 128
                        ru = self.srow[0:1, 0:512]
                        evru = p.v(lambda e, o=ru[:, 0:ng],
                                   i=self.pbank[5][0:1, 0:ng]:
                                   e.reciprocal(o, i),
                                   waits=[su_mm] + self.ru_war)
                        self.read_bank(5, evru)
                        bc = self.mm(self.pbank[6][:, 0:ng], self.cp["c_onesrow"],
                                     ru[:, 0:ng], [6], waits=[evru])
                        self.ru_war = [bc]
                        dst = self.obuf(h, qg0, qg0 + ng)
                        epo = p.s(lambda e, o=dst, i=self.pbank[4][:, 0:ng]:
                                  e.copy(o, i), waits=[po_mm] + first_ob)
                        first_ob = []
                        self.read_bank(4, epo)
                        evob = p.v(lambda e, o=dst, a=self.pbank[6][:, 0:ng], b=dst:
                                   e.tensor_mul(o, a, b), waits=[bc, epo])
                        self.read_bank(6, evob)
                        av_first = True
        return evob

    def emit_out_proj(self, wap, nq, Q0, xap, cast, dap, fin, ob_ready):
        p = self.p
        evout = None
        for fo in range(16):
            par = fo % 2
            wev = self.load_wchunk(wap, 2048, fo, par, waits=[self.ws_war[par]])
            for tt in range(nq // 512):
                bk = (fo * (nq // 512) + tt) % 2
                pbx = self.pbank[bk]
                mmev = None
                for kc in range(16):
                    mmev = self.mm(pbx, self.ws_tile(par, kc),
                                   self.obuf(kc, tt * 512, (tt + 1) * 512),
                                   [bk] if kc == 0 else [],
                                   waits=[wev, ob_ready] if kc == 0 else [],
                                   start=(kc == 0), stop=(kc == 15))
                self.ws_war[par] = mmev
                g = tt % 2
                sx = xap.rearrange("(kc p) t -> kc p t", p=128)[
                    fo, :, Q0 + tt * 512:Q0 + (tt + 1) * 512]
                dx = self.xt[:, g * 1024:g * 1024 + 512]
                evd = (p.dma_g if cast else p.dma_s)(dx, sx,
                                                     waits=[self.xt_war[g]],
                                                     sem=f"xt{g}")
                if fin:
                    otile = self.of[:, (tt % 2) * 512:(tt % 2) * 512 + 512]
                else:
                    otile = self.rtmp[:, (tt % 2) * 512:(tt % 2) * 512 + 512]
                evo = p.v(lambda e, o=otile, a=pbx, b=dx: e.tensor_add(o, a, b),
                          waits=[mmev, evd, self.ot_war[tt % 2]])
                self.read_bank(bk, evo)
                self.xt_war[g] = evo
                dd = dap.rearrange("(kc p) t -> kc p t", p=128)[
                    fo, :, Q0 + tt * 512:Q0 + (tt + 1) * 512]
                evout = p.dma_s(dd, otile, waits=[evo], sem=f"st{tt % 2}")
                self.ot_war[tt % 2] = evout
        return evout

    def emit_wprep(self, wap, tern, rows, cols, sidx):
        p = self.p
        PR = self.pbank[2][0:1, 0:512]
        ntile = (rows // 128) * (cols // 512)
        mmev = None
        i = 0
        for r in range(rows // 128):
            for cc in range(cols // 512):
                sl = wap.rearrange("(kc p) f -> kc p f", p=128)[
                    r, :, cc * 512:(cc + 1) * 512]
                evd = p.dma_s(self.wf[:, 0:512], sl, waits=[self.wf_war],
                              sem="wf")
                ab = self.sq[:, 0:512]
                eva = p.s(lambda e, o=ab, i_=self.wf[:, 0:512]:
                          e.activation(o, i_, AF.Abs), waits=[evd, self.sq_war[0]])
                self.wf_war = eva
                mmev = self.mm(PR, self.cp["c_ones"], ab, [2] if i == 0 else [],
                               waits=[eva], start=(i == 0), stop=(i == ntile - 1))
                self.sq_war[0] = mmev
                i += 1
        tot = self.scal[0:1, sidx:sidx + 1]
        ev1 = p.v(lambda e, o=tot, i=PR: e.tensor_reduce(o, i, AX.X, ALU.add),
                  waits=[mmev])
        self.read_bank(2, ev1)
        N = float(rows * cols)
        sval = self.scal[0:1, sidx + 1:sidx + 2]
        ev2 = p.v(lambda e, o=sval, i=tot, n=1.0 / N:
                  e.tensor_scalar(o, i, n, EPSQ, ALU.mult, ALU.max),
                  waits=[hard(ev1)])
        rs = self.scal[0:1, sidx + 2:sidx + 3]
        ev3 = p.v(lambda e, o=rs, i=sval: e.reciprocal(o, i),
                  waits=[hard(ev2)])
        bc = self.mm(self.pbank[6][:, 0:1], self.cp["c_onesrow"], rs, [6],
                     waits=[ev3])
        rscol = self.scal[:, sidx + 3:sidx + 4]
        ev4 = p.s(lambda e, o=rscol, i=self.pbank[6][:, 0:1]: e.copy(o, i),
                  waits=[bc])
        self.read_bank(6, ev4)
        evw = None
        for r in range(rows // 128):
            for cc in range(cols // 512):
                sl = wap.rearrange("(kc p) f -> kc p f", p=128)[
                    r, :, cc * 512:(cc + 1) * 512]
                evd = p.dma_s(self.wf[:, 0:512], sl, waits=[self.wf_war],
                              sem="wf")
                t1 = self.wf[:, 512:1024]
                ea = p.s(lambda e, o=t1, i=self.wf[:, 0:512], sc=rscol:
                         e.activation(o, i, AF.Copy, scale=sc, bias=MAGIC),
                         waits=[evd, ev4, self.wf_war2])
                self.wf_war = ea
                tb = self.sq[:, 0:512]
                eb = p.v(lambda e, o=tb, i=t1:
                         e.tensor_scalar(o, i, -MAGIC, 1.0, ALU.add, ALU.min),
                         waits=[ea, self.sq_war[0]])
                self.wf_war2 = eb
                ec = p.g(lambda e, o=tb, i=tb:
                         e.tensor_scalar_max(o, i, -1.0), waits=[eb])
                dl = tern.rearrange("(kc p) f -> kc p f", p=128)[
                    r, :, cc * 512:(cc + 1) * 512]
                evw = p.dma_s(dl, tb, waits=[ec], sem="tern")
                self.sq_war[0] = evw
        return evw

    def emit_butterfly(self, tiles, waits):
        p = self.p
        n = len(tiles)
        ev = None
        s = 1
        first = True
        while s < n:
            for base in range(0, n, 2 * s):
                for k in range(s):
                    a = tiles[base + k]
                    b = tiles[base + k + s]
                    ev = p.v(lambda e, o=a, x=a, y=b: e.tensor_add(o, x, y),
                             waits=waits if first else [])
                    first = False
                    ev = p.v(lambda e, o=b, x=b, y=a:
                             e.scalar_tensor_tensor(o, x, -2.0, y,
                                                    ALU.mult, ALU.add), waits=[])
            s *= 2
        return ev

    def emit_actquant(self, nch, out_fn, eps_const, eps_row, waits, w=512):
        p = self.p
        M = self.rb[:, 0:512][:, 0:w]
        fv = self.fscr[:, 0:nch * w].rearrange("p (c w) -> p w c", w=w)
        ev = p.v(lambda e, o=M, i=fv:
                 e.tensor_reduce(o, i, AX.X, ALU.max, apply_absolute_value=True),
                 waits=list(waits) + ([self.rb_war[0]] if self.rb_war[0] else []))
        PR = self.pbank[3][0:1, 0:w]
        evtr = None
        for q in range(w // 128):
            tw = [ev] + ([self.bwar[2]] if self.bwar[2] else [])
            self.bwar[2] = None
            tev = p.t(lambda e, o=self.pbank[2][:, 0:128],
                      i=M[:, q * 128:(q + 1) * 128], idn=self.cp["c_id"]:
                      e.transpose(o, i, idn), waits=tw)
            col = self.srow[:, 4 + q:5 + q]
            rev = p.v(lambda e, o=col, i=self.pbank[2][:, 0:128]:
                      e.tensor_reduce(o, i, AX.X, ALU.max), waits=[tev])
            self.read_bank(2, rev)
            tw2 = [rev] + ([self.bwar[3]] if self.bwar[3] else [])
            self.bwar[3] = None
            evtr = p.t(lambda e, o=PR[:, q * 128:(q + 1) * 128], i=col,
                       idn=self.cp["c_id"]: e.transpose(o, i, idn), waits=tw2)
        self.rb_war[0] = None
        am = self.srow[0:1, 0:w]
        aw = [evtr] + ([self.am_war] if self.am_war else [])
        if eps_row is None:
            evam = p.v(lambda e, o=am, i=PR, cst=eps_const:
                       e.tensor_scalar_max(o, i, cst), waits=aw)
        else:
            evam = p.v(lambda e, o=am, a=PR, b=eps_row:
                       e.tensor_tensor(o, a, b, ALU.max), waits=aw)
        self.read_bank(3, evam)
        sc = self.srow[0:1, 512:512 + w]
        p.v(lambda e, o=sc, i=am: e.reciprocal(o, i), waits=[])
        ev6 = p.v(lambda e, o=sc, i=sc: e.tensor_scalar_mul(o, i, 127.0),
                  waits=[])
        bc = self.mm(self.pbank[6][:, 0:w], self.cp["c_onesrow"], sc, [6],
                     waits=[ev6])
        SCB = self.rb[:, 512:512 + w]
        sw = [bc] + ([self.rb_war[1]] if self.rb_war[1] else [])
        evb = p.s(lambda e, o=SCB, i=self.pbank[6][:, 0:w]: e.copy(o, i),
                  waits=sw)
        self.read_bank(6, evb)
        self.am_war = evb
        evq = None
        for c in range(nch):
            z = self.fscr[:, c * w:(c + 1) * w]
            p.v(lambda e, o=z, a=z, b=SCB: e.tensor_mul(o, a, b), waits=[evb])
            evq = p.v(lambda e, o=out_fn(c), i=z:
                      e.tensor_scalar(o, i, MAGIC, -MAGIC, ALU.add, ALU.add),
                      waits=[])
        self.rb_war[1] = evq
        self.rb_war[0] = evq
        return evq, am

    def emit_ffn0(self, si):
        p = self.p
        self.nq = 1024
        xap, cast = self.xsrc(si)
        dap, fin = self.xdst(si)
        p.fence(list(self.phase_end))
        evw_up = self.emit_wprep(self.io["l0_wupT"], self.ternup, 2048, 4096, 0)
        evw_dn = self.emit_wprep(self.io["l0_wdnT"], self.terndn, 4096, 2048, 4)
        EPS1 = EPSQ / C2
        for hf in range(2):
            T0 = hf * 1024
            evy = None
            for tt in range(2):
                evy = self.emit_ln(xap, cast, T0 + tt * 512, tt * 512)
            tiles = [self.yb(c, 0, 1024) for c in range(16)]
            evb = self.emit_butterfly(tiles, [evy])
            lamrow = self.srow[0:1, 1024:2048]
            evq2 = None
            evl = None
            for sub in range(2):
                evf = None
                for c in range(16):
                    bk = c % 2
                    m1 = self.mm(self.pbank[bk], self.cp["c_h128"],
                                 self.yb(c, sub * 512, (sub + 1) * 512),
                                 [bk], waits=[evb, evq2])
                    evf = p.s(lambda e, o=self.fscr[:, c * 512:(c + 1) * 512],
                              i=self.pbank[bk]: e.copy(o, i), waits=[m1])
                    self.read_bank(bk, evf)
                def yqout(c, s=sub):
                    return self.qb(c, s * 512, (s + 1) * 512)
                evq, am = self.emit_actquant(16, yqout, EPS1, None, [evf])
                lm = lamrow[:, sub * 512:(sub + 1) * 512]
                evl = p.v(lambda e, o=lm, i=am, s1=self.scal[0:1, 1:2]:
                          e.tensor_scalar(o, i, s1, C2 / 127.0, ALU.mult,
                                          ALU.mult), waits=[])
                evq2 = evq
            evu = None
            for fo in range(32):
                par = fo % 2
                wev = self.load_wchunk(self.ternup, 2048, fo, par,
                                       waits=[self.ws_war[par], evw_up],
                                       cast=False)
                for tt in range(2):
                    bk = (fo * 2 + tt) % 2
                    pbx = self.pbank[bk]
                    mmev = None
                    for kc in range(16):
                        mmev = self.mm(pbx, self.ws_tile(par, kc),
                                       self.qb(kc, tt * 512, (tt + 1) * 512),
                                       [bk] if kc == 0 else [],
                                       waits=[wev, evq2] if kc == 0 else [],
                                       start=(kc == 0), stop=(kc == 15))
                    dst = self.arC[:, fo * 1024 + tt * 512:
                                   fo * 1024 + (tt + 1) * 512]
                    evu = p.s(lambda e, o=dst, i=pbx: e.activation(o, i, AF.Relu),
                              waits=[mmev])
                    self.read_bank(bk, evu)
                    self.ws_war[par] = mmev
            evsq = None
            for fo in range(32):
                hb = self.arC[:, fo * 1024:(fo + 1) * 1024]
                evsq = p.v(lambda e, o=hb, a=hb, b=hb: e.tensor_mul(o, a, b),
                           waits=[evu])
            tiles = [self.arC[:, c * 1024:(c + 1) * 1024] for c in range(32)]
            evb4 = self.emit_butterfly(tiles, [evsq])
            evq4 = None
            e5 = None
            mu_row = self.of[0:1, 0:1024]
            for sub in range(4):
                lm = lamrow[:, sub * 256:(sub + 1) * 256]
                l2r = self.rb[0:1, 768:1024][:, 0:256]
                p.v(lambda e, o=l2r, a=lm, b=lm: e.tensor_mul(o, a, b),
                    waits=[evl] + ([self.rb_war[1]] if self.rb_war[1] else []))
                ep2 = self.rb[0:1, 512:768][:, 0:256]
                p.v(lambda e, o=ep2, i=l2r: e.reciprocal(o, i), waits=[])
                e3 = p.v(lambda e, o=ep2, i=ep2:
                         e.tensor_scalar_mul(o, i, EPSQ / C4), waits=[])
                evf = None
                for c in range(32):
                    bk = c % 2
                    m1 = self.mm(self.pbank[bk][:, 0:256], self.cp["c_h128"],
                                 self.arC[:, c * 1024 + sub * 256:
                                          c * 1024 + (sub + 1) * 256],
                                 [bk], waits=[evb4, evq4])
                    evf = p.s(lambda e, o=self.fscr[:, c * 256:(c + 1) * 256],
                              i=self.pbank[bk][:, 0:256]: e.copy(o, i),
                              waits=[m1])
                    self.read_bank(bk, evf)
                def hqout(c, s=sub):
                    return self.arC[:, c * 1024 + s * 256:
                                    c * 1024 + (s + 1) * 256]
                evq4, am2 = self.emit_actquant(32, hqout, None, ep2,
                                               [evf, e3], w=256)
                mu = mu_row[:, sub * 256:(sub + 1) * 256]
                p.v(lambda e, o=mu, a=am2, b=l2r: e.tensor_mul(o, a, b),
                    waits=[self.mu_war] if self.mu_war else [])
                e5 = p.v(lambda e, o=mu, i=mu, s2=self.scal[0:1, 5:6]:
                         e.tensor_scalar(o, i, s2, C4 / 127.0, ALU.mult,
                                         ALU.mult), waits=[])
            evmu = []
            for tt in range(2):
                bw = [e5] + [x for x in self.rb_war if x]
                bc = self.mm(self.pbank[6], self.cp["c_onesrow"],
                             mu_row[:, tt * 512:(tt + 1) * 512], [6], waits=bw)
                evx = p.s(lambda e, o=self.rb[:, tt * 512:(tt + 1) * 512],
                          i=self.pbank[6]: e.copy(o, i), waits=[bc])
                self.read_bank(6, evx)
                evmu.append(evx)
            self.rb_war = [evmu[0], evmu[1]]
            self.mu_war = evmu[1]
            evout = None
            for fo in range(16):
                par = fo % 2
                wev = self.load_wchunk(self.terndn, 4096, fo, par,
                                       waits=[self.ws_war[par], evw_dn],
                                       cast=False)
                for tt in range(2):
                    bk = (fo * 2 + tt) % 2
                    pbx = self.pbank[bk]
                    mmev = None
                    for kc in range(32):
                        mmev = self.mm(pbx, self.ws_tile(par, kc),
                                       self.arC[:, kc * 1024 + tt * 512:
                                                kc * 1024 + (tt + 1) * 512],
                                       [bk] if kc == 0 else [],
                                       waits=[wev, evq4] if kc == 0 else [],
                                       start=(kc == 0), stop=(kc == 31))
                    t1 = self.rtmp[:, tt * 512:(tt + 1) * 512]
                    ev1 = p.v(lambda e, o=t1, a=pbx,
                              b=self.rb[:, tt * 512:(tt + 1) * 512]:
                              e.tensor_mul(o, a, b),
                              waits=[mmev, evmu[tt]] +
                                    ([self.t1_war] if self.t1_war else []))
                    self.read_bank(bk, ev1)
                    g = tt % 2
                    sx = xap.rearrange("(kc p) t -> kc p t", p=128)[
                        fo, :, T0 + tt * 512:T0 + (tt + 1) * 512]
                    dx = self.xt[:, g * 1024:g * 1024 + 512]
                    evd = (p.dma_g if cast else p.dma_s)(
                        dx, sx, waits=[self.xt_war[g]], sem=f"xt{g}")
                    if fin:
                        otile = self.of[:, tt * 512:(tt + 1) * 512]
                    else:
                        otile = self.sq[:, tt * 512:(tt + 1) * 512]
                    evo = p.v(lambda e, o=otile, a=t1, b=dx: e.tensor_add(o, a, b),
                              waits=[evd] +
                                    ([self.ot_war[tt]] if self.ot_war[tt] else []) +
                                    ([self.sq_war[tt]] if self.sq_war[tt] else []))
                    self.xt_war[g] = evo
                    self.t1_war = evo
                    dd = dap.rearrange("(kc p) t -> kc p t", p=128)[
                        fo, :, T0 + tt * 512:T0 + (tt + 1) * 512]
                    evout = p.dma_s(dd, otile, waits=[evo], sem=f"st{tt}")
                    self.ot_war[tt] = evout
                    if not fin:
                        self.sq_war[tt] = evout
            self.phase_end = [e for e in self.ot_war if e]
            self.row_war = self.row_war + [evl]
        return self.phase_end

    def emit_ffn1(self, si):
        p = self.p
        self.nq = 1024
        xap, cast = self.xsrc(si)
        dap, fin = self.xdst(si)
        p.fence(list(self.phase_end))
        evs = p.dma_s(self.scal[:, 16:32], self.io["l1_fscale"], sem="const")
        for hf in range(2):
            T0 = hf * 1024
            evy = None
            for tt in range(2):
                evy = self.emit_ln(xap, cast, T0 + tt * 512, tt * 512)
            evo = None
            for fo in range(16):
                par = fo % 2
                wev = self.load_wchunk(self.io["l1_fw"], 2048, fo, par,
                                       waits=[self.ws_war[par]])
                for tt in range(2):
                    bk = (fo * 2 + tt) % 2
                    pbx = self.pbank[bk]
                    mmev = None
                    for kc in range(16):
                        mmev = self.mm(pbx, self.ws_tile(par, kc),
                                       self.yb(kc, tt * 512, (tt + 1) * 512),
                                       [bk] if kc == 0 else [],
                                       waits=[wev, evy, evs] if kc == 0 else [],
                                       start=(kc == 0), stop=(kc == 15))
                    dst = self.qb(fo, tt * 512, (tt + 1) * 512)
                    evo = p.s(lambda e, o=dst, i=pbx, sc=self.scal[:, 16 + fo:17 + fo]:
                              e.activation(o, i, AF.Copy, scale=sc), waits=[mmev])
                    self.read_bank(bk, evo)
                    self.ws_war[par] = mmev
            evout = None
            for tt in range(2):
                PR = self.pbank[3][0:1, 0:512]
                mmev = None
                for fo in range(16):
                    sqs = self.sq[:, (fo % 2) * 512:(fo % 2 + 1) * 512]
                    evsq = p.s(lambda e, o=sqs,
                               i=self.qb(fo, tt * 512, (tt + 1) * 512):
                               e.activation(o, i, AF.Square),
                               waits=[evo, self.sq_war[fo % 2]])
                    mmev = self.mm(PR, self.cp["c_ones"], sqs,
                                   [3] if fo == 0 else [], waits=[evsq],
                                   start=(fo == 0), stop=(fo == 15))
                    self.sq_war[fo % 2] = mmev
                r1 = self.srow[0:1, 0:512]
                r2 = self.srow[0:1, 512:1024]
                r3 = self.srow[0:1, 1024:1536]
                r4 = self.srow[0:1, 1536:2048]
                e1 = p.s(lambda e, o=r1, i=PR: e.activation(o, i, AF.Ln,
                                                            bias=1e-38),
                         waits=[mmev] + self.row_war)
                self.read_bank(3, e1)
                e2 = p.s(lambda e, o=r1, i=r1: e.activation(o, i, AF.Exp,
                                                            scale=0.5), waits=[])
                e3 = p.v(lambda e, o=r2, i=r1: e.tensor_scalar_max(o, i, 1e-7),
                         waits=[e2])
                e4 = p.s(lambda e, o=r3, i=r2: e.activation(o, i, AF.Exp,
                                                            scale=-2.0),
                         waits=[e3])
                e5 = p.v(lambda e, o=r4, i=r3:
                         e.tensor_scalar(o, i, -1.0, 1.0, ALU.mult, ALU.add),
                         waits=[e4])
                e6 = p.v(lambda e, o=r3, i=r3: e.tensor_scalar_add(o, i, 1.0),
                         waits=[e4])
                e7 = p.v(lambda e, o=r3, i=r3: e.reciprocal(o, i), waits=[])
                e8 = p.v(lambda e, o=r4, a=r4, b=r3: e.tensor_mul(o, a, b),
                         waits=[])
                e9 = p.v(lambda e, o=r3, i=r2: e.tensor_scalar_add(o, i, 1e-7),
                         waits=[])
                e10 = p.v(lambda e, o=r3, i=r3: e.reciprocal(o, i), waits=[])
                e11 = p.v(lambda e, o=r4, a=r4, b=r3: e.tensor_mul(o, a, b),
                          waits=[])
                e12 = p.v(lambda e, o=r1, a=r1, b=r4: e.tensor_mul(o, a, b),
                          waits=[])
                e13 = p.v(lambda e, o=r1, i=r1: e.tensor_scalar_max(o, i, 1e-7),
                          waits=[])
                e14 = p.v(lambda e, o=r1, i=r1: e.reciprocal(o, i), waits=[])
                e15 = p.v(lambda e, o=r1, i=r1:
                          e.tensor_scalar(o, i, 0.99, 1.0, ALU.mult, ALU.min),
                          waits=[])
                e16 = p.v(lambda e, o=r4, a=r4, b=r1: e.tensor_mul(o, a, b),
                          waits=[])
                self.row_war = [e16]
                bc = self.mm(self.pbank[6], self.cp["c_onesrow"], r4, [6],
                             waits=[e16])
                RB = self.rb[:, tt * 512:(tt + 1) * 512]
                evb = p.s(lambda e, o=RB, i=self.pbank[6]: e.copy(o, i),
                          waits=[bc, self.rb_war[tt]])
                self.read_bank(6, evb)
                self.row_war = [bc]
                for fo in range(16):
                    t1 = self.rtmp[:, 0:512]
                    ev1 = p.v(lambda e, o=t1,
                              a=self.qb(fo, tt * 512, (tt + 1) * 512), b=RB:
                              e.tensor_mul(o, a, b),
                              waits=[evb] + ([self.t1_war] if self.t1_war else []))
                    g = tt % 2
                    sx = xap.rearrange("(kc p) t -> kc p t", p=128)[
                        fo, :, T0 + tt * 512:T0 + (tt + 1) * 512]
                    dx = self.xt[:, g * 1024:g * 1024 + 512]
                    evd = (p.dma_g if cast else p.dma_s)(dx, sx,
                                                         waits=[self.xt_war[g]],
                                                         sem=f"xt{g}")
                    if fin:
                        otile = self.of[:, (fo % 2) * 512:(fo % 2 + 1) * 512]
                    else:
                        otile = self.sq[:, (fo % 2) * 512:(fo % 2 + 1) * 512]
                    evo2 = p.v(lambda e, o=otile, a=t1, b=dx:
                               e.tensor_add(o, a, b),
                               waits=[evd] +
                                     ([self.ot_war[fo % 2]] if self.ot_war[fo % 2] else []) +
                                     ([self.sq_war[fo % 2]] if self.sq_war[fo % 2] else []))
                    self.xt_war[g] = evo2
                    self.t1_war = evo2
                    dd = dap.rearrange("(kc p) t -> kc p t", p=128)[
                        fo, :, T0 + tt * 512:T0 + (tt + 1) * 512]
                    evout = p.dma_s(dd, otile, waits=[evo2], sem=f"st{fo % 2}")
                    self.ot_war[fo % 2] = evout
                    if not fin:
                        self.sq_war[fo % 2] = evout
                self.rb_war[tt] = ev1
            self.phase_end = [e for e in self.ot_war if e]
        return self.phase_end



def build_kb(sublayers=(0, 1, 2, 3)):
    nc = bass.Bass("TRN2", target_bir_lowering=False)
    kb = KB(nc, sublayers)
    for si in sorted(sublayers):
        if si == 0:
            kb.emit_attn(0, 0)
        elif si == 1:
            kb.emit_ffn0(1)
        elif si == 2:
            kb.emit_attn(2, 1)
        elif si == 3:
            kb.emit_ffn1(3)
    return nc, kb


def finalize(nc, kb, extra_evs=()):
    kb.p.fence(list(kb.phase_end) + list(extra_evs))
    with nc.Block() as block:
        kb.p.replay(block)
    return nc


def build_nc(sublayers=(0, 1, 2, 3)):
    nc, kb = build_kb(sublayers)
    finalize(nc, kb)
    return nc, kb


def _prep_inputs(inputs):
    f = np.float32
    def cT(a):
        return np.ascontiguousarray(np.asarray(a, dtype=f).T)
    def c(a):
        return np.ascontiguousarray(np.asarray(a, dtype=f))
    com = dict(
        l0_wq=c(inputs["L0_attn_wq"]), l0_wk=c(inputs["L0_attn_wk"]),
        l0_wv=c(inputs["L0_attn_wv"]), l0_wo=c(inputs["L0_attn_wo"]),
        l0_wupT=cT(inputs["L0_ffn_w_up"]), l0_wdnT=cT(inputs["L0_ffn_w_down"]),
        l1_wqT=cT(inputs["L1_attn_Wq"]), l1_wkT=cT(inputs["L1_attn_Wk"]),
        l1_wvT=cT(inputs["L1_attn_Wv"]), l1_woT=cT(inputs["L1_attn_Wo"]),
        l1_fw=c(inputs["L1_ffn_weight"]),
        l1_fscale=np.ascontiguousarray(
            np.asarray(inputs["L1_ffn_scale"], dtype=f).reshape(16, 128).T),
    )
    com.update(host_constants())
    x = np.asarray(inputs["L0_attn_x"], dtype=f)
    maps = []
    for b in range(8):
        m = dict(com)
        m["xT"] = np.ascontiguousarray(x[b].T)
        maps.append(m)
    return maps


def _run(inputs, sublayers=(0, 1, 2, 3)):
    maps = _prep_inputs(inputs)
    nc, kb = build_nc(sublayers)
    res = run_bass_kernel_spmd(nc, maps, core_ids=list(range(8)))
    out = np.stack([np.asarray(r["outT"]).T for r in res.results])
    return np.ascontiguousarray(out)


def kernel(**inputs):
    return _run(inputs, (0, 1, 2, 3))


# version 05
# speedup vs baseline: 1.1531x; 1.1531x over previous
import math
import numpy as np
import ml_dtypes
import concourse.bass as bass
import concourse.mybir as mybir
from concourse.bass_utils import run_bass_kernel_spmd
from contextlib import ExitStack

F32 = mybir.dt.float32
BF16 = mybir.dt.bfloat16
AF = mybir.ActivationFunctionType
ALU = mybir.AluOpType
AX = mybir.AxisListType

D = 2048
T = 2048
NCH = 16
WIN = 512
DH = 128
NH = 16
MAGIC = 1.5 * 2.0 ** 23
SCALE = DH ** -0.5
ACT_CONSTS = [1e-5, 1e-38, -0.5, 0.5, -2.0, SCALE, MAGIC, math.log(SCALE)]
C2 = 1.0 / math.sqrt(2048.0)
C4 = 1.0 / math.sqrt(4096.0)
EPSQ = 1e-5
NEG = -1.0e30

ENGINES = ("tensor", "vector", "scalar", "gpsimd", "sync")



def _rope_tables():
    j = np.arange(64)
    invf = 10000.0 ** (-2.0 * j / 128.0)
    t = np.arange(T)
    fr = t[None, :] * invf[:, None]
    cos = np.cos(fr).astype(np.float32)
    sin = np.sin(fr).astype(np.float32)
    ct0 = np.concatenate([cos, cos], axis=0)
    st0 = np.concatenate([-sin, sin], axis=0)
    ct1 = np.zeros((128, T), np.float32)
    st1 = np.zeros((128, T), np.float32)
    ct1[0::2] = cos
    ct1[1::2] = cos
    st1[0::2] = -sin
    st1[1::2] = sin
    bf = ml_dtypes.bfloat16
    return (ct0.astype(bf), st0.astype(bf), ct1.astype(bf), st1.astype(bf))


def _hadamard(n):
    h = np.array([[1.0]])
    while h.shape[0] < n:
        h = np.block([[h, h], [h, -h]])
    return h


def host_constants():
    bf = ml_dtypes.bfloat16
    ct0, st0, ct1, st1 = _rope_tables()
    h128 = _hadamard(128).astype(bf)
    psw = np.zeros((128, 128), np.float32)
    for i in range(64):
        psw[2 * i + 1, 2 * i] = 1.0
        psw[2 * i, 2 * i + 1] = 1.0
    psw = psw.astype(bf)
    ph0 = np.zeros((128, 128), np.float32)
    for i in range(64):
        ph0[i + 64, i] = 1.0
        ph0[i, i + 64] = 1.0
    ph0 = ph0.astype(bf)
    r = np.arange(128)[:, None]
    c = np.arange(640)[None, :]
    mband = np.where((c >= r) & (c < r + WIN), 0.0, NEG).astype(np.float32)
    ohc = np.zeros((128, 16 * 16), np.float32)
    for h in range(16):
        ohc[:, 16 * h + h] = 1.0
    ohc = ohc.astype(bf)
    ohr = np.zeros((16, 16 * 128), np.float32)
    for h in range(16):
        ohr[h, 128 * h:128 * (h + 1)] = 1.0
    ones_col = np.ones((128, 1), np.float32).astype(bf)
    cmean_col = np.full((128, 1), 1.0 / 2048.0, np.float32).astype(bf)
    ones_row = np.ones((1, 128), np.float32)
    acv = np.array(ACT_CONSTS, np.float32)
    actc = np.broadcast_to(acv[None, :], (128, len(ACT_CONSTS))).copy()
    return dict(
        c_actc=actc,
        c_ct0=ct0, c_st0=st0, c_ct1=ct1, c_st1=st1,
        c_h128=h128, c_psw=psw, c_psw0=ph0, c_mband=mband,
        c_ohc=ohc, c_ohr=ohr.astype(np.float32),
        c_ones=ones_col, c_cmean=cmean_col, c_onesrow=ones_row,
        c_id=np.eye(128, dtype=np.float32),
    )



class Sem:
    __slots__ = ("name", "h", "n")

    def __init__(self, name, handle):
        self.name = name
        self.h = handle
        self.n = 0


class Ev:
    __slots__ = ("sem", "n", "eng")

    def __init__(self, sem, n, eng):
        self.sem = sem
        self.n = n
        self.eng = eng


def hard(ev):
    return Ev(ev.sem, ev.n, "hard") if ev is not None else None


class Prog:

    def __init__(self, nc):
        self.nc = nc
        self.streams = {e: [] for e in ENGINES}
        self.sems = {}
        for name in ("pe", "dv", "ac", "pl"):
            self.sems[name] = Sem(name, nc.alloc_semaphore("s_" + name))
        self.dsems = {}
        self.bank_war = {}
        self.nsteps = 0

    def go(self, eng, fn, waits=(), inc=None, inck=1):
        waits = tuple(w for w in waits if w is not None)
        for w in waits:
            assert w.n <= w.sem.n, f"wait on future event {w.sem.name}:{w.n}>{w.sem.n}"
        self.streams[eng].append((waits, fn, inc, inck))
        self.nsteps += 1
        if inc is not None:
            inc.n += inck
            return Ev(inc, inc.n, "dma" if eng in ("sync",) and inck >= 16 else eng)
        return None

    def t(self, fn, waits=(), inc=True):
        return self.go("tensor", fn, waits, self.sems["pe"] if inc else None)

    def v(self, fn, waits=(), inc=True):
        return self.go("vector", fn, waits, self.sems["dv"] if inc else None)

    def s(self, fn, waits=(), inc=True):
        return self.go("scalar", fn, waits, self.sems["ac"] if inc else None)

    def g(self, fn, waits=(), inc=True):
        return self.go("gpsimd", fn, waits, self.sems["pl"] if inc else None)

    def _dsem(self, name):
        if name not in self.dsems:
            self.dsems[name] = Sem(name, self.nc.alloc_semaphore("d_" + name))
        return self.dsems[name]

    def dma_s(self, out, in_, waits=(), sem="misc_s"):
        ev = self.go("sync", lambda e, o=out, i=in_: e.dma_start(out=o, in_=i),
                     waits, self._dsem(sem), 16)
        return Ev(ev.sem, ev.n, "dma")

    def dma_g(self, out, in_, waits=(), sem="misc_g"):
        ev = self.go("gpsimd", lambda e, o=out, i=in_: e.dma_start(out=o, in_=i),
                     waits, self._dsem(sem), 16)
        return Ev(ev.sem, ev.n, "dma")

    def fence(self, evs, engines=ENGINES):
        evs = [e for e in evs if e is not None]
        for eng in engines:
            self.go(eng, None, tuple(Ev(e.sem, e.n, "dma") for e in evs))

    def replay(self, block):
        for eng in ENGINES:
            def body(e, eng=eng):
                seen = {}
                for (waits, fn, inc, inck) in self.streams[eng]:
                    for w in waits:
                        if w.eng == eng:
                            continue
                        if seen.get(w.sem.name, -1) >= w.n:
                            continue
                        e.wait_ge(w.sem.h, w.n)
                        seen[w.sem.name] = w.n
                    if fn is None:
                        continue
                    ins = fn(e)
                    if inc is not None:
                        ins.then_inc(inc.h, inck)
            getattr(block, eng)(body)



class KB:
    def __init__(self, nc, sublayers):
        self.nc = nc
        self.sub = tuple(sublayers)
        self.p = Prog(nc)

        dt = nc.dram_tensor
        self.xT = dt("xT", [D, T], F32, kind="ExternalInput").ap()
        self.io = {}
        for nm, sh, ty in [
            ("l0_wq", [D, 2048], F32), ("l0_wk", [D, 512], F32),
            ("l0_wv", [D, 512], F32), ("l0_wo", [D, 2048], F32),
            ("l0_wupT", [D, 4096], F32), ("l0_wdnT", [4096, D], F32),
            ("l1_wqT", [D, 2048], F32), ("l1_wkT", [D, 2048], F32),
            ("l1_wvT", [D, 2048], F32), ("l1_woT", [D, 2048], F32),
            ("l1_fw", [D, 2048], F32), ("l1_fscale", [128, 16], F32),
            ("c_ct0", [128, T], BF16), ("c_st0", [128, T], BF16),
            ("c_ct1", [128, T], BF16), ("c_st1", [128, T], BF16),
            ("c_h128", [128, 128], BF16), ("c_psw", [128, 128], BF16),
            ("c_psw0", [128, 128], BF16),
            ("c_mband", [128, 640], F32), ("c_ohc", [128, 256], BF16),
            ("c_ohr", [16, 2048], F32), ("c_ones", [128, 1], BF16),
            ("c_cmean", [128, 1], BF16), ("c_onesrow", [1, 128], F32),
            ("c_id", [128, 128], F32),
            ("c_actc", [128, len(ACT_CONSTS)], F32),
        ]:
            self.io[nm] = dt(nm, sh, ty, kind="ExternalInput").ap()
        self.xres = dt("xres", [D, T], BF16, kind="Internal").ap()
        self.ternup = dt("ternup", [D, 4096], BF16, kind="Internal").ap()
        self.terndn = dt("terndn", [4096, D], BF16, kind="Internal").ap()
        self.outT = dt("outT", [D, T], F32, kind="ExternalOutput").ap()

        A = nc.alloc_sbuf_tensor
        self.arA = A("arA", [128, 16384], BF16).ap()
        self.arB = A("arB", [128, 16384], BF16).ap()
        self.arC = A("arC", [128, 32768], BF16).ap()
        ovl_t = A("ovl", [128, 20480], BF16)
        ob = ovl_t.ap()
        obf = ovl_t.bitcast(F32).ap()
        self.ring = ob[:, 0:3840]
        self.ct = ob[:, 3840:5888]
        self.st = ob[:, 5888:7936]
        self.sq = ob[:, 7936:8960]
        self.rtmp = ob[:, 8960:9984]
        self.mband = obf[:, 4992:5632]
        self.ohc = ob[:, 11264:11520]
        self.ohr = obf[0:16, 5760:7808]
        self.fscr = obf[:, 0:8192]
        self.wf = obf[:, 8192:9216]
        self.of = obf[:, 9216:10240]
        self.ws = A("ws", [128, 8192], BF16).ap()
        self.srow = A("srow", [128, 2048], F32).ap()
        self.rb = A("rb", [128, 1024], F32).ap()
        self.xt = A("xt", [128, 4096], BF16).ap()
        self.cp = {}
        for nm, sh, ty in [("c_h128", [128, 128], BF16), ("c_psw", [128, 128], BF16),
                           ("c_psw0", [128, 128], BF16),
                           ("c_ones", [128, 1], BF16), ("c_cmean", [128, 1], BF16),
                           ("c_onesrow", [1, 128], F32), ("c_id", [128, 128], F32)]:
            self.cp[nm] = A(nm + "_s", sh, ty).ap()
        self.scal = A("scal", [128, 32], F32).ap()
        self.actc = A("actc", [128, len(ACT_CONSTS)], F32).ap()
        self.tchs = A("tchs", [128, 2], BF16).ap()[:, 0:1]
        for j, v in enumerate(ACT_CONSTS):
            nc.const_aps.register(F32, float(v), self.actc[:, j:j + 1])
        rem = nc.sbuf_bytes_remaining
        assert rem >= 0, f"sbuf over budget by {-rem}"
        self.sbuf_rem = rem
        self.first = min(self.sub)
        self.xt_war = [None, None]
        self.sq_war = [None, None]
        self.rb_war = [None, None]
        self.row_war = []
        self.ws_war = [None, None]
        self.kv_war = None
        self.phase_end = []
        self.ring_war = [None] * 6
        self.ru_war = []
        self.rn_war = []
        self.wf_war = None
        self.wf_war2 = None
        self.am_war = None
        self.lam_war = None
        self.mu_war = None
        self.t1_war = None
        self.ot_war = [None, None]
        self.limit_passes = None
        self.xgev = {0: None, 1: None}

        self.pbank = [nc.alloc_psum_tensor(f"pbk{i}", [128, 512], F32).ap()
                      for i in range(8)]
        self.bwar = [None] * 8

        evs = [self.p.dma_s(self.cp[nm], self.io[nm], sem="const")
               for nm in self.cp]
        evs.append(self.p.dma_s(self.actc, self.io["c_actc"], sem="const"))
        self.p.fence(evs)
        self._preload_act()

    def _preload_act(self):
        idx = None
        try:
            from concourse.hw_specs import get_activation_tables
            tabs = get_activation_tables(self.nc.m.arch)
            for i, fns in enumerate(tabs.values()):
                if AF.Ln in fns and AF.Exp in fns:
                    idx = i
                    break
        except Exception:
            idx = 6
        if idx is None:
            return
        def emit(e, idx=idx):
            return e.add_instruction(mybir.InstLoadActFuncSet(
                name=self.nc.get_next_instruction_name(),
                act_func_set_id=idx, ins=[], outs=[]))
        self.p.s(emit, inc=False)

    def mm(self, out_ap, lhsT, rhs, banks, waits=(), start=True, stop=True):
        w = list(waits)
        if start:
            for b in banks:
                if self.bwar[b] is not None:
                    w.append(self.bwar[b])
                    self.bwar[b] = None
        return self.p.t(lambda e, o=out_ap, l=lhsT, r=rhs, st=start, sp=stop:
                        e.matmul(o, l, r, start=st, stop=sp), w)

    def read_bank(self, b, ev):
        self.bwar[b] = ev

    def xsrc(self, i):
        return (self.xT, True) if i == self.first else (self.xres, False)

    def xdst(self, i):
        return (self.outT, True) if i == max(self.sub) else (self.xres, False)

    def load_x4(self, src, cast, c0, t0, g, waits=()):
        sap = src.rearrange("(kc p) t -> p kc t", p=128)[:, c0:c0 + 4, t0:t0 + 512]
        dstv = self.xt[:, g * 2048:(g + 1) * 2048].rearrange(
            "p (kc t) -> p kc t", t=512)
        return (self.p.dma_g if cast else self.p.dma_s)(dstv, sap, waits,
                                                        sem=f"xt{g}")

    def xsl(self, g, j):
        return self.xt[:, g * 2048 + j * 512: g * 2048 + (j + 1) * 512]

    def load_wchunk(self, w_ap, din, fo, par, waits=(), cast=True, touch=True):
        kch = din // 128
        src = w_ap.rearrange("(kc p) f -> p kc f", p=128)[
            :, :, fo * 128:(fo + 1) * 128]
        dst = self.ws[:, par * 4096:par * 4096 + kch * 128].rearrange(
            "p (kc f) -> p kc f", f=128)
        return (self.p.dma_g if cast else self.p.dma_s)(dst, src, waits,
                                                        sem=f"ws{par}")

    def ws_tile(self, par, kc):
        return self.ws[:, par * 4096 + kc * 128: par * 4096 + (kc + 1) * 128]

    def yb(self, c, a, b):
        return self.arA[:, c * self.nq + a: c * self.nq + b]

    def qb(self, c, a, b):
        return self.arB[:, c * self.nq + a: c * self.nq + b]

    def obuf(self, c, a, b):
        return self.arA[:, c * self.nq + a: c * self.nq + b]

    def kb(self, c, a, b):
        return self.arC[:, c * self.nkv + a: c * self.nkv + b]

    def vb(self, tile, a, b):
        off = self.kvch * self.nkv
        return self.arC[:, off + tile * self.fv + a: off + tile * self.fv + b]

    def ringst(self, kt, a, b):
        s = (kt % 6) * 640
        return self.ring[:, s + a: s + b]

    def emit_ln(self, xap, cast, t0, tl):
        p = self.p
        PR1 = self.pbank[2][0:1, 0:512]
        PR2 = self.pbank[3][0:1, 0:512]
        cme = self.cp["c_cmean"]
        mm1 = mm2 = None
        sq0 = self.sq[:, 0:512]
        sq1 = self.sq[:, 512:1024]
        for g in range(4):
            c0 = 4 * g
            evd = self.load_x4(xap, cast, c0, t0, g % 2,
                               waits=[self.xt_war[g % 2]])
            for j in range(4):
                xs = self.xsl(g % 2, j)
                sqs = sq0 if j % 2 == 0 else sq1
                evq = p.s(lambda e, o=sqs, i=xs: e.activation(o, i, AF.Square),
                          waits=[evd, self.sq_war[j % 2]])
                first = (g == 0 and j == 0)
                last = (g == 3 and j == 3)
                mm1 = self.mm(PR1, cme, xs, [2] if first else [],
                              waits=[evd], start=first, stop=last)
                mm2 = self.mm(PR2, cme, sqs, [3] if first else [],
                              waits=[evq], start=first, stop=last)
                self.sq_war[j % 2] = mm2
            self.xt_war[g % 2] = mm1
        mrow = self.srow[0:1, 1024:1536]
        vrow = self.srow[0:1, 1536:2048]
        evm = p.s(lambda e, o=mrow, i=PR1: e.copy(o, i),
                  waits=[mm1] + self.row_war)
        ev2 = p.v(lambda e, o=vrow, a=mrow: e.tensor_mul(o, a, a),
                  waits=[evm] + self.row_war)
        ev3 = p.v(lambda e, o=vrow, a=PR2, b=vrow: e.tensor_sub(o, a, b),
                  waits=[mm2])
        self.read_bank(2, evm)
        self.read_bank(3, ev3)
        ev4 = p.s(lambda e, o=vrow, i=vrow: e.activation(o, i, AF.Ln, bias=1e-5),
                  waits=[ev3])
        evr = p.s(lambda e, o=vrow, i=vrow: e.activation(o, i, AF.Exp, scale=-0.5),
                  waits=[ev4])
        bm = self.mm(self.pbank[6], self.cp["c_onesrow"], mrow, [6], waits=[evm])
        br = self.mm(self.pbank[7], self.cp["c_onesrow"], vrow, [7], waits=[evr])
        RBm = self.rb[:, 0:512]
        RBr = self.rb[:, 512:1024]
        evbm = p.s(lambda e, o=RBm, i=self.pbank[6]: e.copy(o, i),
                   waits=[bm, self.rb_war[0]])
        evbr = p.s(lambda e, o=RBr, i=self.pbank[7]: e.copy(o, i),
                   waits=[br, self.rb_war[1]])
        self.read_bank(6, evbm)
        self.read_bank(7, evbr)
        self.row_war = [bm, br, ev2]
        evy = None
        for g in range(4):
            c0 = 4 * g
            evd = self.load_x4(xap, cast, c0, t0, g % 2,
                               waits=[self.xt_war[g % 2]])
            for j in range(4):
                c = c0 + j
                xs = self.xsl(g % 2, j)
                rt = self.rtmp[:, (j % 2) * 512:(j % 2 + 1) * 512]
                ev5 = p.v(lambda e, o=rt, a=xs, b=RBm: e.tensor_sub(o, a, b),
                          waits=[evd, evbm])
                ys = self.yb(c, tl, tl + 512)
                evy = p.v(lambda e, o=ys, a=rt, b=RBr: e.tensor_mul(o, a, b),
                          waits=[evbr])
                self.xt_war[g % 2] = ev5
        self.rb_war[0] = evy
        self.rb_war[1] = evy
        return evy

    def emit_proj(self, wap, nfo, rhs_fn, ntt, out_fn, waits, scale=1.0):
        p = self.p
        eve = None
        for fo in range(nfo):
            par = fo % 2
            wev = self.load_wchunk(wap, 2048, fo, par, waits=[self.ws_war[par]])
            for tt in range(ntt):
                bk = (fo * ntt + tt) % 2
                pbx = self.pbank[bk]
                mmev = None
                for kc in range(16):
                    mmev = self.mm(pbx, self.ws_tile(par, kc), rhs_fn(kc, tt),
                                   [bk] if kc == 0 else [],
                                   waits=([wev] + list(waits)) if kc == 0 else [],
                                   start=(kc == 0), stop=(kc == 15))
                eve = p.s(lambda e, o=out_fn(fo, tt), i=pbx, sc=scale:
                          e.activation(o, i, AF.Copy, scale=sc), waits=[mmev])
                self.read_bank(bk, eve)
                self.ws_war[par] = mmev
        return eve

    def emit_attn(self, si, L):
        p = self.p
        nq = 1024 if L == 0 else 512
        self.nq = nq
        self.nkv = nq + 512
        self.kvch = 4 if L == 0 else 16
        self.fv = 512 if L == 0 else 2048
        rep = 4 if L == 0 else 1
        names = (("l0_wq", "l0_wk", "l0_wv", "l0_wo", "c_ct0", "c_st0") if L == 0
                 else ("l1_wqT", "l1_wkT", "l1_wvT", "l1_woT", "c_ct1", "c_st1"))
        wq, wk, wv, wo, ctn, stn = names
        xap, cast = self.xsrc(si)
        dap, fin = self.xdst(si)
        npass = T // nq if self.limit_passes is None else self.limit_passes

        pw = list(self.phase_end)
        evs = [p.dma_s(self.ct, self.io[ctn], waits=pw, sem="const"),
               p.dma_s(self.st, self.io[stn], waits=pw, sem="const"),
               p.dma_s(self.mband, self.io["c_mband"], waits=pw, sem="const"),
               p.dma_s(self.ohc, self.io["c_ohc"], waits=pw, sem="const"),
               p.dma_s(self.ohr, self.io["c_ohr"], waits=pw, sem="const")]
        p.fence(evs + pw)

        for ps in range(npass):
            if ps > 0:
                p.fence(list(self.phase_end))
            Q0 = ps * nq
            B0 = max(0, Q0 - 512)
            halo = Q0 - B0
            nkt = (Q0 + nq - B0) // 128
            hkt = halo // 128

            cpev = []
            if ps > 0:
                B0p = max(0, Q0 - nq - 512)
                soff = Q0 - 512 - B0p
                for c in range(self.kvch):
                    src = self.kb(c, soff, soff + 512)
                    dst = self.kb(c, 0, 512)
                    cpev.append(p.g(lambda e, o=dst, i=src: e.tensor_copy(o, i),
                                    waits=[self.kv_war]))
                stile = soff // 128
                for tl in range(4):
                    src = self.vb(stile + tl, 0, self.fv)
                    dst = self.vb(tl, 0, self.fv)
                    cpev.append(p.g(lambda e, o=dst, i=src: e.tensor_copy(o, i),
                                    waits=[self.kv_war]))
            cpall = cpev[-1] if cpev else None

            evy = None
            for tt in range(nq // 512):
                evy = self.emit_ln(xap, cast, Q0 + tt * 512, tt * 512)
            ylast = evy

            def yrhs(kc, tt):
                return self.yb(kc, tt * 512, (tt + 1) * 512)
            evk = self.emit_proj(self.io[wk], self.kvch, yrhs, nq // 512,
                                 lambda fo, tt: self.kb(fo, halo + tt * 512,
                                                        halo + (tt + 1) * 512),
                                 waits=[ylast] + cpev)
            evv = None
            for fcg in range(self.fv // 256):
                par = fcg % 2
                src = self.io[wv].rearrange("(kc p) f -> p kc f", p=128)[
                    :, :, fcg * 256:(fcg + 1) * 256]
                dstw = self.ws[:, par * 4096:par * 4096 + 4096].rearrange(
                    "p (kc f) -> p kc f", f=256)
                wev = p.dma_g(dstw, src, waits=[self.ws_war[par]],
                              sem=f"ws{par}")
                for b in range(nq // 128):
                    bk = (fcg * (nq // 128) + b) % 2
                    pbx = self.pbank[bk][:, 0:256]
                    mmev = None
                    for kc in range(16):
                        lt = self.yb(kc, b * 128, (b + 1) * 128)
                        rt = self.ws[:, par * 4096 + kc * 256:
                                     par * 4096 + (kc + 1) * 256]
                        mmev = self.mm(pbx, lt, rt, [bk] if kc == 0 else [],
                                       waits=([wev, ylast] + cpev) if kc == 0 else [],
                                       start=(kc == 0), stop=(kc == 15))
                    dst = self.vb(hkt + b, fcg * 256, (fcg + 1) * 256)
                    evv = p.s(lambda e, o=dst, i=pbx: e.copy(o, i), waits=[mmev])
                    self.read_bank(bk, evv)
                    self.ws_war[par] = mmev
            evq = self.emit_proj(self.io[wq], 16, yrhs, nq // 512,
                                 lambda fo, tt: self.qb(fo, tt * 512, (tt + 1) * 512),
                                 waits=[ylast], scale=(1.0 if L == 0 else SCALE))

            evrope_q = evq
            evrope_k = evk
            if L == 0:
                evrope_q = self.emit_l2norm(True, nq, Q0, evq)
                evrope_k = self.emit_l2norm(False, nq, Q0, evk, halo=halo)
            evrope_q = self.emit_rope(L, True, nq, Q0, evrope_q, halo=0)
            evrope_k = self.emit_rope(L, False, nq, Q0, evrope_k, halo=halo)

            evob = self.emit_attn_core(L, nq, Q0, B0, halo, nkt, rep,
                                       [evrope_q, evrope_k, evv] + cpev,
                                       proj_done=[evq, evk, evv])
            self.kv_war = evob

            self.emit_out_proj(self.io[wo], nq, Q0, xap, cast, dap, fin,
                               ob_ready=evob)
            self.phase_end = [e for e in self.ot_war if e]
        return self.phase_end

    def emit_l2norm(self, qside, nq, Q0, ready, halo=0):
        p = self.p
        nch = 16 if qside else self.kvch
        STAT = self.pbank[2][0:16, 0:512]
        rnoff = 0 if qside else 512
        RN = self.srow[0:16, rnoff:rnoff + 512]
        evn = ready
        for tt in range(nq // 512):
            mmev = None
            for c in range(nch):
                sl = (self.qb(c, tt * 512, (tt + 1) * 512) if qside else
                      self.kb(c, halo + tt * 512, halo + (tt + 1) * 512))
                sqs = self.sq[:, (c % 2) * 512:(c % 2 + 1) * 512]
                evq = p.s(lambda e, o=sqs, i=sl: e.activation(o, i, AF.Square),
                          waits=[ready, self.sq_war[c % 2]])
                mmev = self.mm(STAT, self.ohc[:, c * 16:(c + 1) * 16], sqs,
                               [2] if c == 0 else [], waits=[evq],
                               start=(c == 0), stop=(c == nch - 1))
                self.sq_war[c % 2] = mmev
            e1 = p.s(lambda e, o=RN, i=STAT: e.activation(o, i, AF.Ln, bias=1e-38),
                     waits=[mmev] + list(self.rn_war))
            self.read_bank(2, e1)
            bias = math.log(SCALE) if qside else 0.0
            mv = 1e12 * SCALE if qside else 1e12
            e2 = p.s(lambda e, o=RN, i=RN, b=bias:
                     e.activation(o, i, AF.Exp, scale=-0.5, bias=b), waits=[])
            e3 = p.v(lambda e, o=RN, i=RN, m=mv: e.tensor_scalar_min(o, i, m),
                     waits=[e2])
            evn = None
            bc = None
            for c in range(nch):
                bk = 6 + (c % 2)
                bc = self.mm(self.pbank[bk], self.ohr[0:16, c * 128:(c + 1) * 128],
                             RN, [bk], waits=[e3])
                sl = (self.qb(c, tt * 512, (tt + 1) * 512) if qside else
                      self.kb(c, halo + tt * 512, halo + (tt + 1) * 512))
                evn = p.v(lambda e, o=sl, a=self.pbank[bk], b=sl:
                          e.tensor_mul(o, a, b), waits=[bc])
                self.read_bank(bk, evn)
            self.rn_war = [bc]
        return evn

    def emit_rope(self, L, qside, nq, Q0, ready, halo=0):
        p = self.p
        nch = 16 if qside else self.kvch
        evn = ready
        for tt in range(nq // 512):
            ta = Q0 + tt * 512
            cts = self.ct[:, ta:ta + 512]
            sts = self.st[:, ta:ta + 512]
            for c in range(nch):
                sl = (self.qb(c, tt * 512, (tt + 1) * 512) if qside else
                      self.kb(c, halo + tt * 512, halo + (tt + 1) * 512))
                rt = self.rtmp[:, (c % 2) * 512:(c % 2 + 1) * 512]
                perm = self.cp["c_psw0" if L == 0 else "c_psw"]
                bk = 4
                sw = self.mm(self.pbank[bk], perm, sl, [bk], waits=[ready])
                ev = p.v(lambda e, o=rt, a=self.pbank[bk], b=sts:
                         e.tensor_mul(o, a, b), waits=[sw])
                self.read_bank(bk, ev)
                p.v(lambda e, o=sl, a=sl, b=cts: e.tensor_mul(o, a, b), waits=[])
                evn = p.v(lambda e, o=sl, a=sl, b=rt: e.tensor_add(o, a, b),
                          waits=[])
        return evn

    def emit_attn_core(self, L, nq, Q0, B0, halo, nkt, rep, ready, proj_done):
        p = self.p
        hkt = halo // 128
        nqt = nq // 128
        evob = None
        first_ob = list(proj_done)
        for h in range(NH):
            kc_h = h // rep if L == 0 else h
            strip_ev = {}
            po_mm = None
            su_mm = None
            av_first = True
            for kt in range(nkt):
                K0 = B0 + kt * 128
                c_lo = max(K0, Q0) - K0
                c_hi = min(K0 + 640, Q0 + nq) - K0
                par = kt % 2
                SPA = self.pbank[par]
                SPB = self.pbank[2 + par][:, 0:128]
                ktile = self.kb(kc_h, kt * 128, (kt + 1) * 128)
                a0, a1 = c_lo, min(c_hi, 512)
                evm = []
                if a1 > a0:
                    m1 = self.mm(SPA[:, a0:a1], ktile,
                                 self.qb(h, K0 + a0 - Q0, K0 + a1 - Q0),
                                 [par], waits=ready)
                    e1 = p.v(lambda e, o=SPA[:, a0:a1], i=self.mband[:, a0:a1]:
                             e.tensor_add(o, o, i), waits=[m1])
                    evm.append((SPA, a0, a1, e1))
                if c_hi > 512:
                    b0, b1 = 512, c_hi
                    m2 = self.mm(SPB[:, 0:b1 - b0], ktile,
                                 self.qb(h, K0 + b0 - Q0, K0 + b1 - Q0),
                                 [2 + par], waits=ready)
                    e2 = p.v(lambda e, o=SPB[:, 0:b1 - b0],
                             i=self.mband[:, b0:b1]:
                             e.tensor_add(o, o, i), waits=[m2])
                    evm.append((SPB, b0, b1, e2))
                rw = self.ring_war[kt % 6]
                eexp = None
                for (src, x0, x1, ev) in evm:
                    off = 0 if src is SPA else 512
                    eexp = p.s(lambda e, o=self.ringst(kt, x0, x1),
                               i=src[:, x0 - off:x1 - off]:
                               e.activation(o, i, AF.Exp),
                               waits=[ev] + ([rw] if rw else []))
                    bkid = par if src is SPA else 2 + par
                    self.read_bank(bkid, eexp)
                strip_ev[kt] = eexp
                if K0 >= Q0:
                    qt = (K0 - Q0) // 128
                    q4 = qt % 4
                    kts = list(range(max(0, kt - 4), kt + 1))
                    for i2, kt2 in enumerate(kts):
                        K02 = B0 + kt2 * 128
                        c0 = K0 - K02
                        rsl = self.ringst(kt2, c0, c0 + 128)
                        vt = self.vb(kt2, kc_h * 128 if L == 0 else h * 128,
                                     (kc_h * 128 if L == 0 else h * 128) + 128)
                        st_first = av_first and i2 == 0
                        po_mm = self.mm(self.pbank[4][:, q4 * 128:(q4 + 1) * 128],
                                        vt, rsl, [4] if st_first else [],
                                        waits=[strip_ev[kt2]],
                                        start=(i2 == 0), stop=(i2 == len(kts) - 1))
                        su_mm = self.mm(self.pbank[5][0:1, q4 * 128:(q4 + 1) * 128],
                                        self.cp["c_ones"], rsl,
                                        [5] if st_first else [],
                                        waits=[], start=(i2 == 0),
                                        stop=(i2 == len(kts) - 1))
                        self.ring_war[kt2 % 6] = su_mm
                    av_first = False
                    if q4 == 3 or qt == nqt - 1:
                        ng = (q4 + 1) * 128
                        qg0 = (qt - q4) * 128
                        ru = self.srow[0:1, 0:512]
                        evru = p.v(lambda e, o=ru[:, 0:ng],
                                   i=self.pbank[5][0:1, 0:ng]:
                                   e.reciprocal(o, i),
                                   waits=[su_mm] + self.ru_war)
                        self.read_bank(5, evru)
                        bc = self.mm(self.pbank[6][:, 0:ng], self.cp["c_onesrow"],
                                     ru[:, 0:ng], [6], waits=[evru])
                        self.ru_war = [bc]
                        dst = self.obuf(h, qg0, qg0 + ng)
                        epo = p.s(lambda e, o=dst, i=self.pbank[4][:, 0:ng]:
                                  e.copy(o, i), waits=[po_mm] + first_ob)
                        first_ob = []
                        self.read_bank(4, epo)
                        evob = p.v(lambda e, o=dst, a=self.pbank[6][:, 0:ng], b=dst:
                                   e.tensor_mul(o, a, b), waits=[bc, epo])
                        self.read_bank(6, evob)
                        av_first = True
        return evob

    def emit_out_proj(self, wap, nq, Q0, xap, cast, dap, fin, ob_ready):
        p = self.p
        evout = None
        for fo in range(16):
            par = fo % 2
            wev = self.load_wchunk(wap, 2048, fo, par, waits=[self.ws_war[par]])
            for tt in range(nq // 512):
                bk = (fo * (nq // 512) + tt) % 2
                pbx = self.pbank[bk]
                mmev = None
                for kc in range(16):
                    mmev = self.mm(pbx, self.ws_tile(par, kc),
                                   self.obuf(kc, tt * 512, (tt + 1) * 512),
                                   [bk] if kc == 0 else [],
                                   waits=[wev, ob_ready] if kc == 0 else [],
                                   start=(kc == 0), stop=(kc == 15))
                self.ws_war[par] = mmev
                g = tt % 2
                if fo % 4 == 0:
                    self.xgev[g] = self.load_x4(xap, cast, fo, Q0 + tt * 512, g,
                                                waits=[self.xt_war[g]])
                dx = self.xsl(g, fo % 4)
                evd = self.xgev[g]
                if fin:
                    otile = self.of[:, (tt % 2) * 512:(tt % 2) * 512 + 512]
                else:
                    otile = self.rtmp[:, (tt % 2) * 512:(tt % 2) * 512 + 512]
                evo = p.v(lambda e, o=otile, a=pbx, b=dx: e.tensor_add(o, a, b),
                          waits=[mmev, evd, self.ot_war[tt % 2]])
                self.read_bank(bk, evo)
                self.xt_war[g] = evo
                dd = dap.rearrange("(kc p) t -> kc p t", p=128)[
                    fo, :, Q0 + tt * 512:Q0 + (tt + 1) * 512]
                evout = p.dma_s(dd, otile, waits=[evo], sem=f"st{tt % 2}")
                self.ot_war[tt % 2] = evout
        return evout

    def emit_wprep(self, wap, tern, rows, cols, sidx):
        p = self.p
        PR = self.pbank[2][0:1, 0:512]
        wv = wap.rearrange("(r p) (c f) -> p r c f", p=128, f=1024)
        tv = tern.rearrange("(r p) (c f) -> p r c f", p=128, f=1024)
        nr, ncc = rows // 128, cols // 1024
        ntile = nr * ncc
        wfs = [self.fscr[:, 0:1024], self.fscr[:, 1024:2048]]
        t1s = [self.fscr[:, 2048:3072], self.fscr[:, 3072:4096]]
        tbs = [self.sq, self.rtmp]
        war = [self.sq_war[0], self.sq_war[1]]
        mmev = None
        i = 0
        for r in range(nr):
            for cc in range(ncc):
                sl = i % 2
                evd = p.dma_s(wfs[sl], wv[:, r, cc], waits=[war[sl]],
                              sem=f"wf{sl}")
                eva = p.s(lambda e, o=tbs[sl], i_=wfs[sl]:
                          e.activation(o, i_, AF.Abs), waits=[evd])
                for hh in range(2):
                    mmev = self.mm(PR, self.cp["c_ones"],
                                   tbs[sl][:, hh * 512:(hh + 1) * 512],
                                   [2] if i == 0 and hh == 0 else [],
                                   waits=[eva], start=(i == 0 and hh == 0),
                                   stop=(i == ntile - 1 and hh == 1))
                war[sl] = eva
                self.sq_war[sl] = mmev
                i += 1
        tot = self.scal[0:1, sidx:sidx + 1]
        ev1 = p.v(lambda e, o=tot, i=PR: e.tensor_reduce(o, i, AX.X, ALU.add),
                  waits=[mmev])
        self.read_bank(2, ev1)
        N = float(rows * cols)
        sval = self.scal[0:1, sidx + 1:sidx + 2]
        ev2 = p.v(lambda e, o=sval, i=tot, n=1.0 / N:
                  e.tensor_scalar(o, i, n, EPSQ, ALU.mult, ALU.max),
                  waits=[hard(ev1)])
        rs = self.scal[0:1, sidx + 2:sidx + 3]
        ev3 = p.v(lambda e, o=rs, i=sval: e.reciprocal(o, i),
                  waits=[hard(ev2)])
        bc = self.mm(self.pbank[6][:, 0:1], self.cp["c_onesrow"], rs, [6],
                     waits=[hard(ev3)])
        rscol = self.scal[:, sidx + 3:sidx + 4]
        ev4 = p.s(lambda e, o=rscol, i=self.pbank[6][:, 0:1]: e.copy(o, i),
                  waits=[bc])
        self.read_bank(6, ev4)
        evw = None
        wfw = [war[0], war[1]]
        tbw = [self.sq_war[0], self.sq_war[1]]
        t1w = [None, None]
        i = 0
        for r in range(nr):
            for cc in range(ncc):
                sl = i % 2
                evd = p.dma_s(wfs[sl], wv[:, r, cc], waits=[wfw[sl]],
                              sem=f"wf{sl}")
                ea = p.s(lambda e, o=t1s[sl], i_=wfs[sl], sc=rscol:
                         e.activation(o, i_, AF.Copy, scale=sc, bias=MAGIC),
                         waits=[evd, hard(ev4), t1w[sl]])
                wfw[sl] = ea
                eb = p.v(lambda e, o=tbs[sl], i_=t1s[sl]:
                         e.tensor_scalar(o, i_, -MAGIC, 1.0, ALU.add, ALU.min),
                         waits=[ea, tbw[sl]])
                t1w[sl] = eb
                ec = p.g(lambda e, o=tbs[sl], i_=tbs[sl]:
                         e.tensor_scalar_max(o, i_, -1.0), waits=[eb])
                evw = p.dma_s(tv[:, r, cc], tbs[sl], waits=[ec], sem="tern")
                tbw[sl] = evw
                i += 1
        self.sq_war = [evw, evw]
        return evw

    def emit_butterfly(self, tiles, waits):
        p = self.p
        n = len(tiles)
        ev = None
        s = 1
        first = True
        while s < n:
            for base in range(0, n, 2 * s):
                for k in range(s):
                    a = tiles[base + k]
                    b = tiles[base + k + s]
                    ev = p.v(lambda e, o=a, x=a, y=b: e.tensor_add(o, x, y),
                             waits=waits if first else [])
                    first = False
                    ev = p.v(lambda e, o=b, x=b, y=a:
                             e.scalar_tensor_tensor(o, x, -2.0, y,
                                                    ALU.mult, ALU.add), waits=[])
            s *= 2
        return ev

    def emit_actquant(self, nch, out_fn, eps_const, eps_row, waits, w=512):
        p = self.p
        M = self.rb[:, 0:512][:, 0:w]
        fv = self.fscr[:, 0:nch * w].rearrange("p (c w) -> p w c", w=w)
        ev = p.v(lambda e, o=M, i=fv:
                 e.tensor_reduce(o, i, AX.X, ALU.max, apply_absolute_value=True),
                 waits=list(waits) + ([self.rb_war[0]] if self.rb_war[0] else []))
        PR = self.pbank[3][0:1, 0:w]
        evtr = None
        for q in range(w // 128):
            tw = [ev] + ([self.bwar[2]] if self.bwar[2] else [])
            self.bwar[2] = None
            tev = p.t(lambda e, o=self.pbank[2][:, 0:128],
                      i=M[:, q * 128:(q + 1) * 128], idn=self.cp["c_id"]:
                      e.transpose(o, i, idn), waits=tw)
            col = self.srow[:, 4 + q:5 + q]
            rev = p.v(lambda e, o=col, i=self.pbank[2][:, 0:128]:
                      e.tensor_reduce(o, i, AX.X, ALU.max), waits=[tev])
            self.read_bank(2, rev)
            tw2 = [rev] + ([self.bwar[3]] if self.bwar[3] else [])
            self.bwar[3] = None
            evtr = p.t(lambda e, o=PR[:, q * 128:(q + 1) * 128], i=col,
                       idn=self.cp["c_id"]: e.transpose(o, i, idn), waits=tw2)
        self.rb_war[0] = None
        am = self.srow[0:1, 0:w]
        aw = [evtr] + ([self.am_war] if self.am_war else [])
        if eps_row is None:
            evam = p.v(lambda e, o=am, i=PR, cst=eps_const:
                       e.tensor_scalar_max(o, i, cst), waits=aw)
        else:
            evam = p.v(lambda e, o=am, a=PR, b=eps_row:
                       e.tensor_tensor(o, a, b, ALU.max), waits=aw)
        self.read_bank(3, evam)
        sc = self.srow[0:1, 512:512 + w]
        p.v(lambda e, o=sc, i=am: e.reciprocal(o, i), waits=[])
        ev6 = p.v(lambda e, o=sc, i=sc: e.tensor_scalar_mul(o, i, 127.0),
                  waits=[])
        bc = self.mm(self.pbank[6][:, 0:w], self.cp["c_onesrow"], sc, [6],
                     waits=[ev6])
        SCB = self.rb[:, 512:512 + w]
        sw = [bc] + ([self.rb_war[1]] if self.rb_war[1] else [])
        evb = p.s(lambda e, o=SCB, i=self.pbank[6][:, 0:w]: e.copy(o, i),
                  waits=sw)
        self.read_bank(6, evb)
        self.am_war = evb
        evq = None
        for c in range(nch):
            z = self.fscr[:, c * w:(c + 1) * w]
            p.v(lambda e, o=z, a=z, b=SCB: e.tensor_mul(o, a, b), waits=[evb])
            evq = p.v(lambda e, o=out_fn(c), i=z:
                      e.tensor_scalar(o, i, MAGIC, -MAGIC, ALU.add, ALU.add),
                      waits=[])
        self.rb_war[1] = evq
        self.rb_war[0] = evq
        return evq, am

    def emit_ffn0(self, si):
        p = self.p
        self.nq = 1024
        xap, cast = self.xsrc(si)
        dap, fin = self.xdst(si)
        p.fence(list(self.phase_end))
        evw_up = self.emit_wprep(self.io["l0_wupT"], self.ternup, 2048, 4096, 0)
        evw_dn = self.emit_wprep(self.io["l0_wdnT"], self.terndn, 4096, 2048, 4)
        EPS1 = EPSQ / C2
        for hf in range(2):
            T0 = hf * 1024
            evy = None
            for tt in range(2):
                evy = self.emit_ln(xap, cast, T0 + tt * 512, tt * 512)
            tiles = [self.yb(c, 0, 1024) for c in range(16)]
            evb = self.emit_butterfly(tiles, [evy])
            lamrow = self.srow[0:1, 1024:2048]
            evq2 = None
            evl = None
            for sub in range(2):
                evf = None
                for c in range(16):
                    bk = c % 2
                    m1 = self.mm(self.pbank[bk], self.cp["c_h128"],
                                 self.yb(c, sub * 512, (sub + 1) * 512),
                                 [bk], waits=[evb, evq2])
                    evf = p.s(lambda e, o=self.fscr[:, c * 512:(c + 1) * 512],
                              i=self.pbank[bk]: e.copy(o, i), waits=[m1])
                    self.read_bank(bk, evf)
                def yqout(c, s=sub):
                    return self.qb(c, s * 512, (s + 1) * 512)
                evq, am = self.emit_actquant(16, yqout, EPS1, None, [evf])
                lm = lamrow[:, sub * 512:(sub + 1) * 512]
                evl = p.v(lambda e, o=lm, i=am, s1=self.scal[0:1, 1:2]:
                          e.tensor_scalar(o, i, s1, C2 / 127.0, ALU.mult,
                                          ALU.mult), waits=[])
                evq2 = evq
            evu = None
            for fo in range(32):
                par = fo % 2
                wev = self.load_wchunk(self.ternup, 2048, fo, par,
                                       waits=[self.ws_war[par], evw_up],
                                       cast=False)
                for tt in range(2):
                    bk = (fo * 2 + tt) % 2
                    pbx = self.pbank[bk]
                    mmev = None
                    for kc in range(16):
                        mmev = self.mm(pbx, self.ws_tile(par, kc),
                                       self.qb(kc, tt * 512, (tt + 1) * 512),
                                       [bk] if kc == 0 else [],
                                       waits=[wev, evq2] if kc == 0 else [],
                                       start=(kc == 0), stop=(kc == 15))
                    dst = self.arC[:, fo * 1024 + tt * 512:
                                   fo * 1024 + (tt + 1) * 512]
                    evu = p.s(lambda e, o=dst, i=pbx: e.activation(o, i, AF.Relu),
                              waits=[mmev])
                    self.read_bank(bk, evu)
                    self.ws_war[par] = mmev
            evsq = None
            for fo in range(32):
                hb = self.arC[:, fo * 1024:(fo + 1) * 1024]
                evsq = p.v(lambda e, o=hb, a=hb, b=hb: e.tensor_mul(o, a, b),
                           waits=[evu])
            tiles = [self.arC[:, c * 1024:(c + 1) * 1024] for c in range(32)]
            evb4 = self.emit_butterfly(tiles, [evsq])
            evq4 = None
            e5 = None
            mu_row = self.of[0:1, 0:1024]
            for sub in range(4):
                lm = lamrow[:, sub * 256:(sub + 1) * 256]
                l2r = self.rb[0:1, 768:1024][:, 0:256]
                p.v(lambda e, o=l2r, a=lm, b=lm: e.tensor_mul(o, a, b),
                    waits=[evl] + ([self.rb_war[1]] if self.rb_war[1] else []))
                ep2 = self.rb[0:1, 512:768][:, 0:256]
                p.v(lambda e, o=ep2, i=l2r: e.reciprocal(o, i), waits=[])
                e3 = p.v(lambda e, o=ep2, i=ep2:
                         e.tensor_scalar_mul(o, i, EPSQ / C4), waits=[])
                evf = None
                for c in range(32):
                    bk = c % 2
                    m1 = self.mm(self.pbank[bk][:, 0:256], self.cp["c_h128"],
                                 self.arC[:, c * 1024 + sub * 256:
                                          c * 1024 + (sub + 1) * 256],
                                 [bk], waits=[evb4, evq4])
                    evf = p.s(lambda e, o=self.fscr[:, c * 256:(c + 1) * 256],
                              i=self.pbank[bk][:, 0:256]: e.copy(o, i),
                              waits=[m1])
                    self.read_bank(bk, evf)
                def hqout(c, s=sub):
                    return self.arC[:, c * 1024 + s * 256:
                                    c * 1024 + (s + 1) * 256]
                evq4, am2 = self.emit_actquant(32, hqout, None, ep2,
                                               [evf, e3], w=256)
                mu = mu_row[:, sub * 256:(sub + 1) * 256]
                p.v(lambda e, o=mu, a=am2, b=l2r: e.tensor_mul(o, a, b),
                    waits=[self.mu_war] if self.mu_war else [])
                e5 = p.v(lambda e, o=mu, i=mu, s2=self.scal[0:1, 5:6]:
                         e.tensor_scalar(o, i, s2, C4 / 127.0, ALU.mult,
                                         ALU.mult), waits=[])
            evmu = []
            for tt in range(2):
                bw = [e5] + [x for x in self.rb_war if x]
                bc = self.mm(self.pbank[6], self.cp["c_onesrow"],
                             mu_row[:, tt * 512:(tt + 1) * 512], [6], waits=bw)
                evx = p.s(lambda e, o=self.rb[:, tt * 512:(tt + 1) * 512],
                          i=self.pbank[6]: e.copy(o, i), waits=[bc])
                self.read_bank(6, evx)
                evmu.append(evx)
            self.rb_war = [evmu[0], evmu[1]]
            self.mu_war = evmu[1]
            evout = None
            for fo in range(16):
                par = fo % 2
                wev = self.load_wchunk(self.terndn, 4096, fo, par,
                                       waits=[self.ws_war[par], evw_dn],
                                       cast=False)
                for tt in range(2):
                    bk = (fo * 2 + tt) % 2
                    pbx = self.pbank[bk]
                    mmev = None
                    for kc in range(32):
                        mmev = self.mm(pbx, self.ws_tile(par, kc),
                                       self.arC[:, kc * 1024 + tt * 512:
                                                kc * 1024 + (tt + 1) * 512],
                                       [bk] if kc == 0 else [],
                                       waits=[wev, evq4] if kc == 0 else [],
                                       start=(kc == 0), stop=(kc == 31))
                    t1 = self.rtmp[:, tt * 512:(tt + 1) * 512]
                    ev1 = p.v(lambda e, o=t1, a=pbx,
                              b=self.rb[:, tt * 512:(tt + 1) * 512]:
                              e.tensor_mul(o, a, b),
                              waits=[mmev, evmu[tt]] +
                                    ([self.t1_war] if self.t1_war else []))
                    self.read_bank(bk, ev1)
                    g = tt % 2
                    if fo % 4 == 0:
                        self.xgev[g] = self.load_x4(xap, cast, fo,
                                                    T0 + tt * 512, g,
                                                    waits=[self.xt_war[g]])
                    dx = self.xsl(g, fo % 4)
                    evd = self.xgev[g]
                    if fin:
                        otile = self.of[:, tt * 512:(tt + 1) * 512]
                    else:
                        otile = self.sq[:, tt * 512:(tt + 1) * 512]
                    evo = p.v(lambda e, o=otile, a=t1, b=dx: e.tensor_add(o, a, b),
                              waits=[evd] +
                                    ([self.ot_war[tt]] if self.ot_war[tt] else []) +
                                    ([self.sq_war[tt]] if self.sq_war[tt] else []))
                    self.xt_war[g] = evo
                    self.t1_war = evo
                    dd = dap.rearrange("(kc p) t -> kc p t", p=128)[
                        fo, :, T0 + tt * 512:T0 + (tt + 1) * 512]
                    evout = p.dma_s(dd, otile, waits=[evo], sem=f"st{tt}")
                    self.ot_war[tt] = evout
                    if not fin:
                        self.sq_war[tt] = evout
            self.phase_end = [e for e in self.ot_war if e]
            self.row_war = self.row_war + [evl]
        return self.phase_end

    def emit_ffn1(self, si):
        p = self.p
        self.nq = 1024
        xap, cast = self.xsrc(si)
        dap, fin = self.xdst(si)
        p.fence(list(self.phase_end))
        evs = p.dma_s(self.scal[:, 16:32], self.io["l1_fscale"], sem="const")
        for hf in range(2):
            T0 = hf * 1024
            evy = None
            for tt in range(2):
                evy = self.emit_ln(xap, cast, T0 + tt * 512, tt * 512)
            evo = None
            for fo in range(16):
                par = fo % 2
                wev = self.load_wchunk(self.io["l1_fw"], 2048, fo, par,
                                       waits=[self.ws_war[par]])
                for tt in range(2):
                    bk = (fo * 2 + tt) % 2
                    pbx = self.pbank[bk]
                    mmev = None
                    for kc in range(16):
                        mmev = self.mm(pbx, self.ws_tile(par, kc),
                                       self.yb(kc, tt * 512, (tt + 1) * 512),
                                       [bk] if kc == 0 else [],
                                       waits=[wev, evy, evs] if kc == 0 else [],
                                       start=(kc == 0), stop=(kc == 15))
                    dst = self.qb(fo, tt * 512, (tt + 1) * 512)
                    evo = p.s(lambda e, o=dst, i=pbx, sc=self.scal[:, 16 + fo:17 + fo]:
                              e.activation(o, i, AF.Copy, scale=sc), waits=[mmev])
                    self.read_bank(bk, evo)
                    self.ws_war[par] = mmev
            evout = None
            for tt in range(2):
                PR = self.pbank[3][0:1, 0:512]
                mmev = None
                for fo in range(16):
                    sqs = self.sq[:, (fo % 2) * 512:(fo % 2 + 1) * 512]
                    evsq = p.s(lambda e, o=sqs,
                               i=self.qb(fo, tt * 512, (tt + 1) * 512):
                               e.activation(o, i, AF.Square),
                               waits=[evo, self.sq_war[fo % 2]])
                    mmev = self.mm(PR, self.cp["c_ones"], sqs,
                                   [3] if fo == 0 else [], waits=[evsq],
                                   start=(fo == 0), stop=(fo == 15))
                    self.sq_war[fo % 2] = mmev
                r1 = self.srow[0:1, 0:512]
                r2 = self.srow[0:1, 512:1024]
                r3 = self.srow[0:1, 1024:1536]
                r4 = self.srow[0:1, 1536:2048]
                e1 = p.s(lambda e, o=r1, i=PR: e.activation(o, i, AF.Ln,
                                                            bias=1e-38),
                         waits=[mmev] + self.row_war)
                self.read_bank(3, e1)
                e2 = p.s(lambda e, o=r1, i=r1: e.activation(o, i, AF.Exp,
                                                            scale=0.5), waits=[])
                e3 = p.v(lambda e, o=r2, i=r1: e.tensor_scalar_max(o, i, 1e-7),
                         waits=[e2])
                e4 = p.s(lambda e, o=r3, i=r2: e.activation(o, i, AF.Exp,
                                                            scale=-2.0),
                         waits=[e3])
                e5 = p.v(lambda e, o=r4, i=r3:
                         e.tensor_scalar(o, i, -1.0, 1.0, ALU.mult, ALU.add),
                         waits=[e4])
                e6 = p.v(lambda e, o=r3, i=r3: e.tensor_scalar_add(o, i, 1.0),
                         waits=[e4])
                e7 = p.v(lambda e, o=r3, i=r3: e.reciprocal(o, i), waits=[])
                e8 = p.v(lambda e, o=r4, a=r4, b=r3: e.tensor_mul(o, a, b),
                         waits=[])
                e9 = p.v(lambda e, o=r3, i=r2: e.tensor_scalar_add(o, i, 1e-7),
                         waits=[])
                e10 = p.v(lambda e, o=r3, i=r3: e.reciprocal(o, i), waits=[])
                e11 = p.v(lambda e, o=r4, a=r4, b=r3: e.tensor_mul(o, a, b),
                          waits=[])
                e12 = p.v(lambda e, o=r1, a=r1, b=r4: e.tensor_mul(o, a, b),
                          waits=[])
                e13 = p.v(lambda e, o=r1, i=r1: e.tensor_scalar_max(o, i, 1e-7),
                          waits=[])
                e14 = p.v(lambda e, o=r1, i=r1: e.reciprocal(o, i), waits=[])
                e15 = p.v(lambda e, o=r1, i=r1:
                          e.tensor_scalar(o, i, 0.99, 1.0, ALU.mult, ALU.min),
                          waits=[])
                e16 = p.v(lambda e, o=r4, a=r4, b=r1: e.tensor_mul(o, a, b),
                          waits=[])
                self.row_war = [e16]
                bc = self.mm(self.pbank[6], self.cp["c_onesrow"], r4, [6],
                             waits=[e16])
                RB = self.rb[:, tt * 512:(tt + 1) * 512]
                evb = p.s(lambda e, o=RB, i=self.pbank[6]: e.copy(o, i),
                          waits=[bc, self.rb_war[tt]])
                self.read_bank(6, evb)
                self.row_war = [bc]
                for fo in range(16):
                    t1 = self.rtmp[:, 0:512]
                    ev1 = p.v(lambda e, o=t1,
                              a=self.qb(fo, tt * 512, (tt + 1) * 512), b=RB:
                              e.tensor_mul(o, a, b),
                              waits=[evb] + ([self.t1_war] if self.t1_war else []))
                    g = tt % 2
                    if fo % 4 == 0:
                        self.xgev[g] = self.load_x4(xap, cast, fo,
                                                    T0 + tt * 512, g,
                                                    waits=[self.xt_war[g]])
                    dx = self.xsl(g, fo % 4)
                    evd = self.xgev[g]
                    if fin:
                        otile = self.of[:, (fo % 2) * 512:(fo % 2 + 1) * 512]
                    else:
                        otile = self.sq[:, (fo % 2) * 512:(fo % 2 + 1) * 512]
                    evo2 = p.v(lambda e, o=otile, a=t1, b=dx:
                               e.tensor_add(o, a, b),
                               waits=[evd] +
                                     ([self.ot_war[fo % 2]] if self.ot_war[fo % 2] else []) +
                                     ([self.sq_war[fo % 2]] if self.sq_war[fo % 2] else []))
                    self.xt_war[g] = evo2
                    self.t1_war = evo2
                    dd = dap.rearrange("(kc p) t -> kc p t", p=128)[
                        fo, :, T0 + tt * 512:T0 + (tt + 1) * 512]
                    evout = p.dma_s(dd, otile, waits=[evo2], sem=f"st{fo % 2}")
                    self.ot_war[fo % 2] = evout
                    if not fin:
                        self.sq_war[fo % 2] = evout
                self.rb_war[tt] = ev1
            self.phase_end = [e for e in self.ot_war if e]
        return self.phase_end



def build_kb(sublayers=(0, 1, 2, 3)):
    nc = bass.Bass("TRN2", target_bir_lowering=False)
    kb = KB(nc, sublayers)
    for si in sorted(sublayers):
        if si == 0:
            kb.emit_attn(0, 0)
        elif si == 1:
            kb.emit_ffn0(1)
        elif si == 2:
            kb.emit_attn(2, 1)
        elif si == 3:
            kb.emit_ffn1(3)
    return nc, kb


def finalize(nc, kb, extra_evs=()):
    kb.p.fence(list(kb.phase_end) + list(extra_evs))
    with nc.Block() as block:
        kb.p.replay(block)
    return nc


def build_nc(sublayers=(0, 1, 2, 3)):
    nc, kb = build_kb(sublayers)
    finalize(nc, kb)
    return nc, kb


def _prep_inputs(inputs):
    f = np.float32
    def cT(a):
        return np.ascontiguousarray(np.asarray(a, dtype=f).T)
    def c(a):
        return np.ascontiguousarray(np.asarray(a, dtype=f))
    com = dict(
        l0_wq=c(inputs["L0_attn_wq"]), l0_wk=c(inputs["L0_attn_wk"]),
        l0_wv=c(inputs["L0_attn_wv"]), l0_wo=c(inputs["L0_attn_wo"]),
        l0_wupT=cT(inputs["L0_ffn_w_up"]), l0_wdnT=cT(inputs["L0_ffn_w_down"]),
        l1_wqT=cT(inputs["L1_attn_Wq"]), l1_wkT=cT(inputs["L1_attn_Wk"]),
        l1_wvT=cT(inputs["L1_attn_Wv"]), l1_woT=cT(inputs["L1_attn_Wo"]),
        l1_fw=c(inputs["L1_ffn_weight"]),
        l1_fscale=np.ascontiguousarray(
            np.asarray(inputs["L1_ffn_scale"], dtype=f).reshape(16, 128).T),
    )
    com.update(host_constants())
    x = np.asarray(inputs["L0_attn_x"], dtype=f)
    maps = []
    for b in range(8):
        m = dict(com)
        m["xT"] = np.ascontiguousarray(x[b].T)
        maps.append(m)
    return maps


def _run(inputs, sublayers=(0, 1, 2, 3)):
    maps = _prep_inputs(inputs)
    nc, kb = build_nc(sublayers)
    res = run_bass_kernel_spmd(nc, maps, core_ids=list(range(8)))
    out = np.stack([np.asarray(r["outT"]).T for r in res.results])
    return np.ascontiguousarray(out)


def kernel(**inputs):
    return _run(inputs, (0, 1, 2, 3))


# version 07
# speedup vs baseline: 1.4950x; 1.2965x over previous
import math
import numpy as np
import ml_dtypes
import concourse.bass as bass
import concourse.mybir as mybir
from concourse.bass_utils import run_bass_kernel_spmd
from contextlib import ExitStack

F32 = mybir.dt.float32
BF16 = mybir.dt.bfloat16
AF = mybir.ActivationFunctionType
ALU = mybir.AluOpType
AX = mybir.AxisListType

D = 2048
T = 2048
NCH = 16
WIN = 512
DH = 128
NH = 16
MAGIC = 1.5 * 2.0 ** 23
SCALE = DH ** -0.5
ACT_CONSTS = [1e-5, 1e-38, -0.5, 0.5, -2.0, SCALE, MAGIC, math.log(SCALE)]
C2 = 1.0 / math.sqrt(2048.0)
C4 = 1.0 / math.sqrt(4096.0)
EPSQ = 1e-5
NEG = -1.0e30

ENGINES = ("tensor", "vector", "scalar", "gpsimd", "sync")



def _rope_tables():
    j = np.arange(64)
    invf = 10000.0 ** (-2.0 * j / 128.0)
    t = np.arange(T)
    fr = t[None, :] * invf[:, None]
    cos = np.cos(fr).astype(np.float32)
    sin = np.sin(fr).astype(np.float32)
    ct0 = np.concatenate([cos, cos], axis=0)
    st0 = np.concatenate([-sin, sin], axis=0)
    ct1 = np.zeros((128, T), np.float32)
    st1 = np.zeros((128, T), np.float32)
    ct1[0::2] = cos
    ct1[1::2] = cos
    st1[0::2] = -sin
    st1[1::2] = sin
    bf = ml_dtypes.bfloat16
    return (ct0.astype(bf), st0.astype(bf), ct1.astype(bf), st1.astype(bf))


def _hadamard(n):
    h = np.array([[1.0]])
    while h.shape[0] < n:
        h = np.block([[h, h], [h, -h]])
    return h


def host_constants():
    bf = ml_dtypes.bfloat16
    ct0, st0, ct1, st1 = _rope_tables()
    h128 = _hadamard(128).astype(bf)
    psw = np.zeros((128, 128), np.float32)
    for i in range(64):
        psw[2 * i + 1, 2 * i] = 1.0
        psw[2 * i, 2 * i + 1] = 1.0
    psw = psw.astype(bf)
    ph0 = np.zeros((128, 128), np.float32)
    for i in range(64):
        ph0[i + 64, i] = 1.0
        ph0[i, i + 64] = 1.0
    ph0 = ph0.astype(bf)
    r = np.arange(128)[:, None]
    c = np.arange(640)[None, :]
    mband = np.where((c >= r) & (c < r + WIN), 0.0, NEG).astype(np.float32)
    ohc = np.zeros((128, 16 * 16), np.float32)
    for h in range(16):
        ohc[:, 16 * h + h] = 1.0
    ohc = ohc.astype(bf)
    ohr = np.zeros((16, 16 * 128), np.float32)
    for h in range(16):
        ohr[h, 128 * h:128 * (h + 1)] = 1.0
    ones_col = np.ones((128, 1), np.float32).astype(bf)
    cmean_col = np.full((128, 1), 1.0 / 2048.0, np.float32).astype(bf)
    ones_row = np.ones((1, 128), np.float32)
    acv = np.array(ACT_CONSTS, np.float32)
    actc = np.broadcast_to(acv[None, :], (128, len(ACT_CONSTS))).copy()
    return dict(
        c_actc=actc,
        c_ct0=ct0, c_st0=st0, c_ct1=ct1, c_st1=st1,
        c_h128=h128, c_psw=psw, c_psw0=ph0, c_mband=mband,
        c_ohc=ohc, c_ohr=ohr.astype(np.float32),
        c_ones=ones_col, c_cmean=cmean_col, c_onesrow=ones_row,
        c_id=np.eye(128, dtype=np.float32),
    )



class Sem:
    __slots__ = ("name", "h", "n")

    def __init__(self, name, handle):
        self.name = name
        self.h = handle
        self.n = 0


class Ev:
    __slots__ = ("sem", "n", "eng")

    def __init__(self, sem, n, eng):
        self.sem = sem
        self.n = n
        self.eng = eng


def hard(ev):
    return Ev(ev.sem, ev.n, "hard") if ev is not None else None


class Prog:

    def __init__(self, nc):
        self.nc = nc
        self.streams = {e: [] for e in ENGINES}
        self.sems = {}
        for name in ("pe", "dv", "ac", "pl"):
            self.sems[name] = Sem(name, nc.alloc_semaphore("s_" + name))
        self.dsems = {}
        self.bank_war = {}
        self.nsteps = 0

    def go(self, eng, fn, waits=(), inc=None, inck=1):
        waits = tuple(w for w in waits if w is not None)
        for w in waits:
            assert w.n <= w.sem.n, f"wait on future event {w.sem.name}:{w.n}>{w.sem.n}"
        self.streams[eng].append((waits, fn, inc, inck))
        self.nsteps += 1
        if inc is not None:
            inc.n += inck
            return Ev(inc, inc.n, "dma" if eng in ("sync",) and inck >= 16 else eng)
        return None

    def t(self, fn, waits=(), inc=True):
        return self.go("tensor", fn, waits, self.sems["pe"] if inc else None)

    def v(self, fn, waits=(), inc=True):
        return self.go("vector", fn, waits, self.sems["dv"] if inc else None)

    def s(self, fn, waits=(), inc=True):
        return self.go("scalar", fn, waits, self.sems["ac"] if inc else None)

    def g(self, fn, waits=(), inc=True):
        return self.go("gpsimd", fn, waits, self.sems["pl"] if inc else None)

    def _dsem(self, name):
        if name not in self.dsems:
            self.dsems[name] = Sem(name, self.nc.alloc_semaphore("d_" + name))
        return self.dsems[name]

    def dma_s(self, out, in_, waits=(), sem="misc_s"):
        ev = self.go("sync", lambda e, o=out, i=in_: e.dma_start(out=o, in_=i),
                     waits, self._dsem(sem), 16)
        return Ev(ev.sem, ev.n, "dma")

    def dma_g(self, out, in_, waits=(), sem="misc_g"):
        ev = self.go("gpsimd", lambda e, o=out, i=in_: e.dma_start(out=o, in_=i),
                     waits, self._dsem(sem), 16)
        return Ev(ev.sem, ev.n, "dma")

    def fence(self, evs, engines=ENGINES):
        evs = [e for e in evs if e is not None]
        for eng in engines:
            self.go(eng, None, tuple(Ev(e.sem, e.n, "dma") for e in evs))

    def replay(self, block):
        for eng in ENGINES:
            def body(e, eng=eng):
                seen = {}
                for (waits, fn, inc, inck) in self.streams[eng]:
                    for w in waits:
                        if w.eng == eng:
                            continue
                        if seen.get(w.sem.name, -1) >= w.n:
                            continue
                        e.wait_ge(w.sem.h, w.n)
                        seen[w.sem.name] = w.n
                    if fn is None:
                        continue
                    ins = fn(e)
                    if inc is not None:
                        ins.then_inc(inc.h, inck)
            getattr(block, eng)(body)



class KB:
    def __init__(self, nc, sublayers):
        self.nc = nc
        self.sub = tuple(sublayers)
        self.p = Prog(nc)

        dt = nc.dram_tensor
        self.xT = dt("xT", [D, T], F32, kind="ExternalInput").ap()
        self.io = {}
        for nm, sh, ty in [
            ("l0_wq", [D, 2048], F32), ("l0_wk", [D, 512], F32),
            ("l0_wv", [D, 512], F32), ("l0_wo", [D, 2048], F32),
            ("l0_wupT", [D, 4096], F32), ("l0_wdnT", [4096, D], F32),
            ("l1_wqT", [D, 2048], F32), ("l1_wkT", [D, 2048], F32),
            ("l1_wvT", [D, 2048], F32), ("l1_woT", [D, 2048], F32),
            ("l1_fw", [D, 2048], F32), ("l1_fscale", [128, 16], F32),
            ("c_ct0", [128, T], BF16), ("c_st0", [128, T], BF16),
            ("c_ct1", [128, T], BF16), ("c_st1", [128, T], BF16),
            ("c_h128", [128, 128], BF16), ("c_psw", [128, 128], BF16),
            ("c_psw0", [128, 128], BF16),
            ("c_mband", [128, 640], F32), ("c_ohc", [128, 256], BF16),
            ("c_ohr", [16, 2048], F32), ("c_ones", [128, 1], BF16),
            ("c_cmean", [128, 1], BF16), ("c_onesrow", [1, 128], F32),
            ("c_id", [128, 128], F32),
            ("c_actc", [128, len(ACT_CONSTS)], F32),
        ]:
            self.io[nm] = dt(nm, sh, ty, kind="ExternalInput").ap()
        self.xres = dt("xres", [D, T], BF16, kind="Internal").ap()
        self.ternup = dt("ternup", [D, 4096], BF16, kind="Internal").ap()
        self.terndn = dt("terndn", [4096, D], BF16, kind="Internal").ap()
        self.outT = dt("outT", [D, T], F32, kind="ExternalOutput").ap()

        A = nc.alloc_sbuf_tensor
        self.arA = A("arA", [128, 16384], BF16).ap()
        self.arB = A("arB", [128, 16384], BF16).ap()
        self.arC = A("arC", [128, 32768], BF16).ap()
        ovl_t = A("ovl", [128, 20480], BF16)
        ob = ovl_t.ap()
        obf = ovl_t.bitcast(F32).ap()
        self.ring = ob[:, 0:3840]
        self.ct = ob[:, 3840:5888]
        self.st = ob[:, 5888:7936]
        self.sq = ob[:, 7936:8960]
        self.rtmp = ob[:, 8960:9984]
        self.tb0 = ob[:, 16384:18432]
        self.tb1 = ob[:, 18432:20480]
        self.mband = obf[:, 4992:5632]
        self.ohc = ob[:, 11264:11520]
        self.ohr = obf[0:16, 5760:7808]
        self.fscr = obf[:, 0:8192]
        self.wf = obf[:, 8192:9216]
        self.of = obf[:, 9216:10240]
        self.ws = A("ws", [128, 8192], BF16).ap()
        self.srow = A("srow", [128, 2048], F32).ap()
        self.rb = A("rb", [128, 1024], F32).ap()
        self.xt = A("xt", [128, 4096], BF16).ap()
        self.cp = {}
        for nm, sh, ty in [("c_h128", [128, 128], BF16), ("c_psw", [128, 128], BF16),
                           ("c_psw0", [128, 128], BF16),
                           ("c_ones", [128, 1], BF16), ("c_cmean", [128, 1], BF16),
                           ("c_onesrow", [1, 128], F32), ("c_id", [128, 128], F32)]:
            self.cp[nm] = A(nm + "_s", sh, ty).ap()
        self.scal = A("scal", [128, 32], F32).ap()
        self.actc = A("actc", [128, len(ACT_CONSTS)], F32).ap()
        self.tchs = A("tchs", [128, 2], BF16).ap()[:, 0:1]
        for j, v in enumerate(ACT_CONSTS):
            nc.const_aps.register(F32, float(v), self.actc[:, j:j + 1])
        rem = nc.sbuf_bytes_remaining
        assert rem >= 0, f"sbuf over budget by {-rem}"
        self.sbuf_rem = rem
        self.first = min(self.sub)
        self.xt_war = [None, None]
        self.sq_war = [None, None]
        self.rb_war = [None, None]
        self.row_war = []
        self.ws_war = [None, None]
        self.kv_war = None
        self.phase_end = []
        self.ring_war = [None] * 6
        self.ru_war = []
        self.rn_war = []
        self.wf_war = None
        self.wf_war2 = None
        self.am_war = None
        self.lam_war = None
        self.mu_war = None
        self.t1_war = None
        self.ot_war = [None, None]
        self.limit_passes = None
        self.xgev = {0: None, 1: None}

        self.pbank = [nc.alloc_psum_tensor(f"pbk{i}", [128, 512], F32).ap()
                      for i in range(8)]
        self.bwar = [None] * 8

        evs = [self.p.dma_s(self.cp[nm], self.io[nm], sem="const")
               for nm in self.cp]
        evs.append(self.p.dma_s(self.actc, self.io["c_actc"], sem="const"))
        self.p.fence(evs)
        self._preload_act()

    def _preload_act(self):
        idx = None
        try:
            from concourse.hw_specs import get_activation_tables
            tabs = get_activation_tables(self.nc.m.arch)
            for i, fns in enumerate(tabs.values()):
                if AF.Ln in fns and AF.Exp in fns:
                    idx = i
                    break
        except Exception:
            idx = 6
        if idx is None:
            return
        def emit(e, idx=idx):
            return e.add_instruction(mybir.InstLoadActFuncSet(
                name=self.nc.get_next_instruction_name(),
                act_func_set_id=idx, ins=[], outs=[]))
        self.p.s(emit, inc=False)

    def mm(self, out_ap, lhsT, rhs, banks, waits=(), start=True, stop=True):
        w = list(waits)
        if start:
            for b in banks:
                if self.bwar[b] is not None:
                    w.append(self.bwar[b])
                    self.bwar[b] = None
        return self.p.t(lambda e, o=out_ap, l=lhsT, r=rhs, st=start, sp=stop:
                        e.matmul(o, l, r, start=st, stop=sp), w)

    def read_bank(self, b, ev):
        self.bwar[b] = ev

    def xsrc(self, i):
        return (self.xT, True) if i == self.first else (self.xres, False)

    def xdst(self, i):
        return (self.outT, True) if i == max(self.sub) else (self.xres, False)

    def load_x4(self, src, cast, c0, t0, g, waits=()):
        sap = src.rearrange("(kc p) t -> p kc t", p=128)[:, c0:c0 + 4, t0:t0 + 512]
        dstv = self.xt[:, g * 2048:(g + 1) * 2048].rearrange(
            "p (kc t) -> p kc t", t=512)
        return (self.p.dma_g if cast else self.p.dma_s)(dstv, sap, waits,
                                                        sem=f"xt{g}")

    def xsl(self, g, j):
        return self.xt[:, g * 2048 + j * 512: g * 2048 + (j + 1) * 512]

    def load_wchunk(self, w_ap, din, fo, par, waits=(), cast=True, touch=True):
        kch = din // 128
        src = w_ap.rearrange("(kc p) f -> p kc f", p=128)[
            :, :, fo * 128:(fo + 1) * 128]
        dst = self.ws[:, par * 4096:par * 4096 + kch * 128].rearrange(
            "p (kc f) -> p kc f", f=128)
        return (self.p.dma_g if cast else self.p.dma_s)(dst, src, waits,
                                                        sem=f"ws{par}")

    def ws_tile(self, par, kc):
        return self.ws[:, par * 4096 + kc * 128: par * 4096 + (kc + 1) * 128]

    def load_wpair(self, w_ap, din, fop, par, waits=(), cast=True):
        kch = din // 128
        src = w_ap.rearrange("(kc p) f -> p kc f", p=128)[
            :, :, fop * 256:(fop + 1) * 256]
        dst = self.ws[:, par * 4096:par * 4096 + kch * 256].rearrange(
            "p (kc f) -> p kc f", f=256)
        return (self.p.dma_g if cast else self.p.dma_s)(dst, src, waits,
                                                        sem=f"ws{par}")

    def ws_tile2(self, par, kc, sub):
        o = par * 4096 + kc * 256 + sub * 128
        return self.ws[:, o:o + 128]

    def yb(self, c, a, b):
        return self.arA[:, c * self.nq + a: c * self.nq + b]

    def qb(self, c, a, b):
        return self.arB[:, c * self.nq + a: c * self.nq + b]

    def obuf(self, c, a, b):
        return self.arA[:, c * self.nq + a: c * self.nq + b]

    def kb(self, c, a, b):
        return self.arC[:, c * self.nkv + a: c * self.nkv + b]

    def vb(self, tile, a, b):
        off = self.kvch * self.nkv
        return self.arC[:, off + tile * self.fv + a: off + tile * self.fv + b]

    def ringst(self, kt, a, b):
        s = (kt % 6) * 640
        return self.ring[:, s + a: s + b]

    def emit_ln(self, xap, cast, t0, tl):
        p = self.p
        PR1 = self.pbank[2][0:1, 0:512]
        PR2 = self.pbank[3][0:1, 0:512]
        cme = self.cp["c_cmean"]
        mm1 = mm2 = None
        sq0 = self.sq[:, 0:512]
        sq1 = self.sq[:, 512:1024]
        for g in range(4):
            c0 = 4 * g
            evd = self.load_x4(xap, cast, c0, t0, g % 2,
                               waits=[self.xt_war[g % 2]])
            for j in range(4):
                xs = self.xsl(g % 2, j)
                sqs = sq0 if j % 2 == 0 else sq1
                evq = p.s(lambda e, o=sqs, i=xs: e.activation(o, i, AF.Square),
                          waits=[evd, self.sq_war[j % 2]])
                first = (g == 0 and j == 0)
                last = (g == 3 and j == 3)
                mm1 = self.mm(PR1, cme, xs, [2] if first else [],
                              waits=[evd], start=first, stop=last)
                mm2 = self.mm(PR2, cme, sqs, [3] if first else [],
                              waits=[evq], start=first, stop=last)
                self.sq_war[j % 2] = mm2
            self.xt_war[g % 2] = mm1
        mrow = self.srow[0:1, 1024:1536]
        vrow = self.srow[0:1, 1536:2048]
        evm = p.s(lambda e, o=mrow, i=PR1: e.copy(o, i),
                  waits=[mm1] + self.row_war)
        ev2 = p.v(lambda e, o=vrow, a=mrow: e.tensor_mul(o, a, a),
                  waits=[evm] + self.row_war)
        ev3 = p.v(lambda e, o=vrow, a=PR2, b=vrow: e.tensor_sub(o, a, b),
                  waits=[mm2])
        self.read_bank(2, evm)
        self.read_bank(3, ev3)
        ev4 = p.s(lambda e, o=vrow, i=vrow: e.activation(o, i, AF.Ln, bias=1e-5),
                  waits=[ev3])
        evr = p.s(lambda e, o=vrow, i=vrow: e.activation(o, i, AF.Exp, scale=-0.5),
                  waits=[ev4])
        bm = self.mm(self.pbank[6], self.cp["c_onesrow"], mrow, [6], waits=[evm])
        br = self.mm(self.pbank[7], self.cp["c_onesrow"], vrow, [7], waits=[evr])
        RBm = self.rb[:, 0:512]
        RBr = self.rb[:, 512:1024]
        evbm = p.s(lambda e, o=RBm, i=self.pbank[6]: e.copy(o, i),
                   waits=[bm, self.rb_war[0]])
        evbr = p.s(lambda e, o=RBr, i=self.pbank[7]: e.copy(o, i),
                   waits=[br, self.rb_war[1]])
        self.read_bank(6, evbm)
        self.read_bank(7, evbr)
        self.row_war = [bm, br, ev2]
        evy = None
        for g in range(4):
            c0 = 4 * g
            evd = self.load_x4(xap, cast, c0, t0, g % 2,
                               waits=[self.xt_war[g % 2]])
            for j in range(4):
                c = c0 + j
                xs = self.xsl(g % 2, j)
                rt = self.rtmp[:, (j % 2) * 512:(j % 2 + 1) * 512]
                ev5 = p.v(lambda e, o=rt, a=xs, b=RBm: e.tensor_sub(o, a, b),
                          waits=[evd, evbm])
                ys = self.yb(c, tl, tl + 512)
                evy = p.v(lambda e, o=ys, a=rt, b=RBr: e.tensor_mul(o, a, b),
                          waits=[evbr])
                self.xt_war[g % 2] = ev5
        self.rb_war[0] = evy
        self.rb_war[1] = evy
        return evy

    def emit_proj(self, wap, nfo, rhs_fn, ntt, out_fn, waits, scale=1.0):
        p = self.p
        eve = None
        for fop in range(nfo // 2):
            par = fop % 2
            wev = self.load_wpair(wap, 2048, fop, par, waits=[self.ws_war[par]])
            mmev = None
            for sub in range(2):
                fo = 2 * fop + sub
                for tt in range(ntt):
                    bk = (fo * ntt + tt) % 2
                    pbx = self.pbank[bk]
                    mmev = None
                    for kc in range(16):
                        mmev = self.mm(pbx, self.ws_tile2(par, kc, sub),
                                       rhs_fn(kc, tt),
                                       [bk] if kc == 0 else [],
                                       waits=([wev] + list(waits)) if kc == 0 else [],
                                       start=(kc == 0), stop=(kc == 15))
                    eve = p.s(lambda e, o=out_fn(fo, tt), i=pbx, sc=scale:
                              e.activation(o, i, AF.Copy, scale=sc), waits=[mmev])
                    self.read_bank(bk, eve)
            self.ws_war[par] = mmev
        return eve

    def emit_attn(self, si, L):
        p = self.p
        nq = 1024 if L == 0 else 512
        self.nq = nq
        self.nkv = nq + 512
        self.kvch = 4 if L == 0 else 16
        self.fv = 512 if L == 0 else 2048
        rep = 4 if L == 0 else 1
        names = (("l0_wq", "l0_wk", "l0_wv", "l0_wo", "c_ct0", "c_st0") if L == 0
                 else ("l1_wqT", "l1_wkT", "l1_wvT", "l1_woT", "c_ct1", "c_st1"))
        wq, wk, wv, wo, ctn, stn = names
        xap, cast = self.xsrc(si)
        dap, fin = self.xdst(si)
        npass = T // nq if self.limit_passes is None else self.limit_passes

        pw = list(self.phase_end)
        evs = [p.dma_s(self.ct, self.io[ctn], waits=pw, sem="const"),
               p.dma_s(self.st, self.io[stn], waits=pw, sem="const"),
               p.dma_s(self.mband, self.io["c_mband"], waits=pw, sem="const"),
               p.dma_s(self.ohc, self.io["c_ohc"], waits=pw, sem="const"),
               p.dma_s(self.ohr, self.io["c_ohr"], waits=pw, sem="const")]
        p.fence(evs + pw)

        for ps in range(npass):
            if ps > 0:
                p.fence(list(self.phase_end))
            Q0 = ps * nq
            B0 = max(0, Q0 - 512)
            halo = Q0 - B0
            nkt = (Q0 + nq - B0) // 128
            hkt = halo // 128

            cpev = []
            if ps > 0:
                B0p = max(0, Q0 - nq - 512)
                soff = Q0 - 512 - B0p
                for c in range(self.kvch):
                    src = self.kb(c, soff, soff + 512)
                    dst = self.kb(c, 0, 512)
                    cpev.append(p.g(lambda e, o=dst, i=src: e.tensor_copy(o, i),
                                    waits=[self.kv_war]))
                stile = soff // 128
                for tl in range(4):
                    src = self.vb(stile + tl, 0, self.fv)
                    dst = self.vb(tl, 0, self.fv)
                    cpev.append(p.g(lambda e, o=dst, i=src: e.tensor_copy(o, i),
                                    waits=[self.kv_war]))
            cpall = cpev[-1] if cpev else None

            evy = None
            for tt in range(nq // 512):
                evy = self.emit_ln(xap, cast, Q0 + tt * 512, tt * 512)
            ylast = evy

            def yrhs(kc, tt):
                return self.yb(kc, tt * 512, (tt + 1) * 512)
            evk = self.emit_proj(self.io[wk], self.kvch, yrhs, nq // 512,
                                 lambda fo, tt: self.kb(fo, halo + tt * 512,
                                                        halo + (tt + 1) * 512),
                                 waits=[ylast] + cpev)
            evv = None
            for fcg in range(self.fv // 256):
                par = fcg % 2
                src = self.io[wv].rearrange("(kc p) f -> p kc f", p=128)[
                    :, :, fcg * 256:(fcg + 1) * 256]
                dstw = self.ws[:, par * 4096:par * 4096 + 4096].rearrange(
                    "p (kc f) -> p kc f", f=256)
                wev = p.dma_g(dstw, src, waits=[self.ws_war[par]],
                              sem=f"ws{par}")
                for b in range(nq // 128):
                    bk = (fcg * (nq // 128) + b) % 2
                    pbx = self.pbank[bk][:, 0:256]
                    mmev = None
                    for kc in range(16):
                        lt = self.yb(kc, b * 128, (b + 1) * 128)
                        rt = self.ws[:, par * 4096 + kc * 256:
                                     par * 4096 + (kc + 1) * 256]
                        mmev = self.mm(pbx, lt, rt, [bk] if kc == 0 else [],
                                       waits=([wev, ylast] + cpev) if kc == 0 else [],
                                       start=(kc == 0), stop=(kc == 15))
                    dst = self.vb(hkt + b, fcg * 256, (fcg + 1) * 256)
                    evv = p.s(lambda e, o=dst, i=pbx: e.copy(o, i), waits=[mmev])
                    self.read_bank(bk, evv)
                    self.ws_war[par] = mmev
            evq = self.emit_proj(self.io[wq], 16, yrhs, nq // 512,
                                 lambda fo, tt: self.qb(fo, tt * 512, (tt + 1) * 512),
                                 waits=[ylast], scale=(1.0 if L == 0 else SCALE))

            evrope_q = evq
            evrope_k = evk
            if L == 0:
                evrope_q = self.emit_l2norm(True, nq, Q0, evq)
                evrope_k = self.emit_l2norm(False, nq, Q0, evk, halo=halo)
            evrope_q = self.emit_rope(L, True, nq, Q0, evrope_q, halo=0)
            evrope_k = self.emit_rope(L, False, nq, Q0, evrope_k, halo=halo)

            evob = self.emit_attn_core(L, nq, Q0, B0, halo, nkt, rep,
                                       [evrope_q, evrope_k, evv] + cpev,
                                       proj_done=[evq, evk, evv])
            self.kv_war = evob

            self.emit_out_proj(self.io[wo], nq, Q0, xap, cast, dap, fin,
                               ob_ready=evob)
            self.phase_end = [e for e in self.ot_war if e]
        return self.phase_end

    def emit_l2norm(self, qside, nq, Q0, ready, halo=0):
        p = self.p
        nch = 16 if qside else self.kvch
        STAT = self.pbank[2][0:16, 0:512]
        rnoff = 0 if qside else 512
        RN = self.srow[0:16, rnoff:rnoff + 512]
        evn = ready
        for tt in range(nq // 512):
            mmev = None
            for c in range(nch):
                sl = (self.qb(c, tt * 512, (tt + 1) * 512) if qside else
                      self.kb(c, halo + tt * 512, halo + (tt + 1) * 512))
                sqs = self.sq[:, (c % 2) * 512:(c % 2 + 1) * 512]
                evq = p.s(lambda e, o=sqs, i=sl: e.activation(o, i, AF.Square),
                          waits=[ready, self.sq_war[c % 2]])
                mmev = self.mm(STAT, self.ohc[:, c * 16:(c + 1) * 16], sqs,
                               [2] if c == 0 else [], waits=[evq],
                               start=(c == 0), stop=(c == nch - 1))
                self.sq_war[c % 2] = mmev
            e1 = p.s(lambda e, o=RN, i=STAT: e.activation(o, i, AF.Ln, bias=1e-38),
                     waits=[mmev] + list(self.rn_war))
            self.read_bank(2, e1)
            bias = math.log(SCALE) if qside else 0.0
            mv = 1e12 * SCALE if qside else 1e12
            e2 = p.s(lambda e, o=RN, i=RN, b=bias:
                     e.activation(o, i, AF.Exp, scale=-0.5, bias=b), waits=[])
            e3 = p.v(lambda e, o=RN, i=RN, m=mv: e.tensor_scalar_min(o, i, m),
                     waits=[e2])
            evn = None
            bc = None
            for c in range(nch):
                bk = 6 + (c % 2)
                bc = self.mm(self.pbank[bk], self.ohr[0:16, c * 128:(c + 1) * 128],
                             RN, [bk], waits=[e3])
                sl = (self.qb(c, tt * 512, (tt + 1) * 512) if qside else
                      self.kb(c, halo + tt * 512, halo + (tt + 1) * 512))
                evn = p.v(lambda e, o=sl, a=self.pbank[bk], b=sl:
                          e.tensor_mul(o, a, b), waits=[bc])
                self.read_bank(bk, evn)
            self.rn_war = [bc]
        return evn

    def emit_rope(self, L, qside, nq, Q0, ready, halo=0):
        p = self.p
        nch = 16 if qside else self.kvch
        evn = ready
        for tt in range(nq // 512):
            ta = Q0 + tt * 512
            cts = self.ct[:, ta:ta + 512]
            sts = self.st[:, ta:ta + 512]
            for c in range(nch):
                sl = (self.qb(c, tt * 512, (tt + 1) * 512) if qside else
                      self.kb(c, halo + tt * 512, halo + (tt + 1) * 512))
                rt = self.rtmp[:, (c % 2) * 512:(c % 2 + 1) * 512]
                perm = self.cp["c_psw0" if L == 0 else "c_psw"]
                bk = 4
                sw = self.mm(self.pbank[bk], perm, sl, [bk], waits=[ready])
                ev = p.v(lambda e, o=rt, a=self.pbank[bk], b=sts:
                         e.tensor_mul(o, a, b), waits=[sw])
                self.read_bank(bk, ev)
                p.v(lambda e, o=sl, a=sl, b=cts: e.tensor_mul(o, a, b), waits=[])
                evn = p.v(lambda e, o=sl, a=sl, b=rt: e.tensor_add(o, a, b),
                          waits=[])
        return evn

    def emit_attn_core(self, L, nq, Q0, B0, halo, nkt, rep, ready, proj_done):
        p = self.p
        hkt = halo // 128
        nqt = nq // 128
        evob = None
        first_ob = list(proj_done)
        for h in range(NH):
            kc_h = h // rep if L == 0 else h
            strip_ev = {}
            po_mm = None
            su_mm = None
            av_first = True
            for kt in range(nkt):
                K0 = B0 + kt * 128
                c_lo = max(K0, Q0) - K0
                c_hi = min(K0 + 640, Q0 + nq) - K0
                par = kt % 2
                SPA = self.pbank[par]
                SPB = self.pbank[2 + par][:, 0:128]
                ktile = self.kb(kc_h, kt * 128, (kt + 1) * 128)
                a0, a1 = c_lo, min(c_hi, 512)
                evm = []
                if a1 > a0:
                    m1 = self.mm(SPA[:, a0:a1], ktile,
                                 self.qb(h, K0 + a0 - Q0, K0 + a1 - Q0),
                                 [par], waits=ready)
                    e1 = p.v(lambda e, o=SPA[:, a0:a1], i=self.mband[:, a0:a1]:
                             e.tensor_add(o, o, i), waits=[m1])
                    evm.append((SPA, a0, a1, e1))
                if c_hi > 512:
                    b0, b1 = 512, c_hi
                    m2 = self.mm(SPB[:, 0:b1 - b0], ktile,
                                 self.qb(h, K0 + b0 - Q0, K0 + b1 - Q0),
                                 [2 + par], waits=ready)
                    e2 = p.v(lambda e, o=SPB[:, 0:b1 - b0],
                             i=self.mband[:, b0:b1]:
                             e.tensor_add(o, o, i), waits=[m2])
                    evm.append((SPB, b0, b1, e2))
                rw = self.ring_war[kt % 6]
                eexp = None
                for (src, x0, x1, ev) in evm:
                    off = 0 if src is SPA else 512
                    eexp = p.s(lambda e, o=self.ringst(kt, x0, x1),
                               i=src[:, x0 - off:x1 - off]:
                               e.activation(o, i, AF.Exp),
                               waits=[ev] + ([rw] if rw else []))
                    bkid = par if src is SPA else 2 + par
                    self.read_bank(bkid, eexp)
                strip_ev[kt] = eexp
                if K0 >= Q0:
                    qt = (K0 - Q0) // 128
                    q4 = qt % 4
                    kts = list(range(max(0, kt - 4), kt + 1))
                    for i2, kt2 in enumerate(kts):
                        K02 = B0 + kt2 * 128
                        c0 = K0 - K02
                        rsl = self.ringst(kt2, c0, c0 + 128)
                        vt = self.vb(kt2, kc_h * 128 if L == 0 else h * 128,
                                     (kc_h * 128 if L == 0 else h * 128) + 128)
                        st_first = av_first and i2 == 0
                        po_mm = self.mm(self.pbank[4][:, q4 * 128:(q4 + 1) * 128],
                                        vt, rsl, [4] if st_first else [],
                                        waits=[strip_ev[kt2]],
                                        start=(i2 == 0), stop=(i2 == len(kts) - 1))
                        su_mm = self.mm(self.pbank[5][0:1, q4 * 128:(q4 + 1) * 128],
                                        self.cp["c_ones"], rsl,
                                        [5] if st_first else [],
                                        waits=[], start=(i2 == 0),
                                        stop=(i2 == len(kts) - 1))
                        self.ring_war[kt2 % 6] = su_mm
                    av_first = False
                    if q4 == 3 or qt == nqt - 1:
                        ng = (q4 + 1) * 128
                        qg0 = (qt - q4) * 128
                        ru = self.srow[0:1, 0:512]
                        evru = p.v(lambda e, o=ru[:, 0:ng],
                                   i=self.pbank[5][0:1, 0:ng]:
                                   e.reciprocal(o, i),
                                   waits=[su_mm] + self.ru_war)
                        self.read_bank(5, evru)
                        bc = self.mm(self.pbank[6][:, 0:ng], self.cp["c_onesrow"],
                                     ru[:, 0:ng], [6], waits=[evru])
                        self.ru_war = [bc]
                        dst = self.obuf(h, qg0, qg0 + ng)
                        epo = p.s(lambda e, o=dst, i=self.pbank[4][:, 0:ng]:
                                  e.copy(o, i), waits=[po_mm] + first_ob)
                        first_ob = []
                        self.read_bank(4, epo)
                        evob = p.v(lambda e, o=dst, a=self.pbank[6][:, 0:ng], b=dst:
                                   e.tensor_mul(o, a, b), waits=[bc, epo])
                        self.read_bank(6, evob)
                        av_first = True
        return evob

    def emit_out_proj(self, wap, nq, Q0, xap, cast, dap, fin, ob_ready):
        p = self.p
        evout = None
        for fop in range(8):
            par = fop % 2
            wev = self.load_wpair(wap, 2048, fop, par, waits=[self.ws_war[par]])
            mml = None
            for sub in range(2):
              fo = 2 * fop + sub
              for tt in range(nq // 512):
                bk = (fo * (nq // 512) + tt) % 2
                pbx = self.pbank[bk]
                mmev = None
                for kc in range(16):
                    mmev = self.mm(pbx, self.ws_tile2(par, kc, sub),
                                   self.obuf(kc, tt * 512, (tt + 1) * 512),
                                   [bk] if kc == 0 else [],
                                   waits=[wev, ob_ready] if kc == 0 else [],
                                   start=(kc == 0), stop=(kc == 15))
                self.ws_war[par] = mmev
                g = tt % 2
                if fo % 4 == 0:
                    self.xgev[g] = self.load_x4(xap, cast, fo, Q0 + tt * 512, g,
                                                waits=[self.xt_war[g]])
                dx = self.xsl(g, fo % 4)
                evd = self.xgev[g]
                if fin:
                    otile = self.of[:, (tt % 2) * 512:(tt % 2) * 512 + 512]
                else:
                    otile = self.rtmp[:, (tt % 2) * 512:(tt % 2) * 512 + 512]
                evo = p.v(lambda e, o=otile, a=pbx, b=dx: e.tensor_add(o, a, b),
                          waits=[mmev, evd, self.ot_war[tt % 2]])
                self.read_bank(bk, evo)
                self.xt_war[g] = evo
                dd = dap.rearrange("(kc p) t -> kc p t", p=128)[
                    fo, :, Q0 + tt * 512:Q0 + (tt + 1) * 512]
                evout = p.dma_s(dd, otile, waits=[evo], sem=f"st{tt % 2}")
                self.ot_war[tt % 2] = evout
        return evout

    def emit_wprep(self, wap, tern, rows, cols, sidx):
        p = self.p
        PR = self.pbank[2][0:1, 0:512]
        wv = wap.rearrange("(r p) (c f) -> p r c f", p=128, f=2048)
        tv = tern.rearrange("(r p) (c f) -> p r c f", p=128, f=2048)
        nr, ncc = rows // 128, cols // 2048
        ntile = nr * ncc
        wfs = [self.fscr[:, 0:2048], self.fscr[:, 2048:4096]]
        t1s = [self.fscr[:, 4096:6144], self.fscr[:, 6144:8192]]
        tbs = [self.tb0, self.tb1]
        war = [self.sq_war[0], self.sq_war[1]]
        mmev = None
        i = 0
        for r in range(nr):
            for cc in range(ncc):
                sl = i % 2
                evd = p.dma_s(wfs[sl], wv[:, r, cc], waits=[war[sl]],
                              sem=f"wf{sl}")
                eva = p.s(lambda e, o=tbs[sl], i_=wfs[sl]:
                          e.activation(o, i_, AF.Abs), waits=[evd])
                for hh in range(4):
                    mmev = self.mm(PR, self.cp["c_ones"],
                                   tbs[sl][:, hh * 512:(hh + 1) * 512],
                                   [2] if i == 0 and hh == 0 else [],
                                   waits=[eva], start=(i == 0 and hh == 0),
                                   stop=(i == ntile - 1 and hh == 3))
                war[sl] = eva
                self.sq_war[sl] = mmev
                i += 1
        tot = self.scal[0:1, sidx:sidx + 1]
        ev1 = p.v(lambda e, o=tot, i=PR: e.tensor_reduce(o, i, AX.X, ALU.add),
                  waits=[mmev])
        self.read_bank(2, ev1)
        N = float(rows * cols)
        sval = self.scal[0:1, sidx + 1:sidx + 2]
        ev2 = p.v(lambda e, o=sval, i=tot, n=1.0 / N:
                  e.tensor_scalar(o, i, n, EPSQ, ALU.mult, ALU.max),
                  waits=[hard(ev1)])
        rs = self.scal[0:1, sidx + 2:sidx + 3]
        ev3 = p.v(lambda e, o=rs, i=sval: e.reciprocal(o, i),
                  waits=[hard(ev2)])
        bc = self.mm(self.pbank[6][:, 0:1], self.cp["c_onesrow"], rs, [6],
                     waits=[hard(ev3)])
        rscol = self.scal[:, sidx + 3:sidx + 4]
        ev4 = p.s(lambda e, o=rscol, i=self.pbank[6][:, 0:1]: e.copy(o, i),
                  waits=[bc])
        self.read_bank(6, ev4)
        evw = None
        wfw = [war[0], war[1]]
        tbw = [self.sq_war[0], self.sq_war[1]]
        t1w = [None, None]
        i = 0
        for r in range(nr):
            for cc in range(ncc):
                sl = i % 2
                evd = p.dma_s(wfs[sl], wv[:, r, cc], waits=[wfw[sl]],
                              sem=f"wf{sl}")
                ea = p.s(lambda e, o=t1s[sl], i_=wfs[sl], sc=rscol:
                         e.activation(o, i_, AF.Copy, scale=sc, bias=MAGIC),
                         waits=[evd, hard(ev4), t1w[sl]])
                wfw[sl] = ea
                eb = p.v(lambda e, o=tbs[sl], i_=t1s[sl]:
                         e.tensor_scalar(o, i_, -MAGIC, 1.0, ALU.add, ALU.min),
                         waits=[ea, tbw[sl]])
                t1w[sl] = eb
                ec = p.v(lambda e, o=tbs[sl], i_=tbs[sl]:
                         e.tensor_scalar_max(o, i_, -1.0), waits=[])
                evw = p.dma_s(tv[:, r, cc], tbs[sl], waits=[ec], sem="tern")
                tbw[sl] = evw
                i += 1
        self.sq_war = [evw, evw]
        self.mu_war = evw
        return evw

    def emit_butterfly(self, tiles, waits):
        p = self.p
        n = len(tiles)
        ev = None
        s = 1
        first = True
        while s < n:
            for base in range(0, n, 2 * s):
                for k in range(s):
                    a = tiles[base + k]
                    b = tiles[base + k + s]
                    ev = p.v(lambda e, o=a, x=a, y=b: e.tensor_add(o, x, y),
                             waits=waits if first else [])
                    first = False
                    ev = p.v(lambda e, o=b, x=b, y=a:
                             e.scalar_tensor_tensor(o, x, -2.0, y,
                                                    ALU.mult, ALU.add), waits=[])
            s *= 2
        return ev

    def emit_actquant(self, nch, out_fn, eps_const, eps_row, waits, w=512):
        p = self.p
        M = self.rb[:, 0:512][:, 0:w]
        fv = self.fscr[:, 0:nch * w].rearrange("p (c w) -> p w c", w=w)
        ev = p.v(lambda e, o=M, i=fv:
                 e.tensor_reduce(o, i, AX.X, ALU.max, apply_absolute_value=True),
                 waits=list(waits) + ([self.rb_war[0]] if self.rb_war[0] else []))
        PR = self.pbank[3][0:1, 0:w]
        evtr = None
        for q in range(w // 128):
            tw = [ev] + ([self.bwar[2]] if self.bwar[2] else [])
            self.bwar[2] = None
            tev = p.t(lambda e, o=self.pbank[2][:, 0:128],
                      i=M[:, q * 128:(q + 1) * 128], idn=self.cp["c_id"]:
                      e.transpose(o, i, idn), waits=tw)
            col = self.srow[:, 4 + q:5 + q]
            rev = p.v(lambda e, o=col, i=self.pbank[2][:, 0:128]:
                      e.tensor_reduce(o, i, AX.X, ALU.max), waits=[tev])
            self.read_bank(2, rev)
            tw2 = [rev] + ([self.bwar[3]] if self.bwar[3] else [])
            self.bwar[3] = None
            evtr = p.t(lambda e, o=PR[:, q * 128:(q + 1) * 128], i=col,
                       idn=self.cp["c_id"]: e.transpose(o, i, idn), waits=tw2)
        self.rb_war[0] = None
        am = self.srow[0:1, 0:w]
        aw = [evtr] + ([self.am_war] if self.am_war else [])
        if eps_row is None:
            evam = p.v(lambda e, o=am, i=PR, cst=eps_const:
                       e.tensor_scalar_max(o, i, cst), waits=aw)
        else:
            evam = p.v(lambda e, o=am, a=PR, b=eps_row:
                       e.tensor_tensor(o, a, b, ALU.max), waits=aw)
        self.read_bank(3, evam)
        sc = self.srow[0:1, 512:512 + w]
        p.v(lambda e, o=sc, i=am: e.reciprocal(o, i), waits=[])
        ev6 = p.v(lambda e, o=sc, i=sc: e.tensor_scalar_mul(o, i, 127.0),
                  waits=[])
        bc = self.mm(self.pbank[6][:, 0:w], self.cp["c_onesrow"], sc, [6],
                     waits=[ev6])
        SCB = self.rb[:, 512:512 + w]
        sw = [bc] + ([self.rb_war[1]] if self.rb_war[1] else [])
        evb = p.s(lambda e, o=SCB, i=self.pbank[6][:, 0:w]: e.copy(o, i),
                  waits=sw)
        self.read_bank(6, evb)
        self.am_war = evb
        evq = None
        for c in range(nch):
            z = self.fscr[:, c * w:(c + 1) * w]
            p.v(lambda e, o=z, a=z, b=SCB: e.tensor_mul(o, a, b), waits=[evb])
            evq = p.v(lambda e, o=out_fn(c), i=z:
                      e.tensor_scalar(o, i, MAGIC, -MAGIC, ALU.add, ALU.add),
                      waits=[])
        self.rb_war[1] = evq
        self.rb_war[0] = evq
        return evq, am

    def emit_ffn0(self, si):
        p = self.p
        self.nq = 1024
        xap, cast = self.xsrc(si)
        dap, fin = self.xdst(si)
        p.fence(list(self.phase_end))
        evw_up = self.emit_wprep(self.io["l0_wupT"], self.ternup, 2048, 4096, 0)
        evw_dn = self.emit_wprep(self.io["l0_wdnT"], self.terndn, 4096, 2048, 4)
        EPS1 = EPSQ / C2
        for hf in range(2):
            T0 = hf * 1024
            evy = None
            for tt in range(2):
                evy = self.emit_ln(xap, cast, T0 + tt * 512, tt * 512)
            tiles = [self.yb(c, 0, 1024) for c in range(16)]
            evb = self.emit_butterfly(tiles, [evy])
            lamrow = self.srow[0:1, 1024:2048]
            evq2 = None
            evl = None
            for sub in range(2):
                evf = None
                for c in range(16):
                    bk = c % 2
                    m1 = self.mm(self.pbank[bk], self.cp["c_h128"],
                                 self.yb(c, sub * 512, (sub + 1) * 512),
                                 [bk], waits=[evb, evq2])
                    evf = p.s(lambda e, o=self.fscr[:, c * 512:(c + 1) * 512],
                              i=self.pbank[bk]: e.copy(o, i), waits=[m1])
                    self.read_bank(bk, evf)
                def yqout(c, s=sub):
                    return self.qb(c, s * 512, (s + 1) * 512)
                evq, am = self.emit_actquant(16, yqout, EPS1, None, [evf])
                lm = lamrow[:, sub * 512:(sub + 1) * 512]
                evl = p.v(lambda e, o=lm, i=am, s1=self.scal[0:1, 1:2]:
                          e.tensor_scalar(o, i, s1, C2 / 127.0, ALU.mult,
                                          ALU.mult), waits=[])
                evq2 = evq
            evu = None
            for fop in range(16):
                par = fop % 2
                wev = self.load_wpair(self.ternup, 2048, fop, par,
                                      waits=[self.ws_war[par], evw_up],
                                      cast=False)
                for sub in range(2):
                  fo = 2 * fop + sub
                  for tt in range(2):
                    bk = (fo * 2 + tt) % 2
                    pbx = self.pbank[bk]
                    mmev = None
                    for kc in range(16):
                        mmev = self.mm(pbx, self.ws_tile2(par, kc, sub),
                                       self.qb(kc, tt * 512, (tt + 1) * 512),
                                       [bk] if kc == 0 else [],
                                       waits=[wev, evq2] if kc == 0 else [],
                                       start=(kc == 0), stop=(kc == 15))
                    dst = self.arC[:, fo * 1024 + tt * 512:
                                   fo * 1024 + (tt + 1) * 512]
                    evu = p.s(lambda e, o=dst, i=pbx: e.activation(o, i, AF.Relu),
                              waits=[mmev])
                    self.read_bank(bk, evu)
                    self.ws_war[par] = mmev
            evsq = None
            for fo in range(32):
                hb = self.arC[:, fo * 1024:(fo + 1) * 1024]
                evsq = p.v(lambda e, o=hb, a=hb, b=hb: e.tensor_mul(o, a, b),
                           waits=[evu])
            tiles = [self.arC[:, c * 1024:(c + 1) * 1024] for c in range(32)]
            evb4 = self.emit_butterfly(tiles, [evsq])
            evq4 = None
            e5 = None
            mu_row = self.of[0:1, 0:1024]
            for sub in range(4):
                lm = lamrow[:, sub * 256:(sub + 1) * 256]
                l2r = self.rb[0:1, 768:1024][:, 0:256]
                p.v(lambda e, o=l2r, a=lm, b=lm: e.tensor_mul(o, a, b),
                    waits=[evl] + ([self.rb_war[1]] if self.rb_war[1] else []))
                ep2 = self.rb[0:1, 512:768][:, 0:256]
                p.v(lambda e, o=ep2, i=l2r: e.reciprocal(o, i), waits=[])
                e3 = p.v(lambda e, o=ep2, i=ep2:
                         e.tensor_scalar_mul(o, i, EPSQ / C4), waits=[])
                evf = None
                for c in range(32):
                    bk = c % 2
                    m1 = self.mm(self.pbank[bk][:, 0:256], self.cp["c_h128"],
                                 self.arC[:, c * 1024 + sub * 256:
                                          c * 1024 + (sub + 1) * 256],
                                 [bk], waits=[evb4, evq4])
                    evf = p.s(lambda e, o=self.fscr[:, c * 256:(c + 1) * 256],
                              i=self.pbank[bk][:, 0:256]: e.copy(o, i),
                              waits=[m1])
                    self.read_bank(bk, evf)
                def hqout(c, s=sub):
                    return self.arC[:, c * 1024 + s * 256:
                                    c * 1024 + (s + 1) * 256]
                evq4, am2 = self.emit_actquant(32, hqout, None, ep2,
                                               [evf, e3], w=256)
                mu = mu_row[:, sub * 256:(sub + 1) * 256]
                p.v(lambda e, o=mu, a=am2, b=l2r: e.tensor_mul(o, a, b),
                    waits=[self.mu_war] if self.mu_war else [])
                e5 = p.v(lambda e, o=mu, i=mu, s2=self.scal[0:1, 5:6]:
                         e.tensor_scalar(o, i, s2, C4 / 127.0, ALU.mult,
                                         ALU.mult), waits=[])
            evmu = []
            for tt in range(2):
                bw = [e5] + [x for x in self.rb_war if x]
                bc = self.mm(self.pbank[6], self.cp["c_onesrow"],
                             mu_row[:, tt * 512:(tt + 1) * 512], [6], waits=bw)
                evx = p.s(lambda e, o=self.rb[:, tt * 512:(tt + 1) * 512],
                          i=self.pbank[6]: e.copy(o, i), waits=[bc])
                self.read_bank(6, evx)
                evmu.append(evx)
            self.rb_war = [evmu[0], evmu[1]]
            self.mu_war = evmu[1]
            evout = None
            for fo in range(16):
                par = fo % 2
                wev = self.load_wchunk(self.terndn, 4096, fo, par,
                                       waits=[self.ws_war[par], evw_dn],
                                       cast=False)
                for tt in range(2):
                    bk = (fo * 2 + tt) % 2
                    pbx = self.pbank[bk]
                    mmev = None
                    for kc in range(32):
                        mmev = self.mm(pbx, self.ws_tile(par, kc),
                                       self.arC[:, kc * 1024 + tt * 512:
                                                kc * 1024 + (tt + 1) * 512],
                                       [bk] if kc == 0 else [],
                                       waits=[wev, evq4] if kc == 0 else [],
                                       start=(kc == 0), stop=(kc == 31))
                    t1 = self.rtmp[:, tt * 512:(tt + 1) * 512]
                    ev1 = p.v(lambda e, o=t1, a=pbx,
                              b=self.rb[:, tt * 512:(tt + 1) * 512]:
                              e.tensor_mul(o, a, b),
                              waits=[mmev, evmu[tt]] +
                                    ([self.t1_war] if self.t1_war else []))
                    self.read_bank(bk, ev1)
                    g = tt % 2
                    if fo % 4 == 0:
                        self.xgev[g] = self.load_x4(xap, cast, fo,
                                                    T0 + tt * 512, g,
                                                    waits=[self.xt_war[g]])
                    dx = self.xsl(g, fo % 4)
                    evd = self.xgev[g]
                    if fin:
                        otile = self.of[:, tt * 512:(tt + 1) * 512]
                    else:
                        otile = self.sq[:, tt * 512:(tt + 1) * 512]
                    evo = p.v(lambda e, o=otile, a=t1, b=dx: e.tensor_add(o, a, b),
                              waits=[evd] +
                                    ([self.ot_war[tt]] if self.ot_war[tt] else []) +
                                    ([self.sq_war[tt]] if self.sq_war[tt] else []))
                    self.xt_war[g] = evo
                    self.t1_war = evo
                    dd = dap.rearrange("(kc p) t -> kc p t", p=128)[
                        fo, :, T0 + tt * 512:T0 + (tt + 1) * 512]
                    evout = p.dma_s(dd, otile, waits=[evo], sem=f"st{tt}")
                    self.ot_war[tt] = evout
                    if not fin:
                        self.sq_war[tt] = evout
            self.phase_end = [e for e in self.ot_war if e]
            self.row_war = self.row_war + [evl]
        return self.phase_end

    def emit_ffn1(self, si):
        p = self.p
        self.nq = 1024
        xap, cast = self.xsrc(si)
        dap, fin = self.xdst(si)
        p.fence(list(self.phase_end))
        evs = p.dma_s(self.scal[:, 16:32], self.io["l1_fscale"], sem="const")
        for hf in range(2):
            T0 = hf * 1024
            evy = None
            for tt in range(2):
                evy = self.emit_ln(xap, cast, T0 + tt * 512, tt * 512)
            evo = None
            for fop in range(8):
                par = fop % 2
                wev = self.load_wpair(self.io["l1_fw"], 2048, fop, par,
                                      waits=[self.ws_war[par]])
                for sub in range(2):
                  fo = 2 * fop + sub
                  for tt in range(2):
                    bk = (fo * 2 + tt) % 2
                    pbx = self.pbank[bk]
                    mmev = None
                    for kc in range(16):
                        mmev = self.mm(pbx, self.ws_tile2(par, kc, sub),
                                       self.yb(kc, tt * 512, (tt + 1) * 512),
                                       [bk] if kc == 0 else [],
                                       waits=[wev, evy, evs] if kc == 0 else [],
                                       start=(kc == 0), stop=(kc == 15))
                    dst = self.qb(fo, tt * 512, (tt + 1) * 512)
                    evo = p.s(lambda e, o=dst, i=pbx, sc=self.scal[:, 16 + fo:17 + fo]:
                              e.activation(o, i, AF.Copy, scale=sc), waits=[mmev])
                    self.read_bank(bk, evo)
                    self.ws_war[par] = mmev
            evout = None
            for tt in range(2):
                PR = self.pbank[3][0:1, 0:512]
                mmev = None
                for fo in range(16):
                    sqs = self.sq[:, (fo % 2) * 512:(fo % 2 + 1) * 512]
                    evsq = p.s(lambda e, o=sqs,
                               i=self.qb(fo, tt * 512, (tt + 1) * 512):
                               e.activation(o, i, AF.Square),
                               waits=[evo, self.sq_war[fo % 2]])
                    mmev = self.mm(PR, self.cp["c_ones"], sqs,
                                   [3] if fo == 0 else [], waits=[evsq],
                                   start=(fo == 0), stop=(fo == 15))
                    self.sq_war[fo % 2] = mmev
                r1 = self.srow[0:1, 0:512]
                r2 = self.srow[0:1, 512:1024]
                r3 = self.srow[0:1, 1024:1536]
                r4 = self.srow[0:1, 1536:2048]
                e1 = p.s(lambda e, o=r1, i=PR: e.activation(o, i, AF.Ln,
                                                            bias=1e-38),
                         waits=[mmev] + self.row_war)
                self.read_bank(3, e1)
                e2 = p.s(lambda e, o=r1, i=r1: e.activation(o, i, AF.Exp,
                                                            scale=0.5), waits=[])
                e3 = p.v(lambda e, o=r2, i=r1: e.tensor_scalar_max(o, i, 1e-7),
                         waits=[e2])
                e4 = p.s(lambda e, o=r3, i=r2: e.activation(o, i, AF.Exp,
                                                            scale=-2.0),
                         waits=[e3])
                e5 = p.v(lambda e, o=r4, i=r3:
                         e.tensor_scalar(o, i, -1.0, 1.0, ALU.mult, ALU.add),
                         waits=[e4])
                e6 = p.v(lambda e, o=r3, i=r3: e.tensor_scalar_add(o, i, 1.0),
                         waits=[e4])
                e7 = p.v(lambda e, o=r3, i=r3: e.reciprocal(o, i), waits=[])
                e8 = p.v(lambda e, o=r4, a=r4, b=r3: e.tensor_mul(o, a, b),
                         waits=[])
                e9 = p.v(lambda e, o=r3, i=r2: e.tensor_scalar_add(o, i, 1e-7),
                         waits=[])
                e10 = p.v(lambda e, o=r3, i=r3: e.reciprocal(o, i), waits=[])
                e11 = p.v(lambda e, o=r4, a=r4, b=r3: e.tensor_mul(o, a, b),
                          waits=[])
                e12 = p.v(lambda e, o=r1, a=r1, b=r4: e.tensor_mul(o, a, b),
                          waits=[])
                e13 = p.v(lambda e, o=r1, i=r1: e.tensor_scalar_max(o, i, 1e-7),
                          waits=[])
                e14 = p.v(lambda e, o=r1, i=r1: e.reciprocal(o, i), waits=[])
                e15 = p.v(lambda e, o=r1, i=r1:
                          e.tensor_scalar(o, i, 0.99, 1.0, ALU.mult, ALU.min),
                          waits=[])
                e16 = p.v(lambda e, o=r4, a=r4, b=r1: e.tensor_mul(o, a, b),
                          waits=[])
                self.row_war = [e16]
                bc = self.mm(self.pbank[6], self.cp["c_onesrow"], r4, [6],
                             waits=[e16])
                RB = self.rb[:, tt * 512:(tt + 1) * 512]
                evb = p.s(lambda e, o=RB, i=self.pbank[6]: e.copy(o, i),
                          waits=[bc, self.rb_war[tt]])
                self.read_bank(6, evb)
                self.row_war = [bc]
                for fo in range(16):
                    t1 = self.rtmp[:, 0:512]
                    ev1 = p.v(lambda e, o=t1,
                              a=self.qb(fo, tt * 512, (tt + 1) * 512), b=RB:
                              e.tensor_mul(o, a, b),
                              waits=[evb] + ([self.t1_war] if self.t1_war else []))
                    g = tt % 2
                    if fo % 4 == 0:
                        self.xgev[g] = self.load_x4(xap, cast, fo,
                                                    T0 + tt * 512, g,
                                                    waits=[self.xt_war[g]])
                    dx = self.xsl(g, fo % 4)
                    evd = self.xgev[g]
                    if fin:
                        otile = self.of[:, (fo % 2) * 512:(fo % 2 + 1) * 512]
                    else:
                        otile = self.sq[:, (fo % 2) * 512:(fo % 2 + 1) * 512]
                    evo2 = p.v(lambda e, o=otile, a=t1, b=dx:
                               e.tensor_add(o, a, b),
                               waits=[evd] +
                                     ([self.ot_war[fo % 2]] if self.ot_war[fo % 2] else []) +
                                     ([self.sq_war[fo % 2]] if self.sq_war[fo % 2] else []))
                    self.xt_war[g] = evo2
                    self.t1_war = evo2
                    dd = dap.rearrange("(kc p) t -> kc p t", p=128)[
                        fo, :, T0 + tt * 512:T0 + (tt + 1) * 512]
                    evout = p.dma_s(dd, otile, waits=[evo2], sem=f"st{fo % 2}")
                    self.ot_war[fo % 2] = evout
                    if not fin:
                        self.sq_war[fo % 2] = evout
                self.rb_war[tt] = ev1
            self.phase_end = [e for e in self.ot_war if e]
        return self.phase_end



def build_kb(sublayers=(0, 1, 2, 3)):
    nc = bass.Bass("TRN2", target_bir_lowering=False)
    kb = KB(nc, sublayers)
    for si in sorted(sublayers):
        if si == 0:
            kb.emit_attn(0, 0)
        elif si == 1:
            kb.emit_ffn0(1)
        elif si == 2:
            kb.emit_attn(2, 1)
        elif si == 3:
            kb.emit_ffn1(3)
    return nc, kb


def finalize(nc, kb, extra_evs=()):
    kb.p.fence(list(kb.phase_end) + list(extra_evs))
    with nc.Block() as block:
        kb.p.replay(block)
    return nc


def build_nc(sublayers=(0, 1, 2, 3)):
    nc, kb = build_kb(sublayers)
    finalize(nc, kb)
    return nc, kb


def _prep_inputs(inputs):
    f = np.float32
    def cT(a):
        return np.ascontiguousarray(np.asarray(a, dtype=f).T)
    def c(a):
        return np.ascontiguousarray(np.asarray(a, dtype=f))
    com = dict(
        l0_wq=c(inputs["L0_attn_wq"]), l0_wk=c(inputs["L0_attn_wk"]),
        l0_wv=c(inputs["L0_attn_wv"]), l0_wo=c(inputs["L0_attn_wo"]),
        l0_wupT=cT(inputs["L0_ffn_w_up"]), l0_wdnT=cT(inputs["L0_ffn_w_down"]),
        l1_wqT=cT(inputs["L1_attn_Wq"]), l1_wkT=cT(inputs["L1_attn_Wk"]),
        l1_wvT=cT(inputs["L1_attn_Wv"]), l1_woT=cT(inputs["L1_attn_Wo"]),
        l1_fw=c(inputs["L1_ffn_weight"]),
        l1_fscale=np.ascontiguousarray(
            np.asarray(inputs["L1_ffn_scale"], dtype=f).reshape(16, 128).T),
    )
    com.update(host_constants())
    x = np.asarray(inputs["L0_attn_x"], dtype=f)
    maps = []
    for b in range(8):
        m = dict(com)
        m["xT"] = np.ascontiguousarray(x[b].T)
        maps.append(m)
    return maps


def _run(inputs, sublayers=(0, 1, 2, 3)):
    maps = _prep_inputs(inputs)
    nc, kb = build_nc(sublayers)
    res = run_bass_kernel_spmd(nc, maps, core_ids=list(range(8)))
    out = np.stack([np.asarray(r["outT"]).T for r in res.results])
    return np.ascontiguousarray(out)


def kernel(**inputs):
    return _run(inputs, (0, 1, 2, 3))


# version 08
# speedup vs baseline: 1.5146x; 1.0131x over previous
import math
import numpy as np
import ml_dtypes
import concourse.bass as bass
import concourse.mybir as mybir
from concourse.bass_utils import run_bass_kernel_spmd
from contextlib import ExitStack

F32 = mybir.dt.float32
BF16 = mybir.dt.bfloat16
AF = mybir.ActivationFunctionType
ALU = mybir.AluOpType
AX = mybir.AxisListType

D = 2048
T = 2048
NCH = 16
WIN = 512
DH = 128
NH = 16
MAGIC = 1.5 * 2.0 ** 23
SCALE = DH ** -0.5
ACT_CONSTS = [1e-5, 1e-38, -0.5, 0.5, -2.0, SCALE, MAGIC, math.log(SCALE)]
C2 = 1.0 / math.sqrt(2048.0)
C4 = 1.0 / math.sqrt(4096.0)
EPSQ = 1e-5
NEG = -1.0e30

ENGINES = ("tensor", "vector", "scalar", "gpsimd", "sync")



def _rope_tables():
    j = np.arange(64)
    invf = 10000.0 ** (-2.0 * j / 128.0)
    t = np.arange(T)
    fr = t[None, :] * invf[:, None]
    cos = np.cos(fr).astype(np.float32)
    sin = np.sin(fr).astype(np.float32)
    ct0 = np.concatenate([cos, cos], axis=0)
    st0 = np.concatenate([-sin, sin], axis=0)
    ct1 = np.zeros((128, T), np.float32)
    st1 = np.zeros((128, T), np.float32)
    ct1[0::2] = cos
    ct1[1::2] = cos
    st1[0::2] = -sin
    st1[1::2] = sin
    bf = ml_dtypes.bfloat16
    return (ct0.astype(bf), st0.astype(bf), ct1.astype(bf), st1.astype(bf))


def _hadamard(n):
    h = np.array([[1.0]])
    while h.shape[0] < n:
        h = np.block([[h, h], [h, -h]])
    return h


def host_constants():
    bf = ml_dtypes.bfloat16
    ct0, st0, ct1, st1 = _rope_tables()
    h128 = _hadamard(128).astype(bf)
    psw = np.zeros((128, 128), np.float32)
    for i in range(64):
        psw[2 * i + 1, 2 * i] = 1.0
        psw[2 * i, 2 * i + 1] = 1.0
    psw = psw.astype(bf)
    ph0 = np.zeros((128, 128), np.float32)
    for i in range(64):
        ph0[i + 64, i] = 1.0
        ph0[i, i + 64] = 1.0
    ph0 = ph0.astype(bf)
    r = np.arange(128)[:, None]
    c = np.arange(640)[None, :]
    mband = np.where((c >= r) & (c < r + WIN), 0.0, NEG).astype(np.float32)
    ohc = np.zeros((128, 16 * 16), np.float32)
    for h in range(16):
        ohc[:, 16 * h + h] = 1.0
    ohc = ohc.astype(bf)
    ohr = np.zeros((16, 16 * 128), np.float32)
    for h in range(16):
        ohr[h, 128 * h:128 * (h + 1)] = 1.0
    ones_col = np.ones((128, 1), np.float32).astype(bf)
    cmean_col = np.full((128, 1), 1.0 / 2048.0, np.float32).astype(bf)
    ones_row = np.ones((1, 128), np.float32)
    acv = np.array(ACT_CONSTS, np.float32)
    actc = np.broadcast_to(acv[None, :], (128, len(ACT_CONSTS))).copy()
    return dict(
        c_actc=actc,
        c_ct0=ct0, c_st0=st0, c_ct1=ct1, c_st1=st1,
        c_h128=h128, c_psw=psw, c_psw0=ph0, c_mband=mband,
        c_ohc=ohc, c_ohr=ohr.astype(np.float32),
        c_ones=ones_col, c_cmean=cmean_col, c_onesrow=ones_row,
        c_id=np.eye(128, dtype=np.float32),
    )



class Sem:
    __slots__ = ("name", "h", "n")

    def __init__(self, name, handle):
        self.name = name
        self.h = handle
        self.n = 0


class Ev:
    __slots__ = ("sem", "n", "eng")

    def __init__(self, sem, n, eng):
        self.sem = sem
        self.n = n
        self.eng = eng


def hard(ev):
    return Ev(ev.sem, ev.n, "hard") if ev is not None else None


class Prog:

    def __init__(self, nc):
        self.nc = nc
        self.streams = {e: [] for e in ENGINES}
        self.sems = {}
        for name in ("pe", "dv", "ac", "pl"):
            self.sems[name] = Sem(name, nc.alloc_semaphore("s_" + name))
        self.dsems = {}
        self.bank_war = {}
        self.nsteps = 0

    def go(self, eng, fn, waits=(), inc=None, inck=1):
        waits = tuple(w for w in waits if w is not None)
        for w in waits:
            assert w.n <= w.sem.n, f"wait on future event {w.sem.name}:{w.n}>{w.sem.n}"
        self.streams[eng].append((waits, fn, inc, inck))
        self.nsteps += 1
        if inc is not None:
            inc.n += inck
            return Ev(inc, inc.n, "dma" if eng in ("sync",) and inck >= 16 else eng)
        return None

    def t(self, fn, waits=(), inc=True):
        return self.go("tensor", fn, waits, self.sems["pe"] if inc else None)

    def v(self, fn, waits=(), inc=True):
        return self.go("vector", fn, waits, self.sems["dv"] if inc else None)

    def s(self, fn, waits=(), inc=True):
        return self.go("scalar", fn, waits, self.sems["ac"] if inc else None)

    def g(self, fn, waits=(), inc=True):
        return self.go("gpsimd", fn, waits, self.sems["pl"] if inc else None)

    def _dsem(self, name):
        if name not in self.dsems:
            self.dsems[name] = Sem(name, self.nc.alloc_semaphore("d_" + name))
        return self.dsems[name]

    def dma_s(self, out, in_, waits=(), sem="misc_s"):
        ev = self.go("sync", lambda e, o=out, i=in_: e.dma_start(out=o, in_=i),
                     waits, self._dsem(sem), 16)
        return Ev(ev.sem, ev.n, "dma")

    def dma_g(self, out, in_, waits=(), sem="misc_g"):
        ev = self.go("gpsimd", lambda e, o=out, i=in_: e.dma_start(out=o, in_=i),
                     waits, self._dsem(sem), 16)
        return Ev(ev.sem, ev.n, "dma")

    def fence(self, evs, engines=ENGINES):
        evs = [e for e in evs if e is not None]
        for eng in engines:
            self.go(eng, None, tuple(Ev(e.sem, e.n, "dma") for e in evs))

    def replay(self, block):
        for eng in ENGINES:
            def body(e, eng=eng):
                seen = {}
                for (waits, fn, inc, inck) in self.streams[eng]:
                    for w in waits:
                        if w.eng == eng:
                            continue
                        if seen.get(w.sem.name, -1) >= w.n:
                            continue
                        e.wait_ge(w.sem.h, w.n)
                        seen[w.sem.name] = w.n
                    if fn is None:
                        continue
                    ins = fn(e)
                    if inc is not None:
                        ins.then_inc(inc.h, inck)
            getattr(block, eng)(body)



class KB:
    def __init__(self, nc, sublayers):
        self.nc = nc
        self.sub = tuple(sublayers)
        self.p = Prog(nc)

        dt = nc.dram_tensor
        self.xT = dt("xT", [D, T], F32, kind="ExternalInput").ap()
        self.io = {}
        for nm, sh, ty in [
            ("l0_wq", [D, 2048], F32), ("l0_wk", [D, 512], F32),
            ("l0_wv", [D, 512], F32), ("l0_wo", [D, 2048], F32),
            ("l0_wupT", [D, 4096], F32), ("l0_wdnT", [4096, D], F32),
            ("l1_wqT", [D, 2048], F32), ("l1_wkT", [D, 2048], F32),
            ("l1_wvT", [D, 2048], F32), ("l1_woT", [D, 2048], F32),
            ("l1_fw", [D, 2048], F32), ("l1_fscale", [128, 16], F32),
            ("c_ct0", [128, T], BF16), ("c_st0", [128, T], BF16),
            ("c_ct1", [128, T], BF16), ("c_st1", [128, T], BF16),
            ("c_h128", [128, 128], BF16), ("c_psw", [128, 128], BF16),
            ("c_psw0", [128, 128], BF16),
            ("c_mband", [128, 640], F32), ("c_ohc", [128, 256], BF16),
            ("c_ohr", [16, 2048], F32), ("c_ones", [128, 1], BF16),
            ("c_cmean", [128, 1], BF16), ("c_onesrow", [1, 128], F32),
            ("c_id", [128, 128], F32),
            ("c_actc", [128, len(ACT_CONSTS)], F32),
        ]:
            self.io[nm] = dt(nm, sh, ty, kind="ExternalInput").ap()
        self.xres = dt("xres", [D, T], BF16, kind="Internal").ap()
        self.ternup = dt("ternup", [D, 4096], BF16, kind="Internal").ap()
        self.terndn = dt("terndn", [4096, D], BF16, kind="Internal").ap()
        self.outT = dt("outT", [D, T], F32, kind="ExternalOutput").ap()

        A = nc.alloc_sbuf_tensor
        self.arA = A("arA", [128, 16384], BF16).ap()
        self.arB = A("arB", [128, 16384], BF16).ap()
        self.arC = A("arC", [128, 32768], BF16).ap()
        ovl_t = A("ovl", [128, 20480], BF16)
        ob = ovl_t.ap()
        obf = ovl_t.bitcast(F32).ap()
        self.ring = ob[:, 0:3840]
        self.ct = ob[:, 3840:5888]
        self.st = ob[:, 5888:7936]
        self.sq = ob[:, 7936:8960]
        self.rtmp = ob[:, 8960:9984]
        self.tb0 = ob[:, 16384:18432]
        self.tb1 = ob[:, 18432:20480]
        self.mband = obf[:, 4992:5632]
        self.ohc = ob[:, 11264:11520]
        self.ohr = obf[0:16, 5760:7808]
        self.fscr = obf[:, 0:8192]
        self.wf = obf[:, 8192:9216]
        self.of = obf[:, 9216:10240]
        self.ws = A("ws", [128, 8192], BF16).ap()
        self.srow = A("srow", [128, 2048], F32).ap()
        self.rb = A("rb", [128, 1024], F32).ap()
        self.xt = A("xt", [128, 4096], BF16).ap()
        self.cp = {}
        for nm, sh, ty in [("c_h128", [128, 128], BF16), ("c_psw", [128, 128], BF16),
                           ("c_psw0", [128, 128], BF16),
                           ("c_ones", [128, 1], BF16), ("c_cmean", [128, 1], BF16),
                           ("c_onesrow", [1, 128], F32), ("c_id", [128, 128], F32)]:
            self.cp[nm] = A(nm + "_s", sh, ty).ap()
        self.scal = A("scal", [128, 32], F32).ap()
        self.actc = A("actc", [128, len(ACT_CONSTS)], F32).ap()
        self.tchs = A("tchs", [128, 2], BF16).ap()[:, 0:1]
        for j, v in enumerate(ACT_CONSTS):
            nc.const_aps.register(F32, float(v), self.actc[:, j:j + 1])
        rem = nc.sbuf_bytes_remaining
        assert rem >= 0, f"sbuf over budget by {-rem}"
        self.sbuf_rem = rem
        self.first = min(self.sub)
        self.xt_war = [None, None]
        self.sq_war = [None, None]
        self.rb_war = [None, None]
        self.row_war = []
        self.ws_war = [None, None]
        self.kv_war = None
        self.phase_end = []
        self.ring_war = [None] * 6
        self.ru_war = []
        self.rn_war = []
        self.wf_war = None
        self.wf_war2 = None
        self.am_war = None
        self.lam_war = None
        self.mu_war = None
        self.t1_war = None
        self.ot_war = [None, None]
        self.limit_passes = None
        self.xgev = {0: None, 1: None}

        self.pbank = [nc.alloc_psum_tensor(f"pbk{i}", [128, 512], F32).ap()
                      for i in range(8)]
        self.bwar = [None] * 8

        evs = [self.p.dma_s(self.cp[nm], self.io[nm], sem="const")
               for nm in self.cp]
        evs.append(self.p.dma_s(self.actc, self.io["c_actc"], sem="const"))
        self.p.fence(evs)
        self._preload_act()

    def _preload_act(self):
        idx = None
        try:
            from concourse.hw_specs import get_activation_tables
            tabs = get_activation_tables(self.nc.m.arch)
            for i, fns in enumerate(tabs.values()):
                if AF.Ln in fns and AF.Exp in fns:
                    idx = i
                    break
        except Exception:
            idx = 6
        if idx is None:
            return
        def emit(e, idx=idx):
            return e.add_instruction(mybir.InstLoadActFuncSet(
                name=self.nc.get_next_instruction_name(),
                act_func_set_id=idx, ins=[], outs=[]))
        self.p.s(emit, inc=False)

    def mm(self, out_ap, lhsT, rhs, banks, waits=(), start=True, stop=True):
        w = list(waits)
        if start:
            for b in banks:
                if self.bwar[b] is not None:
                    w.append(self.bwar[b])
                    self.bwar[b] = None
        return self.p.t(lambda e, o=out_ap, l=lhsT, r=rhs, st=start, sp=stop:
                        e.matmul(o, l, r, start=st, stop=sp), w)

    def read_bank(self, b, ev):
        self.bwar[b] = ev

    def xsrc(self, i):
        return (self.xT, True) if i == self.first else (self.xres, False)

    def xdst(self, i):
        return (self.outT, True) if i == max(self.sub) else (self.xres, False)

    def load_x4(self, src, cast, c0, t0, g, waits=()):
        sap = src.rearrange("(kc p) t -> p kc t", p=128)[:, c0:c0 + 4, t0:t0 + 512]
        dstv = self.xt[:, g * 2048:(g + 1) * 2048].rearrange(
            "p (kc t) -> p kc t", t=512)
        return (self.p.dma_g if cast else self.p.dma_s)(dstv, sap, waits,
                                                        sem=f"xt{g}")

    def xsl(self, g, j):
        return self.xt[:, g * 2048 + j * 512: g * 2048 + (j + 1) * 512]

    def load_wchunk(self, w_ap, din, fo, par, waits=(), cast=True, touch=True):
        kch = din // 128
        src = w_ap.rearrange("(kc p) f -> p kc f", p=128)[
            :, :, fo * 128:(fo + 1) * 128]
        dst = self.ws[:, par * 4096:par * 4096 + kch * 128].rearrange(
            "p (kc f) -> p kc f", f=128)
        return (self.p.dma_g if cast else self.p.dma_s)(dst, src, waits,
                                                        sem=f"ws{par}")

    def ws_tile(self, par, kc):
        return self.ws[:, par * 4096 + kc * 128: par * 4096 + (kc + 1) * 128]

    def load_wpair(self, w_ap, din, fop, par, waits=(), cast=True):
        kch = din // 128
        src = w_ap.rearrange("(kc p) f -> p kc f", p=128)[
            :, :, fop * 256:(fop + 1) * 256]
        dst = self.ws[:, par * 4096:par * 4096 + kch * 256].rearrange(
            "p (kc f) -> p kc f", f=256)
        return (self.p.dma_g if cast else self.p.dma_s)(dst, src, waits,
                                                        sem=f"ws{par}")

    def ws_tile2(self, par, kc, sub):
        o = par * 4096 + kc * 256 + sub * 128
        return self.ws[:, o:o + 128]

    def yb(self, c, a, b):
        return self.arA[:, c * self.nq + a: c * self.nq + b]

    def qb(self, c, a, b):
        return self.arB[:, c * self.nq + a: c * self.nq + b]

    def obuf(self, c, a, b):
        return self.arA[:, c * self.nq + a: c * self.nq + b]

    def kb(self, c, a, b):
        return self.arC[:, c * self.nkv + a: c * self.nkv + b]

    def vb(self, tile, a, b):
        off = self.kvch * self.nkv
        return self.arC[:, off + tile * self.fv + a: off + tile * self.fv + b]

    def ringst(self, kt, a, b):
        s = (kt % 6) * 640
        return self.ring[:, s + a: s + b]

    def emit_ln(self, xap, cast, t0, tl):
        p = self.p
        PR1 = self.pbank[2][0:1, 0:512]
        PR2 = self.pbank[3][0:1, 0:512]
        cme = self.cp["c_cmean"]
        mm1 = mm2 = None
        sq0 = self.sq[:, 0:512]
        sq1 = self.sq[:, 512:1024]
        for g in range(4):
            c0 = 4 * g
            evd = self.load_x4(xap, cast, c0, t0, g % 2,
                               waits=[self.xt_war[g % 2]])
            for j in range(4):
                xs = self.xsl(g % 2, j)
                sqs = sq0 if j % 2 == 0 else sq1
                evq = p.s(lambda e, o=sqs, i=xs: e.activation(o, i, AF.Square),
                          waits=[evd, self.sq_war[j % 2]])
                first = (g == 0 and j == 0)
                last = (g == 3 and j == 3)
                mm1 = self.mm(PR1, cme, xs, [2] if first else [],
                              waits=[evd], start=first, stop=last)
                mm2 = self.mm(PR2, cme, sqs, [3] if first else [],
                              waits=[evq], start=first, stop=last)
                self.sq_war[j % 2] = mm2
            self.xt_war[g % 2] = mm1
        mrow = self.srow[0:1, 1024:1536]
        vrow = self.srow[0:1, 1536:2048]
        evm = p.s(lambda e, o=mrow, i=PR1: e.copy(o, i),
                  waits=[mm1] + self.row_war)
        ev2 = p.v(lambda e, o=vrow, a=mrow: e.tensor_mul(o, a, a),
                  waits=[evm] + self.row_war)
        ev3 = p.v(lambda e, o=vrow, a=PR2, b=vrow: e.tensor_sub(o, a, b),
                  waits=[mm2])
        self.read_bank(2, evm)
        self.read_bank(3, ev3)
        ev4 = p.s(lambda e, o=vrow, i=vrow: e.activation(o, i, AF.Ln, bias=1e-5),
                  waits=[ev3])
        evr = p.s(lambda e, o=vrow, i=vrow: e.activation(o, i, AF.Exp, scale=-0.5),
                  waits=[ev4])
        bm = self.mm(self.pbank[6], self.cp["c_onesrow"], mrow, [6], waits=[evm])
        br = self.mm(self.pbank[7], self.cp["c_onesrow"], vrow, [7], waits=[evr])
        RBm = self.rb[:, 0:512]
        RBr = self.rb[:, 512:1024]
        evbm = p.s(lambda e, o=RBm, i=self.pbank[6]: e.copy(o, i),
                   waits=[bm, self.rb_war[0]])
        evbr = p.s(lambda e, o=RBr, i=self.pbank[7]: e.copy(o, i),
                   waits=[br, self.rb_war[1]])
        self.read_bank(6, evbm)
        self.read_bank(7, evbr)
        self.row_war = [bm, br, ev2]
        evy = None
        for g in range(4):
            c0 = 4 * g
            evd = self.load_x4(xap, cast, c0, t0, g % 2,
                               waits=[self.xt_war[g % 2]])
            for j in range(4):
                c = c0 + j
                xs = self.xsl(g % 2, j)
                rt = self.rtmp[:, (j % 2) * 512:(j % 2 + 1) * 512]
                ev5 = p.v(lambda e, o=rt, a=xs, b=RBm: e.tensor_sub(o, a, b),
                          waits=[evd, evbm])
                ys = self.yb(c, tl, tl + 512)
                evy = p.v(lambda e, o=ys, a=rt, b=RBr: e.tensor_mul(o, a, b),
                          waits=[evbr])
                self.xt_war[g % 2] = ev5
        self.rb_war[0] = evy
        self.rb_war[1] = evy
        return evy

    def emit_proj(self, wap, nfo, rhs_fn, ntt, out_fn, waits, scale=1.0):
        p = self.p
        eve = None
        for fop in range(nfo // 2):
            par = fop % 2
            wev = self.load_wpair(wap, 2048, fop, par, waits=[self.ws_war[par]])
            mmev = None
            for sub in range(2):
                fo = 2 * fop + sub
                for tt in range(ntt):
                    bk = (fo * ntt + tt) % 2
                    pbx = self.pbank[bk]
                    mmev = None
                    for kc in range(16):
                        mmev = self.mm(pbx, self.ws_tile2(par, kc, sub),
                                       rhs_fn(kc, tt),
                                       [bk] if kc == 0 else [],
                                       waits=([wev] + list(waits)) if kc == 0 else [],
                                       start=(kc == 0), stop=(kc == 15))
                    eve = p.s(lambda e, o=out_fn(fo, tt), i=pbx, sc=scale:
                              e.activation(o, i, AF.Copy, scale=sc), waits=[mmev])
                    self.read_bank(bk, eve)
            self.ws_war[par] = mmev
        return eve

    def emit_attn(self, si, L):
        p = self.p
        nq = 1024 if L == 0 else 512
        self.nq = nq
        self.nkv = nq + 512
        self.kvch = 4 if L == 0 else 16
        self.fv = 512 if L == 0 else 2048
        rep = 4 if L == 0 else 1
        names = (("l0_wq", "l0_wk", "l0_wv", "l0_wo", "c_ct0", "c_st0") if L == 0
                 else ("l1_wqT", "l1_wkT", "l1_wvT", "l1_woT", "c_ct1", "c_st1"))
        wq, wk, wv, wo, ctn, stn = names
        xap, cast = self.xsrc(si)
        dap, fin = self.xdst(si)
        npass = T // nq if self.limit_passes is None else self.limit_passes

        pw = list(self.phase_end)
        evs = [p.dma_s(self.ct, self.io[ctn], waits=pw, sem="const"),
               p.dma_s(self.st, self.io[stn], waits=pw, sem="const"),
               p.dma_s(self.mband, self.io["c_mband"], waits=pw, sem="const"),
               p.dma_s(self.ohc, self.io["c_ohc"], waits=pw, sem="const"),
               p.dma_s(self.ohr, self.io["c_ohr"], waits=pw, sem="const")]
        p.fence(evs + pw)

        for ps in range(npass):
            if ps > 0:
                p.fence(list(self.phase_end))
            Q0 = ps * nq
            B0 = max(0, Q0 - 512)
            halo = Q0 - B0
            nkt = (Q0 + nq - B0) // 128
            hkt = halo // 128

            cpev = []
            if ps > 0:
                B0p = max(0, Q0 - nq - 512)
                soff = Q0 - 512 - B0p
                for c in range(self.kvch):
                    src = self.kb(c, soff, soff + 512)
                    dst = self.kb(c, 0, 512)
                    cpev.append(p.g(lambda e, o=dst, i=src: e.tensor_copy(o, i),
                                    waits=[self.kv_war]))
                stile = soff // 128
                for tl in range(4):
                    src = self.vb(stile + tl, 0, self.fv)
                    dst = self.vb(tl, 0, self.fv)
                    cpev.append(p.g(lambda e, o=dst, i=src: e.tensor_copy(o, i),
                                    waits=[self.kv_war]))
            cpall = cpev[-1] if cpev else None

            evy = None
            for tt in range(nq // 512):
                evy = self.emit_ln(xap, cast, Q0 + tt * 512, tt * 512)
            ylast = evy

            def yrhs(kc, tt):
                return self.yb(kc, tt * 512, (tt + 1) * 512)
            evk = self.emit_proj(self.io[wk], self.kvch, yrhs, nq // 512,
                                 lambda fo, tt: self.kb(fo, halo + tt * 512,
                                                        halo + (tt + 1) * 512),
                                 waits=[ylast] + cpev)
            evv = None
            for fcg in range(self.fv // 256):
                par = fcg % 2
                src = self.io[wv].rearrange("(kc p) f -> p kc f", p=128)[
                    :, :, fcg * 256:(fcg + 1) * 256]
                dstw = self.ws[:, par * 4096:par * 4096 + 4096].rearrange(
                    "p (kc f) -> p kc f", f=256)
                wev = p.dma_g(dstw, src, waits=[self.ws_war[par]],
                              sem=f"ws{par}")
                for b in range(nq // 128):
                    bk = (fcg * (nq // 128) + b) % 2
                    pbx = self.pbank[bk][:, 0:256]
                    mmev = None
                    for kc in range(16):
                        lt = self.yb(kc, b * 128, (b + 1) * 128)
                        rt = self.ws[:, par * 4096 + kc * 256:
                                     par * 4096 + (kc + 1) * 256]
                        mmev = self.mm(pbx, lt, rt, [bk] if kc == 0 else [],
                                       waits=([wev, ylast] + cpev) if kc == 0 else [],
                                       start=(kc == 0), stop=(kc == 15))
                    dst = self.vb(hkt + b, fcg * 256, (fcg + 1) * 256)
                    evv = p.s(lambda e, o=dst, i=pbx: e.copy(o, i), waits=[mmev])
                    self.read_bank(bk, evv)
                    self.ws_war[par] = mmev
            evq = self.emit_proj(self.io[wq], 16, yrhs, nq // 512,
                                 lambda fo, tt: self.qb(fo, tt * 512, (tt + 1) * 512),
                                 waits=[ylast], scale=(1.0 if L == 0 else SCALE))

            evrope_q = evq
            evrope_k = evk
            if L == 0:
                evrope_q = self.emit_l2norm(True, nq, Q0, evq)
                evrope_k = self.emit_l2norm(False, nq, Q0, evk, halo=halo)
            evrope_q = self.emit_rope(L, True, nq, Q0, evrope_q, halo=0)
            evrope_k = self.emit_rope(L, False, nq, Q0, evrope_k, halo=halo)

            evob = self.emit_attn_core(L, nq, Q0, B0, halo, nkt, rep,
                                       [evrope_q, evrope_k, evv] + cpev,
                                       proj_done=[evq, evk, evv])
            self.kv_war = evob

            self.emit_out_proj(self.io[wo], nq, Q0, xap, cast, dap, fin,
                               ob_ready=evob)
            self.phase_end = [e for e in self.ot_war if e]
        return self.phase_end

    def emit_l2norm(self, qside, nq, Q0, ready, halo=0):
        p = self.p
        nch = 16 if qside else self.kvch
        STAT = self.pbank[2][0:16, 0:512]
        rnoff = 0 if qside else 512
        RN = self.srow[0:16, rnoff:rnoff + 512]
        evn = ready
        for tt in range(nq // 512):
            mmev = None
            for c in range(nch):
                sl = (self.qb(c, tt * 512, (tt + 1) * 512) if qside else
                      self.kb(c, halo + tt * 512, halo + (tt + 1) * 512))
                sqs = self.sq[:, (c % 2) * 512:(c % 2 + 1) * 512]
                evq = p.s(lambda e, o=sqs, i=sl: e.activation(o, i, AF.Square),
                          waits=[ready, self.sq_war[c % 2]])
                mmev = self.mm(STAT, self.ohc[:, c * 16:(c + 1) * 16], sqs,
                               [2] if c == 0 else [], waits=[evq],
                               start=(c == 0), stop=(c == nch - 1))
                self.sq_war[c % 2] = mmev
            e1 = p.s(lambda e, o=RN, i=STAT: e.activation(o, i, AF.Ln, bias=1e-38),
                     waits=[mmev] + list(self.rn_war))
            self.read_bank(2, e1)
            bias = math.log(SCALE) if qside else 0.0
            mv = 1e12 * SCALE if qside else 1e12
            e2 = p.s(lambda e, o=RN, i=RN, b=bias:
                     e.activation(o, i, AF.Exp, scale=-0.5, bias=b), waits=[])
            e3 = p.v(lambda e, o=RN, i=RN, m=mv: e.tensor_scalar_min(o, i, m),
                     waits=[e2])
            evn = None
            bc = None
            for c in range(nch):
                bk = 6 + (c % 2)
                bc = self.mm(self.pbank[bk], self.ohr[0:16, c * 128:(c + 1) * 128],
                             RN, [bk], waits=[e3])
                sl = (self.qb(c, tt * 512, (tt + 1) * 512) if qside else
                      self.kb(c, halo + tt * 512, halo + (tt + 1) * 512))
                evn = p.v(lambda e, o=sl, a=self.pbank[bk], b=sl:
                          e.tensor_mul(o, a, b), waits=[bc])
                self.read_bank(bk, evn)
            self.rn_war = [bc]
        return evn

    def emit_rope(self, L, qside, nq, Q0, ready, halo=0):
        p = self.p
        nch = 16 if qside else self.kvch
        evn = ready
        for tt in range(nq // 512):
            ta = Q0 + tt * 512
            cts = self.ct[:, ta:ta + 512]
            sts = self.st[:, ta:ta + 512]
            for c in range(nch):
                sl = (self.qb(c, tt * 512, (tt + 1) * 512) if qside else
                      self.kb(c, halo + tt * 512, halo + (tt + 1) * 512))
                rt = self.rtmp[:, (c % 2) * 512:(c % 2 + 1) * 512]
                perm = self.cp["c_psw0" if L == 0 else "c_psw"]
                bk = 4
                sw = self.mm(self.pbank[bk], perm, sl, [bk], waits=[ready])
                ev = p.v(lambda e, o=rt, a=self.pbank[bk], b=sts:
                         e.tensor_mul(o, a, b), waits=[sw])
                self.read_bank(bk, ev)
                p.v(lambda e, o=sl, a=sl, b=cts: e.tensor_mul(o, a, b), waits=[])
                evn = p.v(lambda e, o=sl, a=sl, b=rt: e.tensor_add(o, a, b),
                          waits=[])
        return evn

    def emit_attn_core(self, L, nq, Q0, B0, halo, nkt, rep, ready, proj_done):
        p = self.p
        hkt = halo // 128
        nqt = nq // 128
        evob = None
        first_ob = list(proj_done)
        for h in range(NH):
            kc_h = h // rep if L == 0 else h
            strip_ev = {}
            po_mm = None
            su_mm = None
            av_first = True
            for kt in range(nkt):
                K0 = B0 + kt * 128
                c_lo = max(K0, Q0) - K0
                c_hi = min(K0 + 640, Q0 + nq) - K0
                par = kt % 2
                SPA = self.pbank[par]
                SPB = self.pbank[2 + par][:, 0:128]
                ktile = self.kb(kc_h, kt * 128, (kt + 1) * 128)
                a0, a1 = c_lo, min(c_hi, 512)
                evm = []
                if a1 > a0:
                    m1 = self.mm(SPA[:, a0:a1], ktile,
                                 self.qb(h, K0 + a0 - Q0, K0 + a1 - Q0),
                                 [par], waits=ready)
                    mk1 = min(a1, 128)
                    if mk1 > a0:
                        e1 = p.v(lambda e, o=SPA[:, a0:mk1],
                                 i=self.mband[:, a0:mk1]:
                                 e.tensor_add(o, o, i), waits=[m1])
                    else:
                        e1 = m1
                    evm.append((SPA, a0, a1, e1))
                if c_hi > 512:
                    b0, b1 = 512, c_hi
                    m2 = self.mm(SPB[:, 0:b1 - b0], ktile,
                                 self.qb(h, K0 + b0 - Q0, K0 + b1 - Q0),
                                 [2 + par], waits=ready)
                    e2 = p.v(lambda e, o=SPB[:, 0:b1 - b0],
                             i=self.mband[:, b0:b1]:
                             e.tensor_add(o, o, i), waits=[m2])
                    evm.append((SPB, b0, b1, e2))
                rw = self.ring_war[kt % 6]
                eexp = None
                for (src, x0, x1, ev) in evm:
                    off = 0 if src is SPA else 512
                    eexp = p.s(lambda e, o=self.ringst(kt, x0, x1),
                               i=src[:, x0 - off:x1 - off]:
                               e.activation(o, i, AF.Exp),
                               waits=[ev] + ([rw] if rw else []))
                    bkid = par if src is SPA else 2 + par
                    self.read_bank(bkid, eexp)
                strip_ev[kt] = eexp
                if K0 >= Q0:
                    qt = (K0 - Q0) // 128
                    q4 = qt % 4
                    kts = list(range(max(0, kt - 4), kt + 1))
                    for i2, kt2 in enumerate(kts):
                        K02 = B0 + kt2 * 128
                        c0 = K0 - K02
                        rsl = self.ringst(kt2, c0, c0 + 128)
                        vt = self.vb(kt2, kc_h * 128 if L == 0 else h * 128,
                                     (kc_h * 128 if L == 0 else h * 128) + 128)
                        st_first = av_first and i2 == 0
                        po_mm = self.mm(self.pbank[4][:, q4 * 128:(q4 + 1) * 128],
                                        vt, rsl, [4] if st_first else [],
                                        waits=[strip_ev[kt2]],
                                        start=(i2 == 0), stop=(i2 == len(kts) - 1))
                        su_mm = self.mm(self.pbank[5][0:1, q4 * 128:(q4 + 1) * 128],
                                        self.cp["c_ones"], rsl,
                                        [5] if st_first else [],
                                        waits=[], start=(i2 == 0),
                                        stop=(i2 == len(kts) - 1))
                        self.ring_war[kt2 % 6] = su_mm
                    av_first = False
                    if q4 == 3 or qt == nqt - 1:
                        ng = (q4 + 1) * 128
                        qg0 = (qt - q4) * 128
                        ru = self.srow[0:1, 0:512]
                        evru = p.v(lambda e, o=ru[:, 0:ng],
                                   i=self.pbank[5][0:1, 0:ng]:
                                   e.reciprocal(o, i),
                                   waits=[su_mm] + self.ru_war)
                        self.read_bank(5, evru)
                        bc = self.mm(self.pbank[6][:, 0:ng], self.cp["c_onesrow"],
                                     ru[:, 0:ng], [6], waits=[evru])
                        self.ru_war = [bc]
                        dst = self.obuf(h, qg0, qg0 + ng)
                        epo = p.s(lambda e, o=dst, i=self.pbank[4][:, 0:ng]:
                                  e.copy(o, i), waits=[po_mm] + first_ob)
                        first_ob = []
                        self.read_bank(4, epo)
                        evob = p.v(lambda e, o=dst, a=self.pbank[6][:, 0:ng], b=dst:
                                   e.tensor_mul(o, a, b), waits=[bc, epo])
                        self.read_bank(6, evob)
                        av_first = True
        return evob

    def emit_out_proj(self, wap, nq, Q0, xap, cast, dap, fin, ob_ready):
        p = self.p
        evout = None
        for fop in range(8):
            par = fop % 2
            wev = self.load_wpair(wap, 2048, fop, par, waits=[self.ws_war[par]])
            mml = None
            for sub in range(2):
              fo = 2 * fop + sub
              for tt in range(nq // 512):
                bk = (fo * (nq // 512) + tt) % 2
                pbx = self.pbank[bk]
                mmev = None
                for kc in range(16):
                    mmev = self.mm(pbx, self.ws_tile2(par, kc, sub),
                                   self.obuf(kc, tt * 512, (tt + 1) * 512),
                                   [bk] if kc == 0 else [],
                                   waits=[wev, ob_ready] if kc == 0 else [],
                                   start=(kc == 0), stop=(kc == 15))
                self.ws_war[par] = mmev
                g = tt % 2
                if fo % 4 == 0:
                    self.xgev[g] = self.load_x4(xap, cast, fo, Q0 + tt * 512, g,
                                                waits=[self.xt_war[g]])
                dx = self.xsl(g, fo % 4)
                evd = self.xgev[g]
                if fin:
                    otile = self.of[:, (tt % 2) * 512:(tt % 2) * 512 + 512]
                else:
                    otile = self.rtmp[:, (tt % 2) * 512:(tt % 2) * 512 + 512]
                evo = p.v(lambda e, o=otile, a=pbx, b=dx: e.tensor_add(o, a, b),
                          waits=[mmev, evd, self.ot_war[tt % 2]])
                self.read_bank(bk, evo)
                self.xt_war[g] = evo
                dd = dap.rearrange("(kc p) t -> kc p t", p=128)[
                    fo, :, Q0 + tt * 512:Q0 + (tt + 1) * 512]
                evout = p.dma_s(dd, otile, waits=[evo], sem=f"st{tt % 2}")
                self.ot_war[tt % 2] = evout
        return evout

    def emit_wprep(self, wap, tern, rows, cols, sidx):
        p = self.p
        PR = self.pbank[2][0:1, 0:512]
        wv = wap.rearrange("(r p) (c f) -> p r c f", p=128, f=2048)
        tv = tern.rearrange("(r p) (c f) -> p r c f", p=128, f=2048)
        nr, ncc = rows // 128, cols // 2048
        ntile = nr * ncc
        wfs = [self.fscr[:, 0:2048], self.fscr[:, 2048:4096]]
        t1s = [self.fscr[:, 4096:6144], self.fscr[:, 6144:8192]]
        tbs = [self.tb0, self.tb1]
        war = [self.sq_war[0], self.sq_war[1]]
        mmev = None
        i = 0
        for r in range(nr):
            for cc in range(ncc):
                sl = i % 2
                evd = p.dma_s(wfs[sl], wv[:, r, cc], waits=[war[sl]],
                              sem=f"wf{sl}")
                eva = p.s(lambda e, o=tbs[sl], i_=wfs[sl]:
                          e.activation(o, i_, AF.Abs), waits=[evd])
                for hh in range(4):
                    mmev = self.mm(PR, self.cp["c_ones"],
                                   tbs[sl][:, hh * 512:(hh + 1) * 512],
                                   [2] if i == 0 and hh == 0 else [],
                                   waits=[eva], start=(i == 0 and hh == 0),
                                   stop=(i == ntile - 1 and hh == 3))
                war[sl] = eva
                self.sq_war[sl] = mmev
                i += 1
        tot = self.scal[0:1, sidx:sidx + 1]
        ev1 = p.v(lambda e, o=tot, i=PR: e.tensor_reduce(o, i, AX.X, ALU.add),
                  waits=[mmev])
        self.read_bank(2, ev1)
        N = float(rows * cols)
        sval = self.scal[0:1, sidx + 1:sidx + 2]
        ev2 = p.v(lambda e, o=sval, i=tot, n=1.0 / N:
                  e.tensor_scalar(o, i, n, EPSQ, ALU.mult, ALU.max),
                  waits=[hard(ev1)])
        rs = self.scal[0:1, sidx + 2:sidx + 3]
        ev3 = p.v(lambda e, o=rs, i=sval: e.reciprocal(o, i),
                  waits=[hard(ev2)])
        bc = self.mm(self.pbank[6][:, 0:1], self.cp["c_onesrow"], rs, [6],
                     waits=[hard(ev3)])
        rscol = self.scal[:, sidx + 3:sidx + 4]
        ev4 = p.s(lambda e, o=rscol, i=self.pbank[6][:, 0:1]: e.copy(o, i),
                  waits=[bc])
        self.read_bank(6, ev4)
        evw = None
        wfw = [war[0], war[1]]
        tbw = [self.sq_war[0], self.sq_war[1]]
        t1w = [None, None]
        i = 0
        for r in range(nr):
            for cc in range(ncc):
                sl = i % 2
                evd = p.dma_s(wfs[sl], wv[:, r, cc], waits=[wfw[sl]],
                              sem=f"wf{sl}")
                ea = p.s(lambda e, o=t1s[sl], i_=wfs[sl], sc=rscol:
                         e.activation(o, i_, AF.Copy, scale=sc, bias=MAGIC),
                         waits=[evd, hard(ev4), t1w[sl]])
                wfw[sl] = ea
                eb = p.v(lambda e, o=tbs[sl], i_=t1s[sl]:
                         e.tensor_scalar(o, i_, -MAGIC, 1.0, ALU.add, ALU.min),
                         waits=[ea, tbw[sl]])
                t1w[sl] = eb
                ec = p.v(lambda e, o=tbs[sl], i_=tbs[sl]:
                         e.tensor_scalar_max(o, i_, -1.0), waits=[])
                evw = p.dma_s(tv[:, r, cc], tbs[sl], waits=[ec], sem="tern")
                tbw[sl] = evw
                i += 1
        self.sq_war = [evw, evw]
        self.mu_war = evw
        return evw

    def emit_butterfly(self, tiles, waits):
        p = self.p
        n = len(tiles)
        ev = None
        s = 1
        first = True
        while s < n:
            for base in range(0, n, 2 * s):
                for k in range(s):
                    a = tiles[base + k]
                    b = tiles[base + k + s]
                    ev = p.v(lambda e, o=a, x=a, y=b: e.tensor_add(o, x, y),
                             waits=waits if first else [])
                    first = False
                    ev = p.v(lambda e, o=b, x=b, y=a:
                             e.scalar_tensor_tensor(o, x, -2.0, y,
                                                    ALU.mult, ALU.add), waits=[])
            s *= 2
        return ev

    def emit_actquant(self, nch, out_fn, eps_const, eps_row, waits, w=512):
        p = self.p
        M = self.rb[:, 0:512][:, 0:w]
        fv = self.fscr[:, 0:nch * w].rearrange("p (c w) -> p w c", w=w)
        ev = p.v(lambda e, o=M, i=fv:
                 e.tensor_reduce(o, i, AX.X, ALU.max, apply_absolute_value=True),
                 waits=list(waits) + ([self.rb_war[0]] if self.rb_war[0] else []))
        PR = self.pbank[3][0:1, 0:w]
        evtr = None
        for q in range(w // 128):
            tw = [ev] + ([self.bwar[2]] if self.bwar[2] else [])
            self.bwar[2] = None
            tev = p.t(lambda e, o=self.pbank[2][:, 0:128],
                      i=M[:, q * 128:(q + 1) * 128], idn=self.cp["c_id"]:
                      e.transpose(o, i, idn), waits=tw)
            col = self.srow[:, 4 + q:5 + q]
            rev = p.v(lambda e, o=col, i=self.pbank[2][:, 0:128]:
                      e.tensor_reduce(o, i, AX.X, ALU.max), waits=[tev])
            self.read_bank(2, rev)
            tw2 = [rev] + ([self.bwar[3]] if self.bwar[3] else [])
            self.bwar[3] = None
            evtr = p.t(lambda e, o=PR[:, q * 128:(q + 1) * 128], i=col,
                       idn=self.cp["c_id"]: e.transpose(o, i, idn), waits=tw2)
        self.rb_war[0] = None
        am = self.srow[0:1, 0:w]
        aw = [evtr] + ([self.am_war] if self.am_war else [])
        if eps_row is None:
            evam = p.v(lambda e, o=am, i=PR, cst=eps_const:
                       e.tensor_scalar_max(o, i, cst), waits=aw)
        else:
            evam = p.v(lambda e, o=am, a=PR, b=eps_row:
                       e.tensor_tensor(o, a, b, ALU.max), waits=aw)
        self.read_bank(3, evam)
        sc = self.srow[0:1, 512:512 + w]
        p.v(lambda e, o=sc, i=am: e.reciprocal(o, i), waits=[])
        ev6 = p.v(lambda e, o=sc, i=sc: e.tensor_scalar_mul(o, i, 127.0),
                  waits=[])
        bc = self.mm(self.pbank[6][:, 0:w], self.cp["c_onesrow"], sc, [6],
                     waits=[ev6])
        SCB = self.rb[:, 512:512 + w]
        sw = [bc] + ([self.rb_war[1]] if self.rb_war[1] else [])
        evb = p.s(lambda e, o=SCB, i=self.pbank[6][:, 0:w]: e.copy(o, i),
                  waits=sw)
        self.read_bank(6, evb)
        self.am_war = evb
        evq = None
        for c in range(nch):
            z = self.fscr[:, c * w:(c + 1) * w]
            p.v(lambda e, o=z, a=z, b=SCB: e.tensor_mul(o, a, b), waits=[evb])
            evq = p.v(lambda e, o=out_fn(c), i=z:
                      e.tensor_scalar(o, i, MAGIC, -MAGIC, ALU.add, ALU.add),
                      waits=[])
        self.rb_war[1] = evq
        self.rb_war[0] = evq
        return evq, am

    def emit_ffn0(self, si):
        p = self.p
        self.nq = 1024
        xap, cast = self.xsrc(si)
        dap, fin = self.xdst(si)
        p.fence(list(self.phase_end))
        evw_up = self.emit_wprep(self.io["l0_wupT"], self.ternup, 2048, 4096, 0)
        evw_dn = self.emit_wprep(self.io["l0_wdnT"], self.terndn, 4096, 2048, 4)
        EPS1 = EPSQ / C2
        for hf in range(2):
            T0 = hf * 1024
            evy = None
            for tt in range(2):
                evy = self.emit_ln(xap, cast, T0 + tt * 512, tt * 512)
            tiles = [self.yb(c, 0, 1024) for c in range(16)]
            evb = self.emit_butterfly(tiles, [evy])
            lamrow = self.srow[0:1, 1024:2048]
            evq2 = None
            evl = None
            for sub in range(2):
                evf = None
                for c in range(16):
                    bk = c % 2
                    m1 = self.mm(self.pbank[bk], self.cp["c_h128"],
                                 self.yb(c, sub * 512, (sub + 1) * 512),
                                 [bk], waits=[evb, evq2])
                    evf = p.s(lambda e, o=self.fscr[:, c * 512:(c + 1) * 512],
                              i=self.pbank[bk]: e.copy(o, i), waits=[m1])
                    self.read_bank(bk, evf)
                def yqout(c, s=sub):
                    return self.qb(c, s * 512, (s + 1) * 512)
                evq, am = self.emit_actquant(16, yqout, EPS1, None, [evf])
                lm = lamrow[:, sub * 512:(sub + 1) * 512]
                evl = p.v(lambda e, o=lm, i=am, s1=self.scal[0:1, 1:2]:
                          e.tensor_scalar(o, i, s1, C2 / 127.0, ALU.mult,
                                          ALU.mult), waits=[])
                evq2 = evq
            evu = None
            for fop in range(16):
                par = fop % 2
                wev = self.load_wpair(self.ternup, 2048, fop, par,
                                      waits=[self.ws_war[par], evw_up],
                                      cast=False)
                for sub in range(2):
                  fo = 2 * fop + sub
                  for tt in range(2):
                    bk = (fo * 2 + tt) % 2
                    pbx = self.pbank[bk]
                    mmev = None
                    for kc in range(16):
                        mmev = self.mm(pbx, self.ws_tile2(par, kc, sub),
                                       self.qb(kc, tt * 512, (tt + 1) * 512),
                                       [bk] if kc == 0 else [],
                                       waits=[wev, evq2] if kc == 0 else [],
                                       start=(kc == 0), stop=(kc == 15))
                    dst = self.arC[:, fo * 1024 + tt * 512:
                                   fo * 1024 + (tt + 1) * 512]
                    evu = p.s(lambda e, o=dst, i=pbx: e.activation(o, i, AF.Relu),
                              waits=[mmev])
                    self.read_bank(bk, evu)
                    self.ws_war[par] = mmev
            evsq = None
            for fo in range(32):
                hb = self.arC[:, fo * 1024:(fo + 1) * 1024]
                evsq = p.v(lambda e, o=hb, a=hb, b=hb: e.tensor_mul(o, a, b),
                           waits=[evu])
            tiles = [self.arC[:, c * 1024:(c + 1) * 1024] for c in range(32)]
            evb4 = self.emit_butterfly(tiles, [evsq])
            evq4 = None
            e5 = None
            mu_row = self.of[0:1, 0:1024]
            for sub in range(4):
                lm = lamrow[:, sub * 256:(sub + 1) * 256]
                l2r = self.rb[0:1, 768:1024][:, 0:256]
                p.v(lambda e, o=l2r, a=lm, b=lm: e.tensor_mul(o, a, b),
                    waits=[evl] + ([self.rb_war[1]] if self.rb_war[1] else []))
                ep2 = self.rb[0:1, 512:768][:, 0:256]
                p.v(lambda e, o=ep2, i=l2r: e.reciprocal(o, i), waits=[])
                e3 = p.v(lambda e, o=ep2, i=ep2:
                         e.tensor_scalar_mul(o, i, EPSQ / C4), waits=[])
                evf = None
                for c in range(32):
                    bk = c % 2
                    m1 = self.mm(self.pbank[bk][:, 0:256], self.cp["c_h128"],
                                 self.arC[:, c * 1024 + sub * 256:
                                          c * 1024 + (sub + 1) * 256],
                                 [bk], waits=[evb4, evq4])
                    evf = p.s(lambda e, o=self.fscr[:, c * 256:(c + 1) * 256],
                              i=self.pbank[bk][:, 0:256]: e.copy(o, i),
                              waits=[m1])
                    self.read_bank(bk, evf)
                def hqout(c, s=sub):
                    return self.arC[:, c * 1024 + s * 256:
                                    c * 1024 + (s + 1) * 256]
                evq4, am2 = self.emit_actquant(32, hqout, None, ep2,
                                               [evf, e3], w=256)
                mu = mu_row[:, sub * 256:(sub + 1) * 256]
                p.v(lambda e, o=mu, a=am2, b=l2r: e.tensor_mul(o, a, b),
                    waits=[self.mu_war] if self.mu_war else [])
                e5 = p.v(lambda e, o=mu, i=mu, s2=self.scal[0:1, 5:6]:
                         e.tensor_scalar(o, i, s2, C4 / 127.0, ALU.mult,
                                         ALU.mult), waits=[])
            evmu = []
            for tt in range(2):
                bw = [e5] + [x for x in self.rb_war if x]
                bc = self.mm(self.pbank[6], self.cp["c_onesrow"],
                             mu_row[:, tt * 512:(tt + 1) * 512], [6], waits=bw)
                evx = p.s(lambda e, o=self.rb[:, tt * 512:(tt + 1) * 512],
                          i=self.pbank[6]: e.copy(o, i), waits=[bc])
                self.read_bank(6, evx)
                evmu.append(evx)
            self.rb_war = [evmu[0], evmu[1]]
            self.mu_war = evmu[1]
            evout = None
            for fo in range(16):
                par = fo % 2
                wev = self.load_wchunk(self.terndn, 4096, fo, par,
                                       waits=[self.ws_war[par], evw_dn],
                                       cast=False)
                for tt in range(2):
                    bk = (fo * 2 + tt) % 2
                    pbx = self.pbank[bk]
                    mmev = None
                    for kc in range(32):
                        mmev = self.mm(pbx, self.ws_tile(par, kc),
                                       self.arC[:, kc * 1024 + tt * 512:
                                                kc * 1024 + (tt + 1) * 512],
                                       [bk] if kc == 0 else [],
                                       waits=[wev, evq4] if kc == 0 else [],
                                       start=(kc == 0), stop=(kc == 31))
                    t1 = self.rtmp[:, tt * 512:(tt + 1) * 512]
                    ev1 = p.v(lambda e, o=t1, a=pbx,
                              b=self.rb[:, tt * 512:(tt + 1) * 512]:
                              e.tensor_mul(o, a, b),
                              waits=[mmev, evmu[tt]] +
                                    ([self.t1_war] if self.t1_war else []))
                    self.read_bank(bk, ev1)
                    g = tt % 2
                    if fo % 4 == 0:
                        self.xgev[g] = self.load_x4(xap, cast, fo,
                                                    T0 + tt * 512, g,
                                                    waits=[self.xt_war[g]])
                    dx = self.xsl(g, fo % 4)
                    evd = self.xgev[g]
                    if fin:
                        otile = self.of[:, tt * 512:(tt + 1) * 512]
                    else:
                        otile = self.sq[:, tt * 512:(tt + 1) * 512]
                    evo = p.v(lambda e, o=otile, a=t1, b=dx: e.tensor_add(o, a, b),
                              waits=[evd] +
                                    ([self.ot_war[tt]] if self.ot_war[tt] else []) +
                                    ([self.sq_war[tt]] if self.sq_war[tt] else []))
                    self.xt_war[g] = evo
                    self.t1_war = evo
                    dd = dap.rearrange("(kc p) t -> kc p t", p=128)[
                        fo, :, T0 + tt * 512:T0 + (tt + 1) * 512]
                    evout = p.dma_s(dd, otile, waits=[evo], sem=f"st{tt}")
                    self.ot_war[tt] = evout
                    if not fin:
                        self.sq_war[tt] = evout
            self.phase_end = [e for e in self.ot_war if e]
            self.row_war = self.row_war + [evl]
        return self.phase_end

    def emit_ffn1(self, si):
        p = self.p
        self.nq = 1024
        xap, cast = self.xsrc(si)
        dap, fin = self.xdst(si)
        p.fence(list(self.phase_end))
        evs = p.dma_s(self.scal[:, 16:32], self.io["l1_fscale"], sem="const")
        for hf in range(2):
            T0 = hf * 1024
            evy = None
            for tt in range(2):
                evy = self.emit_ln(xap, cast, T0 + tt * 512, tt * 512)
            evo = None
            for fop in range(8):
                par = fop % 2
                wev = self.load_wpair(self.io["l1_fw"], 2048, fop, par,
                                      waits=[self.ws_war[par]])
                for sub in range(2):
                  fo = 2 * fop + sub
                  for tt in range(2):
                    bk = (fo * 2 + tt) % 2
                    pbx = self.pbank[bk]
                    mmev = None
                    for kc in range(16):
                        mmev = self.mm(pbx, self.ws_tile2(par, kc, sub),
                                       self.yb(kc, tt * 512, (tt + 1) * 512),
                                       [bk] if kc == 0 else [],
                                       waits=[wev, evy, evs] if kc == 0 else [],
                                       start=(kc == 0), stop=(kc == 15))
                    dst = self.qb(fo, tt * 512, (tt + 1) * 512)
                    evo = p.s(lambda e, o=dst, i=pbx, sc=self.scal[:, 16 + fo:17 + fo]:
                              e.activation(o, i, AF.Copy, scale=sc), waits=[mmev])
                    self.read_bank(bk, evo)
                    self.ws_war[par] = mmev
            evout = None
            for tt in range(2):
                PR = self.pbank[3][0:1, 0:512]
                mmev = None
                for fo in range(16):
                    sqs = self.sq[:, (fo % 2) * 512:(fo % 2 + 1) * 512]
                    evsq = p.s(lambda e, o=sqs,
                               i=self.qb(fo, tt * 512, (tt + 1) * 512):
                               e.activation(o, i, AF.Square),
                               waits=[evo, self.sq_war[fo % 2]])
                    mmev = self.mm(PR, self.cp["c_ones"], sqs,
                                   [3] if fo == 0 else [], waits=[evsq],
                                   start=(fo == 0), stop=(fo == 15))
                    self.sq_war[fo % 2] = mmev
                r1 = self.srow[0:1, 0:512]
                r2 = self.srow[0:1, 512:1024]
                r3 = self.srow[0:1, 1024:1536]
                r4 = self.srow[0:1, 1536:2048]
                e1 = p.s(lambda e, o=r1, i=PR: e.activation(o, i, AF.Ln,
                                                            bias=1e-38),
                         waits=[mmev] + self.row_war)
                self.read_bank(3, e1)
                e2 = p.s(lambda e, o=r1, i=r1: e.activation(o, i, AF.Exp,
                                                            scale=0.5), waits=[])
                e3 = p.v(lambda e, o=r2, i=r1: e.tensor_scalar_max(o, i, 1e-7),
                         waits=[e2])
                e4 = p.s(lambda e, o=r3, i=r2: e.activation(o, i, AF.Exp,
                                                            scale=-2.0),
                         waits=[e3])
                e5 = p.v(lambda e, o=r4, i=r3:
                         e.tensor_scalar(o, i, -1.0, 1.0, ALU.mult, ALU.add),
                         waits=[e4])
                e6 = p.v(lambda e, o=r3, i=r3: e.tensor_scalar_add(o, i, 1.0),
                         waits=[e4])
                e7 = p.v(lambda e, o=r3, i=r3: e.reciprocal(o, i), waits=[])
                e8 = p.v(lambda e, o=r4, a=r4, b=r3: e.tensor_mul(o, a, b),
                         waits=[])
                e9 = p.v(lambda e, o=r3, i=r2: e.tensor_scalar_add(o, i, 1e-7),
                         waits=[])
                e10 = p.v(lambda e, o=r3, i=r3: e.reciprocal(o, i), waits=[])
                e11 = p.v(lambda e, o=r4, a=r4, b=r3: e.tensor_mul(o, a, b),
                          waits=[])
                e12 = p.v(lambda e, o=r1, a=r1, b=r4: e.tensor_mul(o, a, b),
                          waits=[])
                e13 = p.v(lambda e, o=r1, i=r1: e.tensor_scalar_max(o, i, 1e-7),
                          waits=[])
                e14 = p.v(lambda e, o=r1, i=r1: e.reciprocal(o, i), waits=[])
                e15 = p.v(lambda e, o=r1, i=r1:
                          e.tensor_scalar(o, i, 0.99, 1.0, ALU.mult, ALU.min),
                          waits=[])
                e16 = p.v(lambda e, o=r4, a=r4, b=r1: e.tensor_mul(o, a, b),
                          waits=[])
                self.row_war = [e16]
                bc = self.mm(self.pbank[6], self.cp["c_onesrow"], r4, [6],
                             waits=[e16])
                RB = self.rb[:, tt * 512:(tt + 1) * 512]
                evb = p.s(lambda e, o=RB, i=self.pbank[6]: e.copy(o, i),
                          waits=[bc, self.rb_war[tt]])
                self.read_bank(6, evb)
                self.row_war = [bc]
                for fo in range(16):
                    t1 = self.rtmp[:, 0:512]
                    ev1 = p.v(lambda e, o=t1,
                              a=self.qb(fo, tt * 512, (tt + 1) * 512), b=RB:
                              e.tensor_mul(o, a, b),
                              waits=[evb] + ([self.t1_war] if self.t1_war else []))
                    g = tt % 2
                    if fo % 4 == 0:
                        self.xgev[g] = self.load_x4(xap, cast, fo,
                                                    T0 + tt * 512, g,
                                                    waits=[self.xt_war[g]])
                    dx = self.xsl(g, fo % 4)
                    evd = self.xgev[g]
                    if fin:
                        otile = self.of[:, (fo % 2) * 512:(fo % 2 + 1) * 512]
                    else:
                        otile = self.sq[:, (fo % 2) * 512:(fo % 2 + 1) * 512]
                    evo2 = p.v(lambda e, o=otile, a=t1, b=dx:
                               e.tensor_add(o, a, b),
                               waits=[evd] +
                                     ([self.ot_war[fo % 2]] if self.ot_war[fo % 2] else []) +
                                     ([self.sq_war[fo % 2]] if self.sq_war[fo % 2] else []))
                    self.xt_war[g] = evo2
                    self.t1_war = evo2
                    dd = dap.rearrange("(kc p) t -> kc p t", p=128)[
                        fo, :, T0 + tt * 512:T0 + (tt + 1) * 512]
                    evout = p.dma_s(dd, otile, waits=[evo2], sem=f"st{fo % 2}")
                    self.ot_war[fo % 2] = evout
                    if not fin:
                        self.sq_war[fo % 2] = evout
                self.rb_war[tt] = ev1
            self.phase_end = [e for e in self.ot_war if e]
        return self.phase_end



def build_kb(sublayers=(0, 1, 2, 3)):
    nc = bass.Bass("TRN2", target_bir_lowering=False)
    kb = KB(nc, sublayers)
    for si in sorted(sublayers):
        if si == 0:
            kb.emit_attn(0, 0)
        elif si == 1:
            kb.emit_ffn0(1)
        elif si == 2:
            kb.emit_attn(2, 1)
        elif si == 3:
            kb.emit_ffn1(3)
    return nc, kb


def finalize(nc, kb, extra_evs=()):
    kb.p.fence(list(kb.phase_end) + list(extra_evs))
    with nc.Block() as block:
        kb.p.replay(block)
    return nc


def build_nc(sublayers=(0, 1, 2, 3)):
    nc, kb = build_kb(sublayers)
    finalize(nc, kb)
    return nc, kb


def _prep_inputs(inputs):
    f = np.float32
    def cT(a):
        return np.ascontiguousarray(np.asarray(a, dtype=f).T)
    def c(a):
        return np.ascontiguousarray(np.asarray(a, dtype=f))
    com = dict(
        l0_wq=c(inputs["L0_attn_wq"]), l0_wk=c(inputs["L0_attn_wk"]),
        l0_wv=c(inputs["L0_attn_wv"]), l0_wo=c(inputs["L0_attn_wo"]),
        l0_wupT=cT(inputs["L0_ffn_w_up"]), l0_wdnT=cT(inputs["L0_ffn_w_down"]),
        l1_wqT=cT(inputs["L1_attn_Wq"]), l1_wkT=cT(inputs["L1_attn_Wk"]),
        l1_wvT=cT(inputs["L1_attn_Wv"]), l1_woT=cT(inputs["L1_attn_Wo"]),
        l1_fw=c(inputs["L1_ffn_weight"]),
        l1_fscale=np.ascontiguousarray(
            np.asarray(inputs["L1_ffn_scale"], dtype=f).reshape(16, 128).T),
    )
    com.update(host_constants())
    x = np.asarray(inputs["L0_attn_x"], dtype=f)
    maps = []
    for b in range(8):
        m = dict(com)
        m["xT"] = np.ascontiguousarray(x[b].T)
        maps.append(m)
    return maps


def _run(inputs, sublayers=(0, 1, 2, 3)):
    maps = _prep_inputs(inputs)
    nc, kb = build_nc(sublayers)
    res = run_bass_kernel_spmd(nc, maps, core_ids=list(range(8)))
    out = np.stack([np.asarray(r["outT"]).T for r in res.results])
    return np.ascontiguousarray(out)


def kernel(**inputs):
    return _run(inputs, (0, 1, 2, 3))


# version 09 (final)
# speedup vs baseline: 1.5172x; 1.0018x over previous
import math
import numpy as np
import ml_dtypes
import concourse.bass as bass
import concourse.mybir as mybir
from concourse.bass_utils import run_bass_kernel_spmd
from contextlib import ExitStack

F32 = mybir.dt.float32
BF16 = mybir.dt.bfloat16
AF = mybir.ActivationFunctionType
ALU = mybir.AluOpType
AX = mybir.AxisListType

D = 2048
T = 2048
NCH = 16
WIN = 512
DH = 128
NH = 16
MAGIC = 1.5 * 2.0 ** 23
SCALE = DH ** -0.5
ACT_CONSTS = [1e-5, 1e-38, -0.5, 0.5, -2.0, SCALE, MAGIC, math.log(SCALE)]
C2 = 1.0 / math.sqrt(2048.0)
C4 = 1.0 / math.sqrt(4096.0)
EPSQ = 1e-5
NEG = -1.0e30

ENGINES = ("tensor", "vector", "scalar", "gpsimd", "sync")



def _rope_tables():
    j = np.arange(64)
    invf = 10000.0 ** (-2.0 * j / 128.0)
    t = np.arange(T)
    fr = t[None, :] * invf[:, None]
    cos = np.cos(fr).astype(np.float32)
    sin = np.sin(fr).astype(np.float32)
    ct0 = np.concatenate([cos, cos], axis=0)
    st0 = np.concatenate([-sin, sin], axis=0)
    ct1 = np.zeros((128, T), np.float32)
    st1 = np.zeros((128, T), np.float32)
    ct1[0::2] = cos
    ct1[1::2] = cos
    st1[0::2] = -sin
    st1[1::2] = sin
    bf = ml_dtypes.bfloat16
    return (ct0.astype(bf), st0.astype(bf), ct1.astype(bf), st1.astype(bf))


def _hadamard(n):
    h = np.array([[1.0]])
    while h.shape[0] < n:
        h = np.block([[h, h], [h, -h]])
    return h


def host_constants():
    bf = ml_dtypes.bfloat16
    ct0, st0, ct1, st1 = _rope_tables()
    h128 = _hadamard(128).astype(bf)
    psw = np.zeros((128, 128), np.float32)
    for i in range(64):
        psw[2 * i + 1, 2 * i] = 1.0
        psw[2 * i, 2 * i + 1] = 1.0
    psw = psw.astype(bf)
    ph0 = np.zeros((128, 128), np.float32)
    for i in range(64):
        ph0[i + 64, i] = 1.0
        ph0[i, i + 64] = 1.0
    ph0 = ph0.astype(bf)
    r = np.arange(128)[:, None]
    c = np.arange(640)[None, :]
    mband = np.where((c >= r) & (c < r + WIN), 0.0, NEG).astype(np.float32)
    ohc = np.zeros((128, 16 * 16), np.float32)
    for h in range(16):
        ohc[:, 16 * h + h] = 1.0
    ohc = ohc.astype(bf)
    ohr = np.zeros((16, 16 * 128), np.float32)
    for h in range(16):
        ohr[h, 128 * h:128 * (h + 1)] = 1.0
    ones_col = np.ones((128, 1), np.float32).astype(bf)
    cmean_col = np.full((128, 1), 1.0 / 2048.0, np.float32).astype(bf)
    ones_row = np.ones((1, 128), np.float32)
    acv = np.array(ACT_CONSTS, np.float32)
    actc = np.broadcast_to(acv[None, :], (128, len(ACT_CONSTS))).copy()
    return dict(
        c_actc=actc,
        c_ct0=ct0, c_st0=st0, c_ct1=ct1, c_st1=st1,
        c_h128=h128, c_psw=psw, c_psw0=ph0, c_mband=mband,
        c_ohc=ohc, c_ohr=ohr.astype(np.float32),
        c_ones=ones_col, c_cmean=cmean_col, c_onesrow=ones_row,
        c_id=np.eye(128, dtype=np.float32),
    )



class Sem:
    __slots__ = ("name", "h", "n")

    def __init__(self, name, handle):
        self.name = name
        self.h = handle
        self.n = 0


class Ev:
    __slots__ = ("sem", "n", "eng")

    def __init__(self, sem, n, eng):
        self.sem = sem
        self.n = n
        self.eng = eng


def hard(ev):
    return Ev(ev.sem, ev.n, "hard") if ev is not None else None


class Prog:

    def __init__(self, nc):
        self.nc = nc
        self.streams = {e: [] for e in ENGINES}
        self.sems = {}
        for name in ("pe", "dv", "ac", "pl"):
            self.sems[name] = Sem(name, nc.alloc_semaphore("s_" + name))
        self.dsems = {}
        self.bank_war = {}
        self.nsteps = 0

    def go(self, eng, fn, waits=(), inc=None, inck=1):
        waits = tuple(w for w in waits if w is not None)
        for w in waits:
            assert w.n <= w.sem.n, f"wait on future event {w.sem.name}:{w.n}>{w.sem.n}"
        self.streams[eng].append((waits, fn, inc, inck))
        self.nsteps += 1
        if inc is not None:
            inc.n += inck
            return Ev(inc, inc.n, "dma" if eng in ("sync",) and inck >= 16 else eng)
        return None

    def t(self, fn, waits=(), inc=True):
        return self.go("tensor", fn, waits, self.sems["pe"] if inc else None)

    def v(self, fn, waits=(), inc=True):
        return self.go("vector", fn, waits, self.sems["dv"] if inc else None)

    def s(self, fn, waits=(), inc=True):
        return self.go("scalar", fn, waits, self.sems["ac"] if inc else None)

    def g(self, fn, waits=(), inc=True):
        return self.go("gpsimd", fn, waits, self.sems["pl"] if inc else None)

    def _dsem(self, name):
        if name not in self.dsems:
            self.dsems[name] = Sem(name, self.nc.alloc_semaphore("d_" + name))
        return self.dsems[name]

    def dma_s(self, out, in_, waits=(), sem="misc_s"):
        ev = self.go("sync", lambda e, o=out, i=in_: e.dma_start(out=o, in_=i),
                     waits, self._dsem(sem), 16)
        return Ev(ev.sem, ev.n, "dma")

    def dma_g(self, out, in_, waits=(), sem="misc_g"):
        ev = self.go("gpsimd", lambda e, o=out, i=in_: e.dma_start(out=o, in_=i),
                     waits, self._dsem(sem), 16)
        return Ev(ev.sem, ev.n, "dma")

    def fence(self, evs, engines=ENGINES):
        evs = [e for e in evs if e is not None]
        for eng in engines:
            self.go(eng, None, tuple(Ev(e.sem, e.n, "dma") for e in evs))

    def replay(self, block):
        for eng in ENGINES:
            def body(e, eng=eng):
                seen = {}
                for (waits, fn, inc, inck) in self.streams[eng]:
                    for w in waits:
                        if w.eng == eng:
                            continue
                        if seen.get(w.sem.name, -1) >= w.n:
                            continue
                        e.wait_ge(w.sem.h, w.n)
                        seen[w.sem.name] = w.n
                    if fn is None:
                        continue
                    ins = fn(e)
                    if inc is not None:
                        ins.then_inc(inc.h, inck)
            getattr(block, eng)(body)



class KB:
    def __init__(self, nc, sublayers):
        self.nc = nc
        self.sub = tuple(sublayers)
        self.p = Prog(nc)

        dt = nc.dram_tensor
        self.xT = dt("xT", [D, T], F32, kind="ExternalInput").ap()
        self.io = {}
        for nm, sh, ty in [
            ("l0_wq", [D, 2048], F32), ("l0_wk", [D, 512], F32),
            ("l0_wv", [D, 512], F32), ("l0_wo", [D, 2048], F32),
            ("l0_wupT", [D, 4096], F32), ("l0_wdnT", [4096, D], F32),
            ("l1_wqT", [D, 2048], F32), ("l1_wkT", [D, 2048], F32),
            ("l1_wvT", [D, 2048], F32), ("l1_woT", [D, 2048], F32),
            ("l1_fw", [D, 2048], F32), ("l1_fscale", [128, 16], F32),
            ("c_ct0", [128, T], BF16), ("c_st0", [128, T], BF16),
            ("c_ct1", [128, T], BF16), ("c_st1", [128, T], BF16),
            ("c_h128", [128, 128], BF16), ("c_psw", [128, 128], BF16),
            ("c_psw0", [128, 128], BF16),
            ("c_mband", [128, 640], F32), ("c_ohc", [128, 256], BF16),
            ("c_ohr", [16, 2048], F32), ("c_ones", [128, 1], BF16),
            ("c_cmean", [128, 1], BF16), ("c_onesrow", [1, 128], F32),
            ("c_id", [128, 128], F32),
            ("c_actc", [128, len(ACT_CONSTS)], F32),
        ]:
            self.io[nm] = dt(nm, sh, ty, kind="ExternalInput").ap()
        self.xres = dt("xres", [D, T], BF16, kind="Internal").ap()
        self.ternup = dt("ternup", [D, 4096], BF16, kind="Internal").ap()
        self.terndn = dt("terndn", [4096, D], BF16, kind="Internal").ap()
        self.outT = dt("outT", [D, T], F32, kind="ExternalOutput").ap()

        A = nc.alloc_sbuf_tensor
        self.arA = A("arA", [128, 16384], BF16).ap()
        self.arB = A("arB", [128, 16384], BF16).ap()
        self.arC = A("arC", [128, 32768], BF16).ap()
        ovl_t = A("ovl", [128, 20480], BF16)
        ob = ovl_t.ap()
        obf = ovl_t.bitcast(F32).ap()
        self.ring = ob[:, 0:3840]
        self.ct = ob[:, 3840:5888]
        self.st = ob[:, 5888:7936]
        self.sq = ob[:, 7936:8960]
        self.rtmp = ob[:, 8960:9984]
        self.tb0 = ob[:, 16384:18432]
        self.tb1 = ob[:, 18432:20480]
        self.mband = obf[:, 4992:5632]
        self.ohc = ob[:, 11264:11520]
        self.ohr = obf[0:16, 5760:7808]
        self.fscr = obf[:, 0:8192]
        self.wf = obf[:, 8192:9216]
        self.of = obf[:, 9216:10240]
        self.ws = A("ws", [128, 8192], BF16).ap()
        self.srow = A("srow", [128, 2048], F32).ap()
        self.rb = A("rb", [128, 1024], F32).ap()
        self.xt = A("xt", [128, 4096], BF16).ap()
        self.cp = {}
        for nm, sh, ty in [("c_h128", [128, 128], BF16), ("c_psw", [128, 128], BF16),
                           ("c_psw0", [128, 128], BF16),
                           ("c_ones", [128, 1], BF16), ("c_cmean", [128, 1], BF16),
                           ("c_onesrow", [1, 128], F32), ("c_id", [128, 128], F32)]:
            self.cp[nm] = A(nm + "_s", sh, ty).ap()
        self.scal = A("scal", [128, 32], F32).ap()
        self.actc = A("actc", [128, len(ACT_CONSTS)], F32).ap()
        self.tchs = A("tchs", [128, 2], BF16).ap()[:, 0:1]
        for j, v in enumerate(ACT_CONSTS):
            nc.const_aps.register(F32, float(v), self.actc[:, j:j + 1])
        rem = nc.sbuf_bytes_remaining
        assert rem >= 0, f"sbuf over budget by {-rem}"
        self.sbuf_rem = rem
        self.first = min(self.sub)
        self.xt_war = [None, None]
        self.sq_war = [None, None]
        self.rb_war = [None, None]
        self.row_war = []
        self.ws_war = [None, None]
        self.kv_war = None
        self.phase_end = []
        self.ring_war = [None] * 6
        self.ru_war = []
        self.rn_war = []
        self.wf_war = None
        self.wf_war2 = None
        self.am_war = None
        self.lam_war = None
        self.mu_war = None
        self.t1_war = None
        self.ot_war = [None, None]
        self.limit_passes = None
        self.xgev = {0: None, 1: None}

        self.pbank = [nc.alloc_psum_tensor(f"pbk{i}", [128, 512], F32).ap()
                      for i in range(8)]
        self.bwar = [None] * 8

        evs = [self.p.dma_s(self.cp[nm], self.io[nm], sem="const")
               for nm in self.cp]
        evs.append(self.p.dma_s(self.actc, self.io["c_actc"], sem="const"))
        self.p.fence(evs)
        self._preload_act()

    def _preload_act(self):
        idx = None
        try:
            from concourse.hw_specs import get_activation_tables
            tabs = get_activation_tables(self.nc.m.arch)
            for i, fns in enumerate(tabs.values()):
                if AF.Ln in fns and AF.Exp in fns:
                    idx = i
                    break
        except Exception:
            idx = 6
        if idx is None:
            return
        def emit(e, idx=idx):
            return e.add_instruction(mybir.InstLoadActFuncSet(
                name=self.nc.get_next_instruction_name(),
                act_func_set_id=idx, ins=[], outs=[]))
        self.p.s(emit, inc=False)

    def mm(self, out_ap, lhsT, rhs, banks, waits=(), start=True, stop=True):
        w = list(waits)
        if start:
            for b in banks:
                if self.bwar[b] is not None:
                    w.append(self.bwar[b])
                    self.bwar[b] = None
        return self.p.t(lambda e, o=out_ap, l=lhsT, r=rhs, st=start, sp=stop:
                        e.matmul(o, l, r, start=st, stop=sp), w)

    def read_bank(self, b, ev):
        self.bwar[b] = ev

    def xsrc(self, i):
        return (self.xT, True) if i == self.first else (self.xres, False)

    def xdst(self, i):
        return (self.outT, True) if i == max(self.sub) else (self.xres, False)

    def load_x4(self, src, cast, c0, t0, g, waits=()):
        sap = src.rearrange("(kc p) t -> p kc t", p=128)[:, c0:c0 + 4, t0:t0 + 512]
        dstv = self.xt[:, g * 2048:(g + 1) * 2048].rearrange(
            "p (kc t) -> p kc t", t=512)
        return (self.p.dma_g if cast else self.p.dma_s)(dstv, sap, waits,
                                                        sem=f"xt{g}")

    def xsl(self, g, j):
        return self.xt[:, g * 2048 + j * 512: g * 2048 + (j + 1) * 512]

    def load_wchunk(self, w_ap, din, fo, par, waits=(), cast=True, touch=True):
        kch = din // 128
        src = w_ap.rearrange("(kc p) f -> p kc f", p=128)[
            :, :, fo * 128:(fo + 1) * 128]
        dst = self.ws[:, par * 4096:par * 4096 + kch * 128].rearrange(
            "p (kc f) -> p kc f", f=128)
        return (self.p.dma_g if cast else self.p.dma_s)(dst, src, waits,
                                                        sem=f"ws{par}")

    def ws_tile(self, par, kc):
        return self.ws[:, par * 4096 + kc * 128: par * 4096 + (kc + 1) * 128]

    def load_wpair(self, w_ap, din, fop, par, waits=(), cast=True):
        kch = din // 128
        src = w_ap.rearrange("(kc p) f -> p kc f", p=128)[
            :, :, fop * 256:(fop + 1) * 256]
        dst = self.ws[:, par * 4096:par * 4096 + kch * 256].rearrange(
            "p (kc f) -> p kc f", f=256)
        return (self.p.dma_g if cast else self.p.dma_s)(dst, src, waits,
                                                        sem=f"ws{par}")

    def ws_tile2(self, par, kc, sub):
        o = par * 4096 + kc * 256 + sub * 128
        return self.ws[:, o:o + 128]

    def yb(self, c, a, b):
        return self.arA[:, c * self.nq + a: c * self.nq + b]

    def qb(self, c, a, b):
        return self.arB[:, c * self.nq + a: c * self.nq + b]

    def obuf(self, c, a, b):
        return self.arA[:, c * self.nq + a: c * self.nq + b]

    def kb(self, c, a, b):
        return self.arC[:, c * self.nkv + a: c * self.nkv + b]

    def vb(self, tile, a, b):
        off = self.kvch * self.nkv
        return self.arC[:, off + tile * self.fv + a: off + tile * self.fv + b]

    def ringst(self, kt, a, b):
        s = (kt % 6) * 640
        return self.ring[:, s + a: s + b]

    def emit_ln(self, xap, cast, t0, tl):
        p = self.p
        PR1 = self.pbank[2][0:1, 0:512]
        PR2 = self.pbank[3][0:1, 0:512]
        cme = self.cp["c_cmean"]
        mm1 = mm2 = None
        sq0 = self.sq[:, 0:512]
        sq1 = self.sq[:, 512:1024]
        for g in range(4):
            c0 = 4 * g
            evd = self.load_x4(xap, cast, c0, t0, g % 2,
                               waits=[self.xt_war[g % 2]])
            for j in range(4):
                xs = self.xsl(g % 2, j)
                sqs = sq0 if j % 2 == 0 else sq1
                evq = p.s(lambda e, o=sqs, i=xs: e.activation(o, i, AF.Square),
                          waits=[evd, self.sq_war[j % 2]])
                first = (g == 0 and j == 0)
                last = (g == 3 and j == 3)
                mm1 = self.mm(PR1, cme, xs, [2] if first else [],
                              waits=[evd], start=first, stop=last)
                mm2 = self.mm(PR2, cme, sqs, [3] if first else [],
                              waits=[evq], start=first, stop=last)
                self.sq_war[j % 2] = mm2
            self.xt_war[g % 2] = mm1
        mrow = self.srow[0:1, 1024:1536]
        vrow = self.srow[0:1, 1536:2048]
        evm = p.s(lambda e, o=mrow, i=PR1: e.copy(o, i),
                  waits=[mm1] + self.row_war)
        ev2 = p.v(lambda e, o=vrow, a=mrow: e.tensor_mul(o, a, a),
                  waits=[evm] + self.row_war)
        ev3 = p.v(lambda e, o=vrow, a=PR2, b=vrow: e.tensor_sub(o, a, b),
                  waits=[mm2])
        self.read_bank(2, evm)
        self.read_bank(3, ev3)
        ev4 = p.s(lambda e, o=vrow, i=vrow: e.activation(o, i, AF.Ln, bias=1e-5),
                  waits=[ev3])
        evr = p.s(lambda e, o=vrow, i=vrow: e.activation(o, i, AF.Exp, scale=-0.5),
                  waits=[ev4])
        bm = self.mm(self.pbank[6], self.cp["c_onesrow"], mrow, [6], waits=[evm])
        br = self.mm(self.pbank[7], self.cp["c_onesrow"], vrow, [7], waits=[evr])
        RBm = self.rb[:, 0:512]
        RBr = self.rb[:, 512:1024]
        evbm = p.s(lambda e, o=RBm, i=self.pbank[6]: e.copy(o, i),
                   waits=[bm, self.rb_war[0]])
        evbr = p.s(lambda e, o=RBr, i=self.pbank[7]: e.copy(o, i),
                   waits=[br, self.rb_war[1]])
        self.read_bank(6, evbm)
        self.read_bank(7, evbr)
        self.row_war = [bm, br, ev2]
        evy = None
        for g in range(4):
            c0 = 4 * g
            evd = self.load_x4(xap, cast, c0, t0, g % 2,
                               waits=[self.xt_war[g % 2]])
            for j in range(4):
                c = c0 + j
                xs = self.xsl(g % 2, j)
                rt = self.rtmp[:, (j % 2) * 512:(j % 2 + 1) * 512]
                ev5 = p.v(lambda e, o=rt, a=xs, b=RBm: e.tensor_sub(o, a, b),
                          waits=[evd, evbm])
                ys = self.yb(c, tl, tl + 512)
                evy = p.v(lambda e, o=ys, a=rt, b=RBr: e.tensor_mul(o, a, b),
                          waits=[evbr])
                self.xt_war[g % 2] = ev5
        self.rb_war[0] = evy
        self.rb_war[1] = evy
        return evy

    def emit_proj(self, wap, nfo, rhs_fn, ntt, out_fn, waits, scale=1.0):
        p = self.p
        eve = None
        for fop in range(nfo // 2):
            par = fop % 2
            wev = self.load_wpair(wap, 2048, fop, par, waits=[self.ws_war[par]])
            mmev = None
            for sub in range(2):
                fo = 2 * fop + sub
                for tt in range(ntt):
                    bk = (fo * ntt + tt) % 4
                    pbx = self.pbank[bk]
                    mmev = None
                    for kc in range(16):
                        mmev = self.mm(pbx, self.ws_tile2(par, kc, sub),
                                       rhs_fn(kc, tt),
                                       [bk] if kc == 0 else [],
                                       waits=([wev] + list(waits)) if kc == 0 else [],
                                       start=(kc == 0), stop=(kc == 15))
                    eve = p.s(lambda e, o=out_fn(fo, tt), i=pbx, sc=scale:
                              e.activation(o, i, AF.Copy, scale=sc), waits=[mmev])
                    self.read_bank(bk, eve)
            self.ws_war[par] = mmev
        return eve

    def emit_attn(self, si, L):
        p = self.p
        nq = 1024 if L == 0 else 512
        self.nq = nq
        self.nkv = nq + 512
        self.kvch = 4 if L == 0 else 16
        self.fv = 512 if L == 0 else 2048
        rep = 4 if L == 0 else 1
        names = (("l0_wq", "l0_wk", "l0_wv", "l0_wo", "c_ct0", "c_st0") if L == 0
                 else ("l1_wqT", "l1_wkT", "l1_wvT", "l1_woT", "c_ct1", "c_st1"))
        wq, wk, wv, wo, ctn, stn = names
        xap, cast = self.xsrc(si)
        dap, fin = self.xdst(si)
        npass = T // nq if self.limit_passes is None else self.limit_passes

        pw = list(self.phase_end)
        evs = [p.dma_s(self.ct, self.io[ctn], waits=pw, sem="const"),
               p.dma_s(self.st, self.io[stn], waits=pw, sem="const"),
               p.dma_s(self.mband, self.io["c_mband"], waits=pw, sem="const"),
               p.dma_s(self.ohc, self.io["c_ohc"], waits=pw, sem="const"),
               p.dma_s(self.ohr, self.io["c_ohr"], waits=pw, sem="const")]
        p.fence(evs + pw)

        for ps in range(npass):
            if ps > 0:
                p.fence(list(self.phase_end))
            Q0 = ps * nq
            B0 = max(0, Q0 - 512)
            halo = Q0 - B0
            nkt = (Q0 + nq - B0) // 128
            hkt = halo // 128

            cpev = []
            if ps > 0:
                B0p = max(0, Q0 - nq - 512)
                soff = Q0 - 512 - B0p
                for c in range(self.kvch):
                    src = self.kb(c, soff, soff + 512)
                    dst = self.kb(c, 0, 512)
                    cpev.append(p.g(lambda e, o=dst, i=src: e.tensor_copy(o, i),
                                    waits=[self.kv_war]))
                stile = soff // 128
                for tl in range(4):
                    src = self.vb(stile + tl, 0, self.fv)
                    dst = self.vb(tl, 0, self.fv)
                    cpev.append(p.g(lambda e, o=dst, i=src: e.tensor_copy(o, i),
                                    waits=[self.kv_war]))
            cpall = cpev[-1] if cpev else None

            evy = None
            for tt in range(nq // 512):
                evy = self.emit_ln(xap, cast, Q0 + tt * 512, tt * 512)
            ylast = evy

            def yrhs(kc, tt):
                return self.yb(kc, tt * 512, (tt + 1) * 512)
            evk = self.emit_proj(self.io[wk], self.kvch, yrhs, nq // 512,
                                 lambda fo, tt: self.kb(fo, halo + tt * 512,
                                                        halo + (tt + 1) * 512),
                                 waits=[ylast] + cpev)
            evv = None
            for fcg in range(self.fv // 256):
                par = fcg % 2
                src = self.io[wv].rearrange("(kc p) f -> p kc f", p=128)[
                    :, :, fcg * 256:(fcg + 1) * 256]
                dstw = self.ws[:, par * 4096:par * 4096 + 4096].rearrange(
                    "p (kc f) -> p kc f", f=256)
                wev = p.dma_g(dstw, src, waits=[self.ws_war[par]],
                              sem=f"ws{par}")
                for b in range(nq // 128):
                    bk = (fcg * (nq // 128) + b) % 4
                    pbx = self.pbank[bk][:, 0:256]
                    mmev = None
                    for kc in range(16):
                        lt = self.yb(kc, b * 128, (b + 1) * 128)
                        rt = self.ws[:, par * 4096 + kc * 256:
                                     par * 4096 + (kc + 1) * 256]
                        mmev = self.mm(pbx, lt, rt, [bk] if kc == 0 else [],
                                       waits=([wev, ylast] + cpev) if kc == 0 else [],
                                       start=(kc == 0), stop=(kc == 15))
                    dst = self.vb(hkt + b, fcg * 256, (fcg + 1) * 256)
                    evv = p.s(lambda e, o=dst, i=pbx: e.copy(o, i), waits=[mmev])
                    self.read_bank(bk, evv)
                    self.ws_war[par] = mmev
            evq = self.emit_proj(self.io[wq], 16, yrhs, nq // 512,
                                 lambda fo, tt: self.qb(fo, tt * 512, (tt + 1) * 512),
                                 waits=[ylast], scale=(1.0 if L == 0 else SCALE))

            evrope_q = evq
            evrope_k = evk
            if L == 0:
                evrope_q = self.emit_l2norm(True, nq, Q0, evq)
                evrope_k = self.emit_l2norm(False, nq, Q0, evk, halo=halo)
            evrope_q = self.emit_rope(L, True, nq, Q0, evrope_q, halo=0)
            evrope_k = self.emit_rope(L, False, nq, Q0, evrope_k, halo=halo)

            evob = self.emit_attn_core(L, nq, Q0, B0, halo, nkt, rep,
                                       [evrope_q, evrope_k, evv] + cpev,
                                       proj_done=[evq, evk, evv])
            self.kv_war = evob

            self.emit_out_proj(self.io[wo], nq, Q0, xap, cast, dap, fin,
                               ob_ready=evob)
            self.phase_end = [e for e in self.ot_war if e]
        return self.phase_end

    def emit_l2norm(self, qside, nq, Q0, ready, halo=0):
        p = self.p
        nch = 16 if qside else self.kvch
        STAT = self.pbank[2][0:16, 0:512]
        rnoff = 0 if qside else 512
        RN = self.srow[0:16, rnoff:rnoff + 512]
        evn = ready
        for tt in range(nq // 512):
            mmev = None
            for c in range(nch):
                sl = (self.qb(c, tt * 512, (tt + 1) * 512) if qside else
                      self.kb(c, halo + tt * 512, halo + (tt + 1) * 512))
                sqs = self.sq[:, (c % 2) * 512:(c % 2 + 1) * 512]
                evq = p.s(lambda e, o=sqs, i=sl: e.activation(o, i, AF.Square),
                          waits=[ready, self.sq_war[c % 2]])
                mmev = self.mm(STAT, self.ohc[:, c * 16:(c + 1) * 16], sqs,
                               [2] if c == 0 else [], waits=[evq],
                               start=(c == 0), stop=(c == nch - 1))
                self.sq_war[c % 2] = mmev
            e1 = p.s(lambda e, o=RN, i=STAT: e.activation(o, i, AF.Ln, bias=1e-38),
                     waits=[mmev] + list(self.rn_war))
            self.read_bank(2, e1)
            bias = math.log(SCALE) if qside else 0.0
            mv = 1e12 * SCALE if qside else 1e12
            e2 = p.s(lambda e, o=RN, i=RN, b=bias:
                     e.activation(o, i, AF.Exp, scale=-0.5, bias=b), waits=[])
            e3 = p.v(lambda e, o=RN, i=RN, m=mv: e.tensor_scalar_min(o, i, m),
                     waits=[e2])
            evn = None
            bc = None
            for c in range(nch):
                bk = 6 + (c % 2)
                bc = self.mm(self.pbank[bk], self.ohr[0:16, c * 128:(c + 1) * 128],
                             RN, [bk], waits=[e3])
                sl = (self.qb(c, tt * 512, (tt + 1) * 512) if qside else
                      self.kb(c, halo + tt * 512, halo + (tt + 1) * 512))
                evn = p.v(lambda e, o=sl, a=self.pbank[bk], b=sl:
                          e.tensor_mul(o, a, b), waits=[bc])
                self.read_bank(bk, evn)
            self.rn_war = [bc]
        return evn

    def emit_rope(self, L, qside, nq, Q0, ready, halo=0):
        p = self.p
        nch = 16 if qside else self.kvch
        evn = ready
        for tt in range(nq // 512):
            ta = Q0 + tt * 512
            cts = self.ct[:, ta:ta + 512]
            sts = self.st[:, ta:ta + 512]
            for c in range(nch):
                sl = (self.qb(c, tt * 512, (tt + 1) * 512) if qside else
                      self.kb(c, halo + tt * 512, halo + (tt + 1) * 512))
                rt = self.rtmp[:, (c % 2) * 512:(c % 2 + 1) * 512]
                perm = self.cp["c_psw0" if L == 0 else "c_psw"]
                bk = 4
                sw = self.mm(self.pbank[bk], perm, sl, [bk], waits=[ready])
                ev = p.v(lambda e, o=rt, a=self.pbank[bk], b=sts:
                         e.tensor_mul(o, a, b), waits=[sw])
                self.read_bank(bk, ev)
                p.v(lambda e, o=sl, a=sl, b=cts: e.tensor_mul(o, a, b), waits=[])
                evn = p.v(lambda e, o=sl, a=sl, b=rt: e.tensor_add(o, a, b),
                          waits=[])
        return evn

    def emit_attn_core(self, L, nq, Q0, B0, halo, nkt, rep, ready, proj_done):
        p = self.p
        hkt = halo // 128
        nqt = nq // 128
        evob = None
        first_ob = list(proj_done)
        for h in range(NH):
            kc_h = h // rep if L == 0 else h
            strip_ev = {}
            po_mm = None
            su_mm = None
            av_first = True
            for kt in range(nkt):
                K0 = B0 + kt * 128
                c_lo = max(K0, Q0) - K0
                c_hi = min(K0 + 640, Q0 + nq) - K0
                par = kt % 2
                SPA = self.pbank[par]
                SPB = self.pbank[2 + par][:, 0:128]
                ktile = self.kb(kc_h, kt * 128, (kt + 1) * 128)
                a0, a1 = c_lo, min(c_hi, 512)
                evm = []
                if a1 > a0:
                    m1 = self.mm(SPA[:, a0:a1], ktile,
                                 self.qb(h, K0 + a0 - Q0, K0 + a1 - Q0),
                                 [par], waits=ready)
                    mk1 = min(a1, 128)
                    if mk1 > a0:
                        e1 = p.v(lambda e, o=SPA[:, a0:mk1],
                                 i=self.mband[:, a0:mk1]:
                                 e.tensor_add(o, o, i), waits=[m1])
                    else:
                        e1 = m1
                    evm.append((SPA, a0, a1, e1))
                if c_hi > 512:
                    b0, b1 = 512, c_hi
                    m2 = self.mm(SPB[:, 0:b1 - b0], ktile,
                                 self.qb(h, K0 + b0 - Q0, K0 + b1 - Q0),
                                 [2 + par], waits=ready)
                    e2 = p.v(lambda e, o=SPB[:, 0:b1 - b0],
                             i=self.mband[:, b0:b1]:
                             e.tensor_add(o, o, i), waits=[m2])
                    evm.append((SPB, b0, b1, e2))
                rw = self.ring_war[kt % 6]
                eexp = None
                for (src, x0, x1, ev) in evm:
                    off = 0 if src is SPA else 512
                    eexp = p.s(lambda e, o=self.ringst(kt, x0, x1),
                               i=src[:, x0 - off:x1 - off]:
                               e.activation(o, i, AF.Exp),
                               waits=[ev] + ([rw] if rw else []))
                    bkid = par if src is SPA else 2 + par
                    self.read_bank(bkid, eexp)
                strip_ev[kt] = eexp
                if K0 >= Q0:
                    qt = (K0 - Q0) // 128
                    q4 = qt % 4
                    kts = list(range(max(0, kt - 4), kt + 1))
                    for i2, kt2 in enumerate(kts):
                        K02 = B0 + kt2 * 128
                        c0 = K0 - K02
                        rsl = self.ringst(kt2, c0, c0 + 128)
                        vt = self.vb(kt2, kc_h * 128 if L == 0 else h * 128,
                                     (kc_h * 128 if L == 0 else h * 128) + 128)
                        st_first = av_first and i2 == 0
                        po_mm = self.mm(self.pbank[4][:, q4 * 128:(q4 + 1) * 128],
                                        vt, rsl, [4] if st_first else [],
                                        waits=[strip_ev[kt2]],
                                        start=(i2 == 0), stop=(i2 == len(kts) - 1))
                        su_mm = self.mm(self.pbank[5][0:1, q4 * 128:(q4 + 1) * 128],
                                        self.cp["c_ones"], rsl,
                                        [5] if st_first else [],
                                        waits=[], start=(i2 == 0),
                                        stop=(i2 == len(kts) - 1))
                        self.ring_war[kt2 % 6] = su_mm
                    av_first = False
                    if q4 == 3 or qt == nqt - 1:
                        ng = (q4 + 1) * 128
                        qg0 = (qt - q4) * 128
                        ru = self.srow[0:1, 0:512]
                        evru = p.v(lambda e, o=ru[:, 0:ng],
                                   i=self.pbank[5][0:1, 0:ng]:
                                   e.reciprocal(o, i),
                                   waits=[su_mm] + self.ru_war)
                        self.read_bank(5, evru)
                        bc = self.mm(self.pbank[6][:, 0:ng], self.cp["c_onesrow"],
                                     ru[:, 0:ng], [6], waits=[evru])
                        self.ru_war = [bc]
                        dst = self.obuf(h, qg0, qg0 + ng)
                        epo = p.s(lambda e, o=dst, i=self.pbank[4][:, 0:ng]:
                                  e.copy(o, i), waits=[po_mm] + first_ob)
                        first_ob = []
                        self.read_bank(4, epo)
                        evob = p.v(lambda e, o=dst, a=self.pbank[6][:, 0:ng], b=dst:
                                   e.tensor_mul(o, a, b), waits=[bc, epo])
                        self.read_bank(6, evob)
                        av_first = True
        return evob

    def emit_out_proj(self, wap, nq, Q0, xap, cast, dap, fin, ob_ready):
        p = self.p
        evout = None
        for fop in range(8):
            par = fop % 2
            wev = self.load_wpair(wap, 2048, fop, par, waits=[self.ws_war[par]])
            mml = None
            for sub in range(2):
              fo = 2 * fop + sub
              for tt in range(nq // 512):
                bk = (fo * (nq // 512) + tt) % 4
                pbx = self.pbank[bk]
                mmev = None
                for kc in range(16):
                    mmev = self.mm(pbx, self.ws_tile2(par, kc, sub),
                                   self.obuf(kc, tt * 512, (tt + 1) * 512),
                                   [bk] if kc == 0 else [],
                                   waits=[wev, ob_ready] if kc == 0 else [],
                                   start=(kc == 0), stop=(kc == 15))
                self.ws_war[par] = mmev
                g = tt % 2
                if fo % 4 == 0:
                    self.xgev[g] = self.load_x4(xap, cast, fo, Q0 + tt * 512, g,
                                                waits=[self.xt_war[g]])
                dx = self.xsl(g, fo % 4)
                evd = self.xgev[g]
                if fin:
                    otile = self.of[:, (tt % 2) * 512:(tt % 2) * 512 + 512]
                else:
                    otile = self.rtmp[:, (tt % 2) * 512:(tt % 2) * 512 + 512]
                evo = p.v(lambda e, o=otile, a=pbx, b=dx: e.tensor_add(o, a, b),
                          waits=[mmev, evd, self.ot_war[tt % 2]])
                self.read_bank(bk, evo)
                self.xt_war[g] = evo
                dd = dap.rearrange("(kc p) t -> kc p t", p=128)[
                    fo, :, Q0 + tt * 512:Q0 + (tt + 1) * 512]
                evout = p.dma_s(dd, otile, waits=[evo], sem=f"st{tt % 2}")
                self.ot_war[tt % 2] = evout
        return evout

    def emit_wprep(self, wap, tern, rows, cols, sidx):
        p = self.p
        PR = self.pbank[2][0:1, 0:512]
        wv = wap.rearrange("(r p) (c f) -> p r c f", p=128, f=2048)
        tv = tern.rearrange("(r p) (c f) -> p r c f", p=128, f=2048)
        nr, ncc = rows // 128, cols // 2048
        ntile = nr * ncc
        wfs = [self.fscr[:, 0:2048], self.fscr[:, 2048:4096]]
        t1s = [self.fscr[:, 4096:6144], self.fscr[:, 6144:8192]]
        tbs = [self.tb0, self.tb1]
        war = [self.sq_war[0], self.sq_war[1]]
        mmev = None
        i = 0
        for r in range(nr):
            for cc in range(ncc):
                sl = i % 2
                evd = p.dma_s(wfs[sl], wv[:, r, cc], waits=[war[sl]],
                              sem=f"wf{sl}")
                eva = p.s(lambda e, o=tbs[sl], i_=wfs[sl]:
                          e.activation(o, i_, AF.Abs), waits=[evd])
                for hh in range(4):
                    mmev = self.mm(PR, self.cp["c_ones"],
                                   tbs[sl][:, hh * 512:(hh + 1) * 512],
                                   [2] if i == 0 and hh == 0 else [],
                                   waits=[eva], start=(i == 0 and hh == 0),
                                   stop=(i == ntile - 1 and hh == 3))
                war[sl] = eva
                self.sq_war[sl] = mmev
                i += 1
        tot = self.scal[0:1, sidx:sidx + 1]
        ev1 = p.v(lambda e, o=tot, i=PR: e.tensor_reduce(o, i, AX.X, ALU.add),
                  waits=[mmev])
        self.read_bank(2, ev1)
        N = float(rows * cols)
        sval = self.scal[0:1, sidx + 1:sidx + 2]
        ev2 = p.v(lambda e, o=sval, i=tot, n=1.0 / N:
                  e.tensor_scalar(o, i, n, EPSQ, ALU.mult, ALU.max),
                  waits=[hard(ev1)])
        rs = self.scal[0:1, sidx + 2:sidx + 3]
        ev3 = p.v(lambda e, o=rs, i=sval: e.reciprocal(o, i),
                  waits=[hard(ev2)])
        bc = self.mm(self.pbank[6][:, 0:1], self.cp["c_onesrow"], rs, [6],
                     waits=[hard(ev3)])
        rscol = self.scal[:, sidx + 3:sidx + 4]
        ev4 = p.s(lambda e, o=rscol, i=self.pbank[6][:, 0:1]: e.copy(o, i),
                  waits=[bc])
        self.read_bank(6, ev4)
        evw = None
        wfw = [war[0], war[1]]
        tbw = [self.sq_war[0], self.sq_war[1]]
        t1w = [None, None]
        i = 0
        for r in range(nr):
            for cc in range(ncc):
                sl = i % 2
                evd = p.dma_s(wfs[sl], wv[:, r, cc], waits=[wfw[sl]],
                              sem=f"wf{sl}")
                ea = p.s(lambda e, o=t1s[sl], i_=wfs[sl], sc=rscol:
                         e.activation(o, i_, AF.Copy, scale=sc, bias=MAGIC),
                         waits=[evd, hard(ev4), t1w[sl]])
                wfw[sl] = ea
                eb = p.v(lambda e, o=tbs[sl], i_=t1s[sl]:
                         e.tensor_scalar(o, i_, -MAGIC, 1.0, ALU.add, ALU.min),
                         waits=[ea, tbw[sl]])
                t1w[sl] = eb
                ec = p.v(lambda e, o=tbs[sl], i_=tbs[sl]:
                         e.tensor_scalar_max(o, i_, -1.0), waits=[])
                evw = p.dma_s(tv[:, r, cc], tbs[sl], waits=[ec], sem="tern")
                tbw[sl] = evw
                i += 1
        self.sq_war = [evw, evw]
        self.mu_war = evw
        return evw

    def emit_butterfly(self, tiles, waits):
        p = self.p
        n = len(tiles)
        ev = None
        s = 1
        first = True
        while s < n:
            for base in range(0, n, 2 * s):
                for k in range(s):
                    a = tiles[base + k]
                    b = tiles[base + k + s]
                    ev = p.v(lambda e, o=a, x=a, y=b: e.tensor_add(o, x, y),
                             waits=waits if first else [])
                    first = False
                    ev = p.v(lambda e, o=b, x=b, y=a:
                             e.scalar_tensor_tensor(o, x, -2.0, y,
                                                    ALU.mult, ALU.add), waits=[])
            s *= 2
        return ev

    def emit_actquant(self, nch, out_fn, eps_const, eps_row, waits, w=512):
        p = self.p
        M = self.rb[:, 0:512][:, 0:w]
        fv = self.fscr[:, 0:nch * w].rearrange("p (c w) -> p w c", w=w)
        ev = p.v(lambda e, o=M, i=fv:
                 e.tensor_reduce(o, i, AX.X, ALU.max, apply_absolute_value=True),
                 waits=list(waits) + ([self.rb_war[0]] if self.rb_war[0] else []))
        PR = self.pbank[3][0:1, 0:w]
        evtr = None
        for q in range(w // 128):
            tw = [ev] + ([self.bwar[2]] if self.bwar[2] else [])
            self.bwar[2] = None
            tev = p.t(lambda e, o=self.pbank[2][:, 0:128],
                      i=M[:, q * 128:(q + 1) * 128], idn=self.cp["c_id"]:
                      e.transpose(o, i, idn), waits=tw)
            col = self.srow[:, 4 + q:5 + q]
            rev = p.v(lambda e, o=col, i=self.pbank[2][:, 0:128]:
                      e.tensor_reduce(o, i, AX.X, ALU.max), waits=[tev])
            self.read_bank(2, rev)
            tw2 = [rev] + ([self.bwar[3]] if self.bwar[3] else [])
            self.bwar[3] = None
            evtr = p.t(lambda e, o=PR[:, q * 128:(q + 1) * 128], i=col,
                       idn=self.cp["c_id"]: e.transpose(o, i, idn), waits=tw2)
        self.rb_war[0] = None
        am = self.srow[0:1, 0:w]
        aw = [evtr] + ([self.am_war] if self.am_war else [])
        if eps_row is None:
            evam = p.v(lambda e, o=am, i=PR, cst=eps_const:
                       e.tensor_scalar_max(o, i, cst), waits=aw)
        else:
            evam = p.v(lambda e, o=am, a=PR, b=eps_row:
                       e.tensor_tensor(o, a, b, ALU.max), waits=aw)
        self.read_bank(3, evam)
        sc = self.srow[0:1, 512:512 + w]
        p.v(lambda e, o=sc, i=am: e.reciprocal(o, i), waits=[])
        ev6 = p.v(lambda e, o=sc, i=sc: e.tensor_scalar_mul(o, i, 127.0),
                  waits=[])
        bc = self.mm(self.pbank[6][:, 0:w], self.cp["c_onesrow"], sc, [6],
                     waits=[ev6])
        SCB = self.rb[:, 512:512 + w]
        sw = [bc] + ([self.rb_war[1]] if self.rb_war[1] else [])
        evb = p.s(lambda e, o=SCB, i=self.pbank[6][:, 0:w]: e.copy(o, i),
                  waits=sw)
        self.read_bank(6, evb)
        self.am_war = evb
        evq = None
        for c in range(nch):
            z = self.fscr[:, c * w:(c + 1) * w]
            p.v(lambda e, o=z, a=z, b=SCB: e.tensor_mul(o, a, b), waits=[evb])
            evq = p.v(lambda e, o=out_fn(c), i=z:
                      e.tensor_scalar(o, i, MAGIC, -MAGIC, ALU.add, ALU.add),
                      waits=[])
        self.rb_war[1] = evq
        self.rb_war[0] = evq
        return evq, am

    def emit_ffn0(self, si):
        p = self.p
        self.nq = 1024
        xap, cast = self.xsrc(si)
        dap, fin = self.xdst(si)
        p.fence(list(self.phase_end))
        evw_up = self.emit_wprep(self.io["l0_wupT"], self.ternup, 2048, 4096, 0)
        evw_dn = self.emit_wprep(self.io["l0_wdnT"], self.terndn, 4096, 2048, 4)
        EPS1 = EPSQ / C2
        for hf in range(2):
            T0 = hf * 1024
            evy = None
            for tt in range(2):
                evy = self.emit_ln(xap, cast, T0 + tt * 512, tt * 512)
            tiles = [self.yb(c, 0, 1024) for c in range(16)]
            evb = self.emit_butterfly(tiles, [evy])
            lamrow = self.srow[0:1, 1024:2048]
            evq2 = None
            evl = None
            for sub in range(2):
                evf = None
                for c in range(16):
                    bk = c % 4
                    m1 = self.mm(self.pbank[bk], self.cp["c_h128"],
                                 self.yb(c, sub * 512, (sub + 1) * 512),
                                 [bk], waits=[evb, evq2])
                    evf = p.s(lambda e, o=self.fscr[:, c * 512:(c + 1) * 512],
                              i=self.pbank[bk]: e.copy(o, i), waits=[m1])
                    self.read_bank(bk, evf)
                def yqout(c, s=sub):
                    return self.qb(c, s * 512, (s + 1) * 512)
                evq, am = self.emit_actquant(16, yqout, EPS1, None, [evf])
                lm = lamrow[:, sub * 512:(sub + 1) * 512]
                evl = p.v(lambda e, o=lm, i=am, s1=self.scal[0:1, 1:2]:
                          e.tensor_scalar(o, i, s1, C2 / 127.0, ALU.mult,
                                          ALU.mult), waits=[])
                evq2 = evq
            evu = None
            for fop in range(16):
                par = fop % 2
                wev = self.load_wpair(self.ternup, 2048, fop, par,
                                      waits=[self.ws_war[par], evw_up],
                                      cast=False)
                for sub in range(2):
                  fo = 2 * fop + sub
                  for tt in range(2):
                    bk = (fo * 2 + tt) % 4
                    pbx = self.pbank[bk]
                    mmev = None
                    for kc in range(16):
                        mmev = self.mm(pbx, self.ws_tile2(par, kc, sub),
                                       self.qb(kc, tt * 512, (tt + 1) * 512),
                                       [bk] if kc == 0 else [],
                                       waits=[wev, evq2] if kc == 0 else [],
                                       start=(kc == 0), stop=(kc == 15))
                    dst = self.arC[:, fo * 1024 + tt * 512:
                                   fo * 1024 + (tt + 1) * 512]
                    evu = p.s(lambda e, o=dst, i=pbx: e.activation(o, i, AF.Relu),
                              waits=[mmev])
                    self.read_bank(bk, evu)
                    self.ws_war[par] = mmev
            evsq = None
            for fo in range(32):
                hb = self.arC[:, fo * 1024:(fo + 1) * 1024]
                evsq = p.v(lambda e, o=hb, a=hb, b=hb: e.tensor_mul(o, a, b),
                           waits=[evu])
            tiles = [self.arC[:, c * 1024:(c + 1) * 1024] for c in range(32)]
            evb4 = self.emit_butterfly(tiles, [evsq])
            evq4 = None
            e5 = None
            mu_row = self.of[0:1, 0:1024]
            for sub in range(4):
                lm = lamrow[:, sub * 256:(sub + 1) * 256]
                l2r = self.rb[0:1, 768:1024][:, 0:256]
                p.v(lambda e, o=l2r, a=lm, b=lm: e.tensor_mul(o, a, b),
                    waits=[evl] + ([self.rb_war[1]] if self.rb_war[1] else []))
                ep2 = self.rb[0:1, 512:768][:, 0:256]
                p.v(lambda e, o=ep2, i=l2r: e.reciprocal(o, i), waits=[])
                e3 = p.v(lambda e, o=ep2, i=ep2:
                         e.tensor_scalar_mul(o, i, EPSQ / C4), waits=[])
                evf = None
                for c in range(32):
                    bk = c % 4
                    m1 = self.mm(self.pbank[bk][:, 0:256], self.cp["c_h128"],
                                 self.arC[:, c * 1024 + sub * 256:
                                          c * 1024 + (sub + 1) * 256],
                                 [bk], waits=[evb4, evq4])
                    evf = p.s(lambda e, o=self.fscr[:, c * 256:(c + 1) * 256],
                              i=self.pbank[bk][:, 0:256]: e.copy(o, i),
                              waits=[m1])
                    self.read_bank(bk, evf)
                def hqout(c, s=sub):
                    return self.arC[:, c * 1024 + s * 256:
                                    c * 1024 + (s + 1) * 256]
                evq4, am2 = self.emit_actquant(32, hqout, None, ep2,
                                               [evf, e3], w=256)
                mu = mu_row[:, sub * 256:(sub + 1) * 256]
                p.v(lambda e, o=mu, a=am2, b=l2r: e.tensor_mul(o, a, b),
                    waits=[self.mu_war] if self.mu_war else [])
                e5 = p.v(lambda e, o=mu, i=mu, s2=self.scal[0:1, 5:6]:
                         e.tensor_scalar(o, i, s2, C4 / 127.0, ALU.mult,
                                         ALU.mult), waits=[])
            evmu = []
            for tt in range(2):
                bw = [e5] + [x for x in self.rb_war if x]
                bc = self.mm(self.pbank[6], self.cp["c_onesrow"],
                             mu_row[:, tt * 512:(tt + 1) * 512], [6], waits=bw)
                evx = p.s(lambda e, o=self.rb[:, tt * 512:(tt + 1) * 512],
                          i=self.pbank[6]: e.copy(o, i), waits=[bc])
                self.read_bank(6, evx)
                evmu.append(evx)
            self.rb_war = [evmu[0], evmu[1]]
            self.mu_war = evmu[1]
            evout = None
            for fo in range(16):
                par = fo % 2
                wev = self.load_wchunk(self.terndn, 4096, fo, par,
                                       waits=[self.ws_war[par], evw_dn],
                                       cast=False)
                for tt in range(2):
                    bk = (fo * 2 + tt) % 4
                    pbx = self.pbank[bk]
                    mmev = None
                    for kc in range(32):
                        mmev = self.mm(pbx, self.ws_tile(par, kc),
                                       self.arC[:, kc * 1024 + tt * 512:
                                                kc * 1024 + (tt + 1) * 512],
                                       [bk] if kc == 0 else [],
                                       waits=[wev, evq4] if kc == 0 else [],
                                       start=(kc == 0), stop=(kc == 31))
                    t1 = self.rtmp[:, tt * 512:(tt + 1) * 512]
                    ev1 = p.v(lambda e, o=t1, a=pbx,
                              b=self.rb[:, tt * 512:(tt + 1) * 512]:
                              e.tensor_mul(o, a, b),
                              waits=[mmev, evmu[tt]] +
                                    ([self.t1_war] if self.t1_war else []))
                    self.read_bank(bk, ev1)
                    g = tt % 2
                    if fo % 4 == 0:
                        self.xgev[g] = self.load_x4(xap, cast, fo,
                                                    T0 + tt * 512, g,
                                                    waits=[self.xt_war[g]])
                    dx = self.xsl(g, fo % 4)
                    evd = self.xgev[g]
                    if fin:
                        otile = self.of[:, tt * 512:(tt + 1) * 512]
                    else:
                        otile = self.sq[:, tt * 512:(tt + 1) * 512]
                    evo = p.v(lambda e, o=otile, a=t1, b=dx: e.tensor_add(o, a, b),
                              waits=[evd] +
                                    ([self.ot_war[tt]] if self.ot_war[tt] else []) +
                                    ([self.sq_war[tt]] if self.sq_war[tt] else []))
                    self.xt_war[g] = evo
                    self.t1_war = evo
                    dd = dap.rearrange("(kc p) t -> kc p t", p=128)[
                        fo, :, T0 + tt * 512:T0 + (tt + 1) * 512]
                    evout = p.dma_s(dd, otile, waits=[evo], sem=f"st{tt}")
                    self.ot_war[tt] = evout
                    if not fin:
                        self.sq_war[tt] = evout
            self.phase_end = [e for e in self.ot_war if e]
            self.row_war = self.row_war + [evl]
        return self.phase_end

    def emit_ffn1(self, si):
        p = self.p
        self.nq = 1024
        xap, cast = self.xsrc(si)
        dap, fin = self.xdst(si)
        p.fence(list(self.phase_end))
        evs = p.dma_s(self.scal[:, 16:32], self.io["l1_fscale"], sem="const")
        for hf in range(2):
            T0 = hf * 1024
            evy = None
            for tt in range(2):
                evy = self.emit_ln(xap, cast, T0 + tt * 512, tt * 512)
            evo = None
            for fop in range(8):
                par = fop % 2
                wev = self.load_wpair(self.io["l1_fw"], 2048, fop, par,
                                      waits=[self.ws_war[par]])
                for sub in range(2):
                  fo = 2 * fop + sub
                  for tt in range(2):
                    bk = (fo * 2 + tt) % 4
                    pbx = self.pbank[bk]
                    mmev = None
                    for kc in range(16):
                        mmev = self.mm(pbx, self.ws_tile2(par, kc, sub),
                                       self.yb(kc, tt * 512, (tt + 1) * 512),
                                       [bk] if kc == 0 else [],
                                       waits=[wev, evy, evs] if kc == 0 else [],
                                       start=(kc == 0), stop=(kc == 15))
                    dst = self.qb(fo, tt * 512, (tt + 1) * 512)
                    evo = p.s(lambda e, o=dst, i=pbx, sc=self.scal[:, 16 + fo:17 + fo]:
                              e.activation(o, i, AF.Copy, scale=sc), waits=[mmev])
                    self.read_bank(bk, evo)
                    self.ws_war[par] = mmev
            evout = None
            for tt in range(2):
                PR = self.pbank[3][0:1, 0:512]
                mmev = None
                for fo in range(16):
                    sqs = self.sq[:, (fo % 2) * 512:(fo % 2 + 1) * 512]
                    evsq = p.s(lambda e, o=sqs,
                               i=self.qb(fo, tt * 512, (tt + 1) * 512):
                               e.activation(o, i, AF.Square),
                               waits=[evo, self.sq_war[fo % 2]])
                    mmev = self.mm(PR, self.cp["c_ones"], sqs,
                                   [3] if fo == 0 else [], waits=[evsq],
                                   start=(fo == 0), stop=(fo == 15))
                    self.sq_war[fo % 2] = mmev
                r1 = self.srow[0:1, 0:512]
                r2 = self.srow[0:1, 512:1024]
                r3 = self.srow[0:1, 1024:1536]
                r4 = self.srow[0:1, 1536:2048]
                e1 = p.s(lambda e, o=r1, i=PR: e.activation(o, i, AF.Ln,
                                                            bias=1e-38),
                         waits=[mmev] + self.row_war)
                self.read_bank(3, e1)
                e2 = p.s(lambda e, o=r1, i=r1: e.activation(o, i, AF.Exp,
                                                            scale=0.5), waits=[])
                e3 = p.v(lambda e, o=r2, i=r1: e.tensor_scalar_max(o, i, 1e-7),
                         waits=[e2])
                e4 = p.s(lambda e, o=r3, i=r2: e.activation(o, i, AF.Exp,
                                                            scale=-2.0),
                         waits=[e3])
                e5 = p.v(lambda e, o=r4, i=r3:
                         e.tensor_scalar(o, i, -1.0, 1.0, ALU.mult, ALU.add),
                         waits=[e4])
                e6 = p.v(lambda e, o=r3, i=r3: e.tensor_scalar_add(o, i, 1.0),
                         waits=[e4])
                e7 = p.v(lambda e, o=r3, i=r3: e.reciprocal(o, i), waits=[])
                e8 = p.v(lambda e, o=r4, a=r4, b=r3: e.tensor_mul(o, a, b),
                         waits=[])
                e9 = p.v(lambda e, o=r3, i=r2: e.tensor_scalar_add(o, i, 1e-7),
                         waits=[])
                e10 = p.v(lambda e, o=r3, i=r3: e.reciprocal(o, i), waits=[])
                e11 = p.v(lambda e, o=r4, a=r4, b=r3: e.tensor_mul(o, a, b),
                          waits=[])
                e12 = p.v(lambda e, o=r1, a=r1, b=r4: e.tensor_mul(o, a, b),
                          waits=[])
                e13 = p.v(lambda e, o=r1, i=r1: e.tensor_scalar_max(o, i, 1e-7),
                          waits=[])
                e14 = p.v(lambda e, o=r1, i=r1: e.reciprocal(o, i), waits=[])
                e15 = p.v(lambda e, o=r1, i=r1:
                          e.tensor_scalar(o, i, 0.99, 1.0, ALU.mult, ALU.min),
                          waits=[])
                e16 = p.v(lambda e, o=r4, a=r4, b=r1: e.tensor_mul(o, a, b),
                          waits=[])
                self.row_war = [e16]
                bc = self.mm(self.pbank[6], self.cp["c_onesrow"], r4, [6],
                             waits=[e16])
                RB = self.rb[:, tt * 512:(tt + 1) * 512]
                evb = p.s(lambda e, o=RB, i=self.pbank[6]: e.copy(o, i),
                          waits=[bc, self.rb_war[tt]])
                self.read_bank(6, evb)
                self.row_war = [bc]
                for fo in range(16):
                    t1 = self.rtmp[:, 0:512]
                    ev1 = p.v(lambda e, o=t1,
                              a=self.qb(fo, tt * 512, (tt + 1) * 512), b=RB:
                              e.tensor_mul(o, a, b),
                              waits=[evb] + ([self.t1_war] if self.t1_war else []))
                    g = tt % 2
                    if fo % 4 == 0:
                        self.xgev[g] = self.load_x4(xap, cast, fo,
                                                    T0 + tt * 512, g,
                                                    waits=[self.xt_war[g]])
                    dx = self.xsl(g, fo % 4)
                    evd = self.xgev[g]
                    if fin:
                        otile = self.of[:, (fo % 2) * 512:(fo % 2 + 1) * 512]
                    else:
                        otile = self.sq[:, (fo % 2) * 512:(fo % 2 + 1) * 512]
                    evo2 = p.v(lambda e, o=otile, a=t1, b=dx:
                               e.tensor_add(o, a, b),
                               waits=[evd] +
                                     ([self.ot_war[fo % 2]] if self.ot_war[fo % 2] else []) +
                                     ([self.sq_war[fo % 2]] if self.sq_war[fo % 2] else []))
                    self.xt_war[g] = evo2
                    self.t1_war = evo2
                    dd = dap.rearrange("(kc p) t -> kc p t", p=128)[
                        fo, :, T0 + tt * 512:T0 + (tt + 1) * 512]
                    evout = p.dma_s(dd, otile, waits=[evo2], sem=f"st{fo % 2}")
                    self.ot_war[fo % 2] = evout
                    if not fin:
                        self.sq_war[fo % 2] = evout
                self.rb_war[tt] = ev1
            self.phase_end = [e for e in self.ot_war if e]
        return self.phase_end



def build_kb(sublayers=(0, 1, 2, 3)):
    nc = bass.Bass("TRN2", target_bir_lowering=False)
    kb = KB(nc, sublayers)
    for si in sorted(sublayers):
        if si == 0:
            kb.emit_attn(0, 0)
        elif si == 1:
            kb.emit_ffn0(1)
        elif si == 2:
            kb.emit_attn(2, 1)
        elif si == 3:
            kb.emit_ffn1(3)
    return nc, kb


def finalize(nc, kb, extra_evs=()):
    kb.p.fence(list(kb.phase_end) + list(extra_evs))
    with nc.Block() as block:
        kb.p.replay(block)
    return nc


def build_nc(sublayers=(0, 1, 2, 3)):
    nc, kb = build_kb(sublayers)
    finalize(nc, kb)
    return nc, kb


def _prep_inputs(inputs):
    f = np.float32
    def cT(a):
        return np.ascontiguousarray(np.asarray(a, dtype=f).T)
    def c(a):
        return np.ascontiguousarray(np.asarray(a, dtype=f))
    com = dict(
        l0_wq=c(inputs["L0_attn_wq"]), l0_wk=c(inputs["L0_attn_wk"]),
        l0_wv=c(inputs["L0_attn_wv"]), l0_wo=c(inputs["L0_attn_wo"]),
        l0_wupT=cT(inputs["L0_ffn_w_up"]), l0_wdnT=cT(inputs["L0_ffn_w_down"]),
        l1_wqT=cT(inputs["L1_attn_Wq"]), l1_wkT=cT(inputs["L1_attn_Wk"]),
        l1_wvT=cT(inputs["L1_attn_Wv"]), l1_woT=cT(inputs["L1_attn_Wo"]),
        l1_fw=c(inputs["L1_ffn_weight"]),
        l1_fscale=np.ascontiguousarray(
            np.asarray(inputs["L1_ffn_scale"], dtype=f).reshape(16, 128).T),
    )
    com.update(host_constants())
    x = np.asarray(inputs["L0_attn_x"], dtype=f)
    maps = []
    for b in range(8):
        m = dict(com)
        m["xT"] = np.ascontiguousarray(x[b].T)
        maps.append(m)
    return maps


def _run(inputs, sublayers=(0, 1, 2, 3)):
    maps = _prep_inputs(inputs)
    nc, kb = build_nc(sublayers)
    res = run_bass_kernel_spmd(nc, maps, core_ids=list(range(8)))
    out = np.stack([np.asarray(r["outT"]).T for r in res.results])
    return np.ascontiguousarray(out)


def kernel(**inputs):
    return _run(inputs, (0, 1, 2, 3))
